# version 02
# baseline (speedup 1.0000x reference)
import numpy as np
import concourse.bass as bass
import concourse.mybir as mybir
from concourse.bass_utils import run_bass_kernel_spmd
from contextlib import ExitStack

F32 = mybir.dt.float32
BF16 = mybir.dt.bfloat16
I32 = mybir.dt.int32
U32 = mybir.dt.uint32
AF = mybir.ActivationFunctionType
ALU = mybir.AluOpType
AX = mybir.AxisListType

R = 8
T = 8192
TC = T // R
D = 2048
DH = 128
HL = 2
HFF = 8192
NKC = T // 128
NPAIR = NKC // 2
QBS = 512
NQB = T // QBS
NIT = HL * NQB
SCALE = float(DH) ** -0.5
NCH = TC // 128
KC = D // 128
NHC = HFF // 128

LN_EPS = 1e-5
RMS_EPS = 1e-8


class Ctr:
    def __init__(self, nc, ctx, name):
        self.h = ctx.enter_context(nc.semaphore(name))
        self.n = 0
        self.name = name

    def inc(self, instr, k=1):
        instr.then_inc(self.h, k)
        self.n += k
        return self.n


def build(stop_phase=None, dbg=()):
    nc = bass.Bass("TRN2", target_bir_lowering=False)

    x_in = nc.dram_tensor("x", [TC, D], F32, kind="ExternalInput").ap()
    wq_in = nc.dram_tensor("wq", [2 * DH, D], F32, kind="ExternalInput").ap()
    wk_in = nc.dram_tensor("wk", [2 * DH, D], F32, kind="ExternalInput").ap()
    wv_in = nc.dram_tensor("wv", [2 * DH, D], F32, kind="ExternalInput").ap()
    wo_in = nc.dram_tensor("wo", [2 * DH, D], F32, kind="ExternalInput").ap()
    wg_in = nc.dram_tensor("wg", [HFF // R, D], F32, kind="ExternalInput").ap()
    wu_in = nc.dram_tensor("wu", [HFF // R, D], F32, kind="ExternalInput").ap()
    wd_in = nc.dram_tensor("wd", [D // R, HFF], F32, kind="ExternalInput").ap()
    y_out = nc.dram_tensor("y", [TC, D], F32, kind="ExternalOutput").ap()

    ag0_in = nc.dram_tensor("ag0_in", [1, 4], F32)
    ag0_out = nc.dram_tensor("ag0_out", [R, 4], F32)
    agw1_in = nc.dram_tensor("agw1_in", [D, 2304], BF16)
    agw1_out = nc.dram_tensor("agw1_out", [R * D, 2304], BF16)
    agw2_in = nc.dram_tensor("agw2_in", [HFF, 256], BF16)
    agw2_out = nc.dram_tensor("agw2_out", [R * HFF, 256], BF16)
    ag2_in = nc.dram_tensor("ag2_in", [D, TC], BF16)
    ag2_out = nc.dram_tensor("ag2_out", [R * D, TC], BF16)
    ag3_in = nc.dram_tensor("ag3_in", [T, 2 * DH], BF16)
    ag3_out = nc.dram_tensor("ag3_out", [R * T, 2 * DH], BF16)
    x2_dram = nc.dram_tensor("x2_dram", [TC, D], F32)
    internals = {"ag0_out": ag0_out, "agw1_out": agw1_out,
                 "agw2_out": agw2_out, "ag2_out": ag2_out,
                 "ag3_out": ag3_out, "x2_dram": x2_dram,
                 "ag2_in": ag2_in, "ag3_in": ag3_in}
    dbg_outs = {}
    for name in dbg:
        h = internals[name]
        dbg_outs[name] = nc.dram_tensor(
            "dbg_" + name, list(h.shape), h.dtype, kind="ExternalOutput")

    hwm = {}
    top = ExitStack()
    with top:
        PA = top.enter_context(nc.psum_tensor("PA", [128, 1024], F32))
        PB = top.enter_context(nc.psum_tensor("PB", [128, 1024], F32))
        PC = top.enter_context(nc.psum_tensor("PC", [128, 512], F32))
        PD = top.enter_context(nc.psum_tensor("PD", [128, 512], F32))
        PE = top.enter_context(nc.psum_tensor("PE", [128, 512], F32))
        PF = top.enter_context(nc.psum_tensor("PF", [128, 512], F32))

        c_sy = Ctr(nc, top, "c_sy")
        c_gp = Ctr(nc, top, "c_gp")
        c_cc = Ctr(nc, top, "c_cc")
        c_te = Ctr(nc, top, "c_te")
        c_ve = Ctr(nc, top, "c_ve")
        c_ac = Ctr(nc, top, "c_ac")
        s_sc = Ctr(nc, top, "s_sc")
        s_ex = Ctr(nc, top, "s_ex")
        s_av = Ctr(nc, top, "s_av")
        s_z = Ctr(nc, top, "s_z")
        s_tr = Ctr(nc, top, "s_tr")
        s_red = Ctr(nc, top, "s_red")
        c_st = Ctr(nc, top, "c_st")
        ALL_CTRS = [c_sy, c_gp, c_cc, c_te, c_ve, c_ac, s_sc, s_ex, s_av,
                    s_z, s_tr, s_red]

        ones_r = top.enter_context(nc.sbuf_tensor("ones_r", [1, 128], F32))
        ones_c = top.enter_context(nc.sbuf_tensor("ones_c", [128, 1], F32))
        ident = top.enter_context(nc.sbuf_tensor("ident", [128, 128], F32))
        g_sc = top.enter_context(nc.sbuf_tensor("g_sc", [128, 8], F32))
        g_scn = top.enter_context(nc.sbuf_tensor("g_scn", [128, 4], F32))
        g_negB = top.enter_context(nc.sbuf_tensor("g_negB", [128, 2], F32))
        scr_x = top.enter_context(nc.sbuf_tensor("scr_x", [128, 2, D], F32))
        pid_sb = top.enter_context(nc.sbuf_tensor("pid_sb", [1, 1], U32))
        blk = top.enter_context(nc.Block())

        def W(eng, ename, ctr, val):
            if val is None or val <= 0:
                return
            key = (ename, ctr.name)
            if hwm.get(key, 0) >= val:
                return
            eng.wait_ge(ctr.h, val)
            hwm[key] = val

        def mkscr(stack, tag):
            class S:
                pass
            s = S()
            s.a = stack.enter_context(nc.sbuf_tensor(f"sa{tag}", [128, D], F32))
            s.b = stack.enter_context(nc.sbuf_tensor(f"sb{tag}", [128, D], F32))
            s.c = stack.enter_context(nc.sbuf_tensor(f"sc{tag}", [128, D], I32))
            s.d = stack.enter_context(nc.sbuf_tensor(f"sd{tag}", [128, D], F32))
            s.e = stack.enter_context(nc.sbuf_tensor(f"se{tag}", [128, D], BF16))
            s.f = stack.enter_context(nc.sbuf_tensor(f"sf{tag}", [128, D], BF16))
            s.st = stack.enter_context(nc.sbuf_tensor(f"st{tag}", [128, 16], F32))
            return s

        TEs, VEs, ACs, SYs, GPs = (blk.tensor, blk.vector, blk.scalar,
                                   blk.sync, blk.gpsimd)

        def finish(gates):
            def s_fin(sy):
                for (ctr, val) in gates:
                    W(sy, "sy", ctr, val)
                for name, outh in dbg_outs.items():
                    i = sy.dma_start(out=outh.ap(), in_=internals[name].ap())
                    c_st.inc(i, 16)
                if c_st.n:
                    sy.wait_ge(c_st.h, c_st.n)
            SYs(s_fin)

        def ln_stats(S, src_ap, eps, out_dst=None, base_ap=None,
                     extra_ve_waits=()):
            def v1(v):
                for (ctr, val) in extra_ve_waits:
                    W(v, "ve", ctr, val)
                c_ve.inc(v.reduce_sum(out=S.st[:, 0:1], in_=src_ap,
                                      axis=AX.X))
                W(v, "ve", c_ve, c_ve.n)
                c_ve.inc(v.tensor_scalar(out=S.st[:, 1:2], in0=S.st[:, 0:1],
                                         scalar1=1.0 / D, scalar2=None,
                                         op0=ALU.mult))
                W(v, "ve", c_ve, c_ve.n)
                v.tensor_scalar(out=S.a[:], in0=src_ap,
                                scalar1=S.st[:, 1:2], scalar2=None,
                                op0=ALU.subtract)
                v.tensor_tensor(out=S.b[:], in0=S.a[:], in1=S.a[:],
                                op=ALU.mult)
                c_ve.inc(v.reduce_sum(out=S.st[:, 2:3], in_=S.b[:],
                                      axis=AX.X))
                W(v, "ve", c_ve, c_ve.n)
                c_ve.inc(v.tensor_scalar(out=S.st[:, 3:4], in0=S.st[:, 2:3],
                                         scalar1=1.0 / D, scalar2=eps,
                                         op0=ALU.mult, op1=ALU.add))
            VEs(v1)

            def a1(sc):
                W(sc, "ac", c_ve, c_ve.n)
                c_ac.inc(sc.activation(out=S.st[:, 4:5], in_=S.st[:, 3:4],
                                       func=AF.Sqrt))
            ACs(a1)

            def v2(v):
                W(v, "ve", c_ac, c_ac.n)
                c_ve.inc(v.reciprocal(out=S.st[:, 5:6], in_=S.st[:, 4:5]))
                W(v, "ve", c_ve, c_ve.n)
                if out_dst is not None:
                    if base_ap is None:
                        i = v.tensor_scalar(out=out_dst, in0=S.a[:],
                                            scalar1=S.st[:, 5:6],
                                            scalar2=None, op0=ALU.mult)
                    else:
                        i = v.scalar_tensor_tensor(out=out_dst, in0=S.a[:],
                                                   scalar=S.st[:, 5:6],
                                                   in1=base_ap, op0=ALU.mult,
                                                   op1=ALU.add)
                    c_ve.inc(i)
            VEs(v2)

        def quant_scale(S, src_abs_ap, clip_eps, with_rstd):
            def v1(v):
                c_ve.inc(v.reduce_max(out=S.st[:, 6:7], in_=src_abs_ap,
                                      axis=AX.X, apply_absolute_value=True))
                W(v, "ve", c_ve, c_ve.n)
                if with_rstd:
                    c_ve.inc(v.tensor_tensor(out=S.st[:, 7:8],
                                             in0=S.st[:, 6:7],
                                             in1=S.st[:, 5:6], op=ALU.mult))
                else:
                    c_ve.inc(v.tensor_copy(out=S.st[:, 7:8],
                                           in_=S.st[:, 6:7]))
                W(v, "ve", c_ve, c_ve.n)
                if clip_eps is not None:
                    c_ve.inc(v.tensor_scalar(out=S.st[:, 7:8],
                                             in0=S.st[:, 7:8],
                                             scalar1=clip_eps, scalar2=None,
                                             op0=ALU.max))
                    W(v, "ve", c_ve, c_ve.n)
                v.tensor_scalar(out=S.st[:, 8:9], in0=S.st[:, 7:8],
                                scalar1=1.0 / 127.0, scalar2=None,
                                op0=ALU.mult)
                c_ve.inc(v.reciprocal(out=S.st[:, 9:10], in_=S.st[:, 7:8]))
                W(v, "ve", c_ve, c_ve.n)
                c_ve.inc(v.tensor_scalar(out=S.st[:, 10:11],
                                         in0=S.st[:, 9:10], scalar1=127.0,
                                         scalar2=None, op0=ALU.mult))
                W(v, "ve", c_ve, c_ve.n)
                if with_rstd:
                    c_ve.inc(v.tensor_tensor(out=S.st[:, 11:12],
                                             in0=S.st[:, 5:6],
                                             in1=S.st[:, 10:11],
                                             op=ALU.mult))
                else:
                    c_ve.inc(v.tensor_copy(out=S.st[:, 11:12],
                                           in_=S.st[:, 10:11]))
                W(v, "ve", c_ve, c_ve.n)
            VEs(v1)

        def quant_core(S, src_ap, dst_f32):
            def a1(sc):
                W(sc, "ac", c_ve, c_ve.n)
                sc.activation(out=S.b[:], in_=src_ap, func=AF.Abs)
                c_ac.inc(sc.activation(out=S.f[:], in_=src_ap, func=AF.Sign))
            ACs(a1)

            def v1(v):
                W(v, "ve", c_ac, c_ac.n)
                v.tensor_scalar(out=S.d[:], in0=S.b[:],
                                scalar1=S.st[:, 11:12], scalar2=None,
                                op0=ALU.mult)
                v.tensor_scalar(out=S.c[:], in0=S.b[:],
                                scalar1=S.st[:, 11:12], scalar2=None,
                                op0=ALU.mult)
                v.scalar_tensor_tensor(out=S.a[:], in0=S.c[:], scalar=-1.0,
                                       in1=S.d[:], op0=ALU.mult, op1=ALU.add)
                v.tensor_scalar(out=S.e[:], in0=S.a[:], scalar1=0.5,
                                scalar2=None, op0=ALU.is_ge)
                v.tensor_tensor(out=S.d[:], in0=S.c[:], in1=S.e[:],
                                op=ALU.add)
                v.tensor_scalar(out=S.b[:], in0=S.f[:],
                                scalar1=S.st[:, 8:9], scalar2=None,
                                op0=ALU.mult)
                c_ve.inc(v.tensor_tensor(out=dst_f32, in0=S.d[:],
                                         in1=S.b[:], op=ALU.mult))
            VEs(v1)

        def transpose16(src_getter, dst_slices, extra_first=(),
                        ready=None, prev_ve=None):
            te_tok = []
            ve_tok = []
            for grp in range(4):
                pt = PE if grp % 2 == 0 else PF

                def t1(t, grp=grp, pt=pt):
                    for (ctr, val) in extra_first:
                        W(t, "te", ctr, val)
                    if ready is not None:
                        W(t, "te", ready[0], ready[1])
                    if grp >= 2:
                        W(t, "te", c_ve, ve_tok[grp - 2])
                    elif prev_ve is not None:
                        W(t, "te", c_ve, prev_ve[grp + 2])
                    src = src_getter()
                    for j in range(4):
                        kc = grp * 4 + j
                        i = t.transpose(
                            pt.ap().rearrange("p (g c) -> p g c",
                                                     g=4)[:, j, :],
                            src[:, kc * 128:(kc + 1) * 128], ident[:])
                    te_tok.append(c_te.inc(i))
                TEs(t1)

                def v1(v, grp=grp, pt=pt):
                    W(v, "ve", c_te, te_tok[grp])
                    i = v.tensor_copy(
                        out=dst_slices[grp],
                        in_=pt.ap().rearrange("p (g c) -> p g c",
                                                     g=4))
                    ve_tok.append(c_ve.inc(i))
                VEs(v1)
            return te_tok, ve_tok

        qkvT_st = ExitStack()
        qkvT = qkvT_st.enter_context(
            nc.sbuf_tensor("qkvT", [128, 16, 768], BF16))
        ph0 = ExitStack()
        S0 = mkscr(ph0, "0")
        pstage = [ph0.enter_context(
            nc.sbuf_tensor(f"pstage{i}", [128, 4, 128], BF16))
            for i in range(2)]
        pcol = ph0.enter_context(nc.sbuf_tensor("pcol", [128, 8], F32))
        prow = ph0.enter_context(nc.sbuf_tensor("prow", [1, 8], F32))
        pag0 = ph0.enter_context(nc.sbuf_tensor("pag0", [8, 4], F32))
        pcv = ph0.enter_context(nc.sbuf_tensor("pcv", [1, 8], F32))
        xqfm = ph0.enter_context(nc.sbuf_tensor("xqfm", [128, 16, TC], BF16))

        def g0(g):
            g.memset(ones_r[:], 1.0)
            g.memset(ones_c[:], 1.0)
            g.memset(ident[:], 0.0)
            i = g.affine_select(out=ident[:], in_=ident[:],
                                compare_op=ALU.not_equal, fill=1.0, base=0,
                                pattern=[[-1, 128]], channel_multiplier=1)
            c_gp.inc(i, 1)
        GPs(g0)
        GP_CONSTS = c_gp.n

        abs_srcs = []
        for ap in (wq_in, wk_in, wv_in, wo_in):
            for ch in range(2):
                abs_srcs.append(ap[ch * 128:(ch + 1) * 128, :])
        abs_ld = []
        abs_red = []
        for i8, src in enumerate(abs_srcs):
            def s1(sy, i8=i8, src=src):
                if i8 >= 2:
                    W(sy, "sy", c_ve, abs_red[i8 - 2])
                c_sy.inc(sy.dma_start(out=scr_x[:, i8 % 2, :], in_=src), 16)
                abs_ld.append(c_sy.n)
            SYs(s1)

            def v1(v, i8=i8):
                W(v, "ve", c_sy, abs_ld[i8])
                abs_red.append(c_ve.inc(
                    v.reduce_sum(out=pcol[:, i8:i8 + 1],
                                 in_=scr_x[:, i8 % 2, :], axis=AX.X,
                                 apply_absolute_value=True)))
            VEs(v1)

        def t2(t):
            W(t, "te", c_ve, abs_red[-1])
            W(t, "te", c_gp, GP_CONSTS)
            c_te.inc(t.matmul(PE[0:1, 0:8], ones_c[:, :], pcol[:, :],
                              start=True, stop=True))
        TEs(t2)
        TE_ABS = c_te.n

        def v3(v):
            W(v, "ve", c_te, TE_ABS)
            c_ve.inc(v.tensor_copy(out=prow[:, :], in_=PE[0:1, 0:8]))
            W(v, "ve", c_ve, c_ve.n)
            pr = prow.ap()
            c_ve.inc(v.tensor_tensor(out=prow[:, 0:4], in0=pr[0:1, 0:8:2],
                                     in1=pr[0:1, 1:8:2], op=ALU.add))
        VEs(v3)
        VE_PROW = c_ve.n

        def g1(g):
            W(g, "gp", c_ve, VE_PROW)
            c_gp.inc(g.dma_start(out=ag0_in.ap(), in_=prow[:, 0:4]), 16)
            W(g, "gp", c_gp, c_gp.n)
            c_cc.inc(g.collective_compute(
                "AllGather", ALU.bypass, replica_groups=[list(range(R))],
                ins=[ag0_in.ap()], outs=[ag0_out.ap()]))
        GPs(g1)
        CC_AG0 = c_cc.n

        def s2(sy):
            W(sy, "sy", c_cc, CC_AG0)
            c_sy.inc(sy.dma_start(out=pag0[:, :], in_=ag0_out.ap()), 16)
        SYs(s2)
        SY_PAG0 = c_sy.n

        def t3(t):
            W(t, "te", c_sy, SY_PAG0)
            c_te.inc(t.matmul(PF[0:1, 0:4], ones_c[0:8, :], pag0[:, :],
                              start=True, stop=True))
        TEs(t3)
        TE_AG0 = c_te.n

        def v4(v):
            W(v, "ve", c_te, TE_AG0)
            c_ve.inc(v.tensor_scalar(out=pcv[:, 0:4], in0=PF[0:1, 0:4],
                                     scalar1=1.0 / (D * D), scalar2=1e-5,
                                     op0=ALU.mult, op1=ALU.max))
            W(v, "ve", c_ve, c_ve.n)
            c_ve.inc(v.tensor_scalar(out=pcv[:, 4:8], in0=pcv[:, 0:4],
                                     scalar1=0.5, scalar2=None,
                                     op0=ALU.mult))
            W(v, "ve", c_ve, c_ve.n)
            c_ve.inc(v.tensor_tensor(out=pcv[:, 0:1], in0=pcv[:, 0:1],
                                     in1=pcv[:, 1:2], op=ALU.mult))
            W(v, "ve", c_ve, c_ve.n)
            c_ve.inc(v.tensor_scalar(out=pcv[:, 1:2], in0=pcv[:, 0:1],
                                     scalar1=SCALE, scalar2=None,
                                     op0=ALU.mult))
        VEs(v4)
        VE_PCV = c_ve.n

        def t4(t):
            W(t, "te", c_ve, VE_PCV)
            t.matmul(PF[:, 8:12], ones_r[:, :], pcv[0:1, 4:8],
                     start=True, stop=True)
            t.matmul(PF[:, 12:14], ones_r[:, :], pcv[0:1, 2:4],
                     start=True, stop=True)
            c_te.inc(t.matmul(PF[:, 14:15], ones_r[:, :], pcv[0:1, 1:2],
                              start=True, stop=True))
        TEs(t4)
        TE_BCS = c_te.n

        def v5(v):
            W(v, "ve", c_te, TE_BCS)
            c_ve.inc(v.tensor_copy(out=g_sc[:, 0:7], in_=PF[:, 8:15]))
            W(v, "ve", c_ve, c_ve.n)
            c_ve.inc(v.tensor_scalar(out=g_scn[:, 0:4], in0=g_sc[:, 0:4],
                                     scalar1=-1.0, scalar2=None,
                                     op0=ALU.mult))
        VEs(v5)
        VE_SCALES = c_ve.n

        pieces = []
        for ch in range(8):
            pieces.append(("sign", "wg", wg_in[ch * 128:(ch + 1) * 128, :],
                           None, ch, 0))
        for ch in range(8):
            pieces.append(("sign", "wu", wu_in[ch * 128:(ch + 1) * 128, :],
                           None, ch, 0))
        for ch in range(2):
            for cp in range(4):
                pieces.append(("sign", "wd",
                               wd_in[ch * 128:(ch + 1) * 128,
                                     cp * D:(cp + 1) * D], None, ch, cp))
        for wi, (nm, ap) in enumerate([("wq", wq_in), ("wk", wk_in),
                                       ("wv", wv_in), ("wo", wo_in)]):
            for ch in range(2):
                pieces.append(("tern", nm, ap[ch * 128:(ch + 1) * 128, :],
                               wi, ch, 0))

        pld = []
        pq_tok = []
        p_te = []
        p_ve = []
        pst_tok = {}
        pstage_last = {}

        for pi, (kind, nm, src, wi, ch, cp) in enumerate(pieces):
            def s3(sy, pi=pi, src=src):
                if pi >= 2:
                    ctr2, v2 = pq_tok[pi - 2]
                    W(sy, "sy", ctr2, v2)
                c_sy.inc(sy.dma_start(out=scr_x[:, pi % 2, :], in_=src), 16)
                pld.append(c_sy.n)
            SYs(s3)

            if kind == "sign":
                def a2(sc, pi=pi):
                    W(sc, "ac", c_sy, pld[pi])
                    if pi >= 1:
                        W(sc, "ac", c_te, p_te[pi - 1][3])
                    c_ac.inc(sc.activation(out=S0.d[:],
                                           in_=scr_x[:, pi % 2, :],
                                           func=AF.Sign))
                ACs(a2)
                pq_tok.append((c_ac, c_ac.n))
            else:
                def v6(v, pi=pi, wi=wi):
                    W(v, "ve", c_sy, pld[pi])
                    W(v, "ve", c_ve, VE_SCALES)
                    if pi >= 1:
                        W(v, "ve", c_te, p_te[pi - 1][3])
                    v.tensor_scalar(out=S0.e[:], in0=scr_x[:, pi % 2, :],
                                    scalar1=g_sc[:, wi:wi + 1], scalar2=None,
                                    op0=ALU.is_ge)
                    v.tensor_scalar(out=S0.f[:], in0=scr_x[:, pi % 2, :],
                                    scalar1=g_scn[:, wi:wi + 1],
                                    scalar2=None, op0=ALU.is_le)
                    c_ve.inc(v.tensor_tensor(out=S0.d[:], in0=S0.e[:],
                                             in1=S0.f[:], op=ALU.subtract))
                VEs(v6)
                pq_tok.append((c_ve, c_ve.n))

            if kind == "tern" and nm in ("wq", "wk", "wv"):
                widx = {"wq": 0, "wk": 1, "wv": 2}[nm]
                dsts = [qkvT[:, 4 * g:4 * g + 4,
                             widx * 256 + ch * 128:
                             widx * 256 + (ch + 1) * 128] for g in range(4)]
                stage = False
            else:
                dsts = [pstage[g % 2][:, :, :] for g in range(4)]
                stage = True

            te_tok = []
            ve_tok = []
            qc_, qv_ = pq_tok[pi]
            for grp in range(4):
                pt = PE if grp % 2 == 0 else PF

                def t5(t, grp=grp, pt=pt, pi=pi, qc_=qc_, qv_=qv_):
                    W(t, "te", qc_, qv_)
                    if pi == 0:
                        W(t, "te", c_ve, VE_SCALES)
                    if grp >= 2:
                        W(t, "te", c_ve, ve_tok[grp - 2])
                    elif pi >= 1:
                        W(t, "te", c_ve, p_ve[pi - 1][grp + 2])
                    for j in range(4):
                        kc = grp * 4 + j
                        i = t.transpose(
                            pt.ap().rearrange("p (g c) -> p g c",
                                                     g=4)[:, j, :],
                            S0.d[:, kc * 128:(kc + 1) * 128], ident[:])
                    te_tok.append(c_te.inc(i))
                TEs(t5)

                def v7(v, grp=grp, pt=pt, pi=pi, stage=stage, dsts=dsts):
                    W(v, "ve", c_te, te_tok[grp])
                    if stage and (grp % 2) in pstage_last:
                        W(v, "ve", c_gp, pstage_last[grp % 2])
                    i = v.tensor_copy(
                        out=dsts[grp],
                        in_=pt.ap().rearrange("p (g c) -> p g c",
                                                     g=4))
                    ve_tok.append(c_ve.inc(i))
                VEs(v7)

                if stage:
                    def g2(g, grp=grp, pi=pi, nm=nm, ch=ch, cp=cp):
                        W(g, "gp", c_ve, ve_tok[grp])
                        if nm == "wo":
                            dst = agw1_in.ap()[grp * 512:(grp + 1) * 512,
                                               ch * 128:(ch + 1) * 128]
                        elif nm == "wg":
                            dst = agw1_in.ap()[grp * 512:(grp + 1) * 512,
                                               256 + ch * 128:
                                               256 + (ch + 1) * 128]
                        elif nm == "wu":
                            dst = agw1_in.ap()[grp * 512:(grp + 1) * 512,
                                               1280 + ch * 128:
                                               1280 + (ch + 1) * 128]
                        else:
                            dst = agw2_in.ap()[cp * D + grp * 512:
                                               cp * D + (grp + 1) * 512,
                                               ch * 128:(ch + 1) * 128]
                        i = g.dma_start(
                            out=dst.rearrange("(g p) c -> p g c", p=128),
                            in_=pstage[grp % 2][:, :, :])
                        pst_tok[(pi, grp)] = c_gp.inc(i, 16)
                        pstage_last[grp % 2] = pst_tok[(pi, grp)]
                    GPs(g2)
            p_te.append(te_tok)
            p_ve.append(ve_tok)

        x_te = []
        x_ve = []
        xq_ld = []
        for tc in range(NCH):
            def s4(sy, tc=tc):
                if tc == 0:
                    ctr2, v2 = pq_tok[-2]
                    W(sy, "sy", ctr2, v2)
                elif tc == 1:
                    ctr2, v2 = pq_tok[-1]
                    W(sy, "sy", ctr2, v2)
                else:
                    W(sy, "sy", c_te, x_te[tc - 2][3])
                c_sy.inc(sy.dma_start(out=scr_x[:, tc % 2, :],
                                      in_=x_in[tc * 128:(tc + 1) * 128, :]),
                         16)
                xq_ld.append(c_sy.n)
            SYs(s4)

            def v8(v, tc=tc):
                W(v, "ve", c_sy, xq_ld[tc])
                if tc == 0:
                    W(v, "ve", c_te, p_te[-1][3])
            VEs(v8)
            ln_stats(S0, scr_x[:, tc % 2, :], LN_EPS)
            quant_scale(S0, S0.a[:], 1e-5, with_rstd=True)
            quant_core(S0, S0.a[:], scr_x[:, tc % 2, :])
            tt, vv = transpose16(
                lambda tc=tc: scr_x[:, tc % 2, :],
                [xqfm[:, 4 * g:4 * g + 4, tc * 128:(tc + 1) * 128]
                 for g in range(4)],
                ready=(c_ve, c_ve.n),
                prev_ve=(x_ve[tc - 1] if tc >= 1 else p_ve[-1]))
            x_te.append(tt)
            x_ve.append(vv)

        def g3(g):
            W(g, "gp", c_ve, x_ve[-1][3])
            i = g.dma_start(
                out=ag2_in.ap().rearrange("(kc p) t -> p kc t", p=128),
                in_=xqfm[:, :, :])
            c_gp.inc(i, 16)
            W(g, "gp", c_gp, c_gp.n)
            c_cc.inc(g.collective_compute(
                "AllGather", ALU.bypass, replica_groups=[list(range(R))],
                ins=[ag2_in.ap()], outs=[ag2_out.ap()]))
            c_cc.inc(g.collective_compute(
                "AllGather", ALU.bypass, replica_groups=[list(range(R))],
                ins=[agw1_in.ap()], outs=[agw1_out.ap()]))
            c_cc.inc(g.collective_compute(
                "AllGather", ALU.bypass, replica_groups=[list(range(R))],
                ins=[agw2_in.ap()], outs=[agw2_out.ap()]))
        GPs(g3)
        CC_AG2 = CC_AG0 + 1
        CC_AGW = CC_AG0 + 3

        if stop_phase == 0:
            finish([(c, c.n) for c in ALL_CTRS])
            ph0.close()
            qkvT_st.close()
            return nc

        P0_DONE = [(c_te, c_te.n), (c_ve, c_ve.n), (c_ac, c_ac.n),
                   (c_gp, c_gp.n), (c_sy, c_sy.n)]
        ph0.close()
        ph1 = ExitStack()
        s_q = ph1.enter_context(nc.sbuf_tensor("s_q", [128, HL, T], BF16))
        s_k = ph1.enter_context(nc.sbuf_tensor("s_k", [128, HL, T], BF16))
        s_v = ph1.enter_context(nc.sbuf_tensor("s_v", [128, NKC, 256], BF16))
        ph1b = ExitStack()
        xb = ph1b.enter_context(nc.sbuf_tensor("xb", [128, 16, 512], BF16))
        nsc = [ph1b.enter_context(nc.sbuf_tensor(f"nsc{i}", [128, 512], F32))
               for i in range(2)]
        mxrow = ph1b.enter_context(nc.sbuf_tensor("mxrow", [1, 16], F32))
        bstat = ph1b.enter_context(nc.sbuf_tensor("bstat", [1, 8], F32))

        BANKS4 = [PC, PD, PE, PF]
        bank_free = [None, None, None, None]
        xb_free = [None]

        for tb in range(R):
            for th in range(2):
                def s5(sy, tb=tb, th=th):
                    for (ctr, val) in P0_DONE:
                        W(sy, "sy", ctr, val)
                    W(sy, "sy", c_cc, CC_AG2)
                    if xb_free[0] is not None:
                        W(sy, "sy", c_te, xb_free[0])
                    i = sy.dma_start(
                        out=xb[:, :, :],
                        in_=ag2_out.ap()[tb * D:(tb + 1) * D, :].rearrange(
                            "(kc p) t -> p kc t",
                            p=128)[:, :, th * 512:(th + 1) * 512])
                    c_sy.inc(i, 16)
                SYs(s5)
                LD_XB = c_sy.n

                groups = [("q", 0), ("q", 1), ("k", 0), ("k", 1),
                          ("v", 0), ("v", 1), ("v", 2), ("v", 3)]
                for gidx, (gk, gsub) in enumerate(groups):
                    bank = BANKS4[gidx % 4]

                    def t6(t, gk=gk, gsub=gsub, gidx=gidx, bank=bank,
                           LD_XB=LD_XB):
                        for (ctr, val) in P0_DONE:
                            W(t, "te", ctr, val)
                        W(t, "te", c_sy, LD_XB)
                        bf = bank_free[gidx % 4]
                        if bf is not None:
                            W(t, "te", bf[0], bf[1])
                        if gk == "q":
                            for kc in range(KC):
                                i = t.matmul(
                                    bank[:, :],
                                    qkvT[:, kc, gsub * 128:(gsub + 1) * 128],
                                    xb[:, kc, :], start=(kc == 0),
                                    stop=(kc == KC - 1))
                        elif gk == "k":
                            for kc in range(KC):
                                i = t.matmul(
                                    bank[:, :],
                                    qkvT[:, kc, 256 + gsub * 128:
                                         256 + (gsub + 1) * 128],
                                    xb[:, kc, :], start=(kc == 0),
                                    stop=(kc == KC - 1))
                        else:
                            for kc in range(KC):
                                i = t.matmul(
                                    bank[:, 0:256],
                                    xb[:, kc, gsub * 128:(gsub + 1) * 128],
                                    qkvT[:, kc, 512:768], start=(kc == 0),
                                    stop=(kc == KC - 1))
                        c_te.inc(i)
                    TEs(t6)
                    MM_TOK = c_te.n

                    def a3(sc, gk=gk, gsub=gsub, bank=bank, tb=tb, th=th,
                           MM_TOK=MM_TOK):
                        W(sc, "ac", c_te, MM_TOK)
                        t0 = tb * 1024 + th * 512
                        if gk == "q":
                            i = sc.copy(out=s_q[:, gsub, t0:t0 + 512],
                                        in_=bank[:, :])
                        elif gk == "k":
                            i = sc.copy(out=s_k[:, gsub, t0:t0 + 512],
                                        in_=bank[:, :])
                        else:
                            i = sc.copy(out=s_v[:, (t0 // 128) + gsub, :],
                                        in_=bank[:, 0:256])
                        c_ac.inc(i)
                    ACs(a3)
                    bank_free[gidx % 4] = (c_ac, c_ac.n)
                xb_free[0] = c_te.n

        QKV_AC = c_ac.n

        combos = [("q", 0), ("k", 0), ("q", 1), ("k", 1)]
        for ci, (src_k, h) in enumerate(combos):
            src = s_q if src_k == "q" else s_k
            sq_te = []
            sq_ve = []
            for bx in range(16):
                buf = nsc[bx % 2]
                pt = PE if bx % 2 == 0 else PF

                def a4(sc, src=src, h=h, bx=bx, buf=buf):
                    W(sc, "ac", c_ac, QKV_AC)
                    if bx >= 2:
                        W(sc, "ac", c_te, sq_te[bx - 2])
                    i = sc.activation(out=buf[:, :],
                                      in_=src[:, h, bx * 512:(bx + 1) * 512],
                                      func=AF.Square)
                    c_ac.inc(i)
                ACs(a4)
                SQ_AC = c_ac.n

                def t7(t, buf=buf, pt=pt, bx=bx, SQ_AC=SQ_AC):
                    W(t, "te", c_ac, SQ_AC)
                    if bx >= 2:
                        W(t, "te", c_ve, sq_ve[bx - 2])
                    elif bank_free[2 + (bx % 2)] is not None:
                        bf = bank_free[2 + (bx % 2)]
                        W(t, "te", bf[0], bf[1])
                    i = t.matmul(pt[0:1, :], ones_c[:, :], buf[:, :],
                                 start=True, stop=True)
                    sq_te.append(c_te.inc(i))
                TEs(t7)

                def v9(v, pt=pt, bx=bx):
                    W(v, "ve", c_te, sq_te[bx])
                    i = v.reduce_max(out=mxrow[:, bx:bx + 1],
                                     in_=pt[0:1, :], axis=AX.X)
                    sq_ve.append(c_ve.inc(i))
                VEs(v9)

            def v10(v, ci=ci):
                W(v, "ve", c_ve, sq_ve[15])
                c_ve.inc(v.reduce_max(out=bstat[:, ci:ci + 1],
                                      in_=mxrow[:, :], axis=AX.X))
            VEs(v10)

        def v11(v):
            W(v, "ve", c_ve, c_ve.n)
            bs = bstat.ap()
            c_ve.inc(v.tensor_tensor(out=bstat[:, 4:6], in0=bs[0:1, 0:4:2],
                                     in1=bs[0:1, 1:4:2], op=ALU.add))
            W(v, "ve", c_ve, c_ve.n)
            c_ve.inc(v.tensor_scalar(out=bstat[:, 4:6], in0=bstat[:, 4:6],
                                     scalar1=-0.5, scalar2=None,
                                     op0=ALU.mult))
            W(v, "ve", c_ve, c_ve.n)
            c_ve.inc(v.tensor_scalar(out=bstat[:, 4:6], in0=bstat[:, 4:6],
                                     scalar1=g_sc[0:1, 6:7], scalar2=None,
                                     op0=ALU.mult))
        VEs(v11)
        VE_NEGB_ROW = c_ve.n

        def t8(t):
            W(t, "te", c_ve, VE_NEGB_ROW)
            c_te.inc(t.matmul(PE[:, 0:2], ones_r[:, :], bstat[0:1, 4:6],
                              start=True, stop=True))
        TEs(t8)
        TE_NEGB = c_te.n

        def v12(v):
            W(v, "ve", c_te, TE_NEGB)
            c_ve.inc(v.tensor_copy(out=g_negB[:, :], in_=PE[:, 0:2]))
        VEs(v12)
        P2_READY = [(c_ve, c_ve.n), (c_ac, c_ac.n)]

        if stop_phase == 1:
            finish([(c, c.n) for c in ALL_CTRS])
            ph1b.close()
            ph1.close()
            qkvT_st.close()
            return nc

        ph1b.close()
        ph2 = ExitStack()
        sp = ph2.enter_context(nc.sbuf_tensor("sp", [128, 2, 2, QBS], BF16))
        sr = ph2.enter_context(nc.sbuf_tensor("sr", [128, QBS], F32))
        sofm = ph2.enter_context(nc.sbuf_tensor("sofm", [128, QBS], F32))
        szv = ph2.enter_context(nc.sbuf_tensor("szv", [128, 4], F32))
        szr = ph2.enter_context(nc.sbuf_tensor("szr", [128, 4], F32))
        s_o = ph2.enter_context(nc.sbuf_tensor("s_o", [128, 2, 4, DH], BF16))

        SEXP = 12
        STRP = 8
        so_dma = {}
        S_SC0, S_EX0, S_AV0, S_Z0 = s_sc.n, s_ex.n, s_av.n, s_z.n
        S_TR0, S_RED0 = s_tr.n, s_red.n

        def v13(v):
            for (ctr, val) in P2_READY:
                W(v, "ve", ctr, val)
            c_ve.inc(v.memset(sr[:, :], 0.0))
        VEs(v13)
        SR_INIT = c_ve.n

        for it in range(NIT):
            h = it // NQB
            qb = it % NQB
            qt = s_q[:, h, qb * QBS:(qb + 1) * QBS]

            for pr in range(NPAIR):
                g = it * NPAIR + pr
                ps = PA if g % 2 == 0 else PB

                def t9(t, g=g, pr=pr, it=it, ps=ps, h=h, qt=qt):
                    for (ctr, val) in P2_READY:
                        W(t, "te", ctr, val)
                    if g >= 2:
                        W(t, "te", s_ex, S_EX0 + g - 1)
                    k0 = (2 * pr) * 128
                    t.matmul(ps[:, 0:QBS], s_k[:, h, k0:k0 + 128], qt,
                             start=True, stop=True)
                    i = t.matmul(ps[:, QBS:2 * QBS],
                                 s_k[:, h, k0 + 128:k0 + 256], qt,
                                 start=True, stop=True)
                    s_sc.inc(i)
                    if pr >= 1:
                        av = g - 1
                        avp = pr - 1
                        W(t, "te", s_ex, S_EX0 + av + 1)
                        if avp == 0:
                            W(t, "te", s_red,
                              S_RED0 + (it - 1) * SEXP + 5 if it >= 1 else 0)
                        kc0 = 2 * avp
                        t.matmul(PC[:, :],
                                 s_v[:, kc0, h * 128:(h + 1) * 128],
                                 sp[:, av % 2, 0, :], start=(avp == 0),
                                 stop=False, skip_group_check=True)
                        i = t.matmul(PC[:, :],
                                     s_v[:, kc0 + 1, h * 128:(h + 1) * 128],
                                     sp[:, av % 2, 1, :], start=False,
                                     stop=(avp == NPAIR - 1),
                                     skip_group_check=True)
                        s_av.inc(i)
                TEs(t9)
                assert s_sc.n == S_SC0 + g + 1

                def a5(sc, g=g, ps=ps, h=h):
                    W(sc, "ac", s_sc, S_SC0 + g + 1)
                    if g >= 2:
                        W(sc, "ac", s_av, S_AV0 + g - 1)
                        W(sc, "ac", s_z, S_Z0 + g - 1)
                    i = sc.activation(out=sp[:, g % 2, :, :], in_=ps[:, :],
                                      func=AF.Exp,
                                      bias=g_negB[:, h:h + 1],
                                      scale=g_sc[:, 6:7])
                    s_ex.inc(i)
                ACs(a5)
                assert s_ex.n == S_EX0 + g + 1

                def v14(v, g=g):
                    W(v, "ve", s_ex, S_EX0 + g + 1)
                    if g == 0:
                        W(v, "ve", c_ve, SR_INIT)
                    v.tensor_tensor(out=sr[:, :], in0=sr[:, :],
                                    in1=sp[:, g % 2, 0, :], op=ALU.add)
                    i = v.tensor_tensor(out=sr[:, :], in0=sr[:, :],
                                        in1=sp[:, g % 2, 1, :], op=ALU.add)
                    s_z.inc(i)
                VEs(v14)
                assert s_z.n == S_Z0 + g + 1

            def t10(t, it=it, h=h):
                av = (it + 1) * NPAIR - 1
                W(t, "te", s_ex, S_EX0 + av + 1)
                kc0 = 2 * (NPAIR - 1)
                t.matmul(PC[:, :], s_v[:, kc0, h * 128:(h + 1) * 128],
                         sp[:, av % 2, 0, :], start=False, stop=False,
                         skip_group_check=True)
                i = t.matmul(PC[:, :], s_v[:, kc0 + 1, h * 128:(h + 1) * 128],
                             sp[:, av % 2, 1, :], start=False, stop=True,
                             skip_group_check=True)
                s_av.inc(i)
            TEs(t10)
            assert s_av.n == S_AV0 + (it + 1) * NPAIR

            for i8 in range(8):
                pt = PE if i8 % 2 == 0 else PF

                def t11(t, i8=i8, it=it, pt=pt):
                    if i8 < 4:
                        W(t, "te", s_z, S_Z0 + (it + 1) * NPAIR)
                        if i8 < 2:
                            if it >= 1:
                                W(t, "te", s_red, S_RED0 + (it - 1) * SEXP +
                                  (11 if i8 == 0 else 12))
                        else:
                            W(t, "te", s_red, S_RED0 + it * SEXP + (i8 - 1))
                        src = sr[:, i8 * 128:(i8 + 1) * 128]
                    else:
                        if i8 < 6:
                            W(t, "te", s_red, S_RED0 + it * SEXP + 5)
                        else:
                            W(t, "te", s_red, S_RED0 + it * SEXP + (i8 + 3))
                        src = sofm[:, (i8 - 4) * 128:(i8 - 3) * 128]
                    i = t.transpose(pt[:, 0:128], src, ident[:])
                    s_tr.inc(i)
                TEs(t11)

                if i8 < 4:
                    def v15(v, i8=i8, it=it, pt=pt):
                        W(v, "ve", s_tr, S_TR0 + it * STRP + i8 + 1)
                        i = v.reduce_sum(out=szv[:, i8:i8 + 1],
                                         in_=pt[:, 0:128], axis=AX.X)
                        s_red.inc(i)
                    VEs(v15)
                elif i8 == 4:
                    def v16(v, it=it):
                        W(v, "ve", s_av, S_AV0 + (it + 1) * NPAIR)
                        if it >= 1:
                            W(v, "ve", s_tr, S_TR0 + (it - 1) * STRP + 8)
                        i = v.tensor_copy(out=sofm[:, :], in_=PC[:, :])
                        s_red.inc(i)
                        W(v, "ve", s_red, S_RED0 + it * SEXP + 4)
                        i = v.reciprocal(out=szr[:, :], in_=szv[:, :])
                        s_red.inc(i)
                        W(v, "ve", s_red, S_RED0 + it * SEXP + 6)
                        i = v.tensor_scalar(out=szr[:, :], in0=szr[:, :],
                                            scalar1=g_sc[:, 4:5],
                                            scalar2=None, op0=ALU.mult)
                        s_red.inc(i)
                        W(v, "ve", s_tr, S_TR0 + it * STRP + 4)
                        i = v.memset(sr[:, :], 0.0)
                        s_red.inc(i)
                    VEs(v16)
                if i8 >= 4:
                    qc = i8 - 4

                    def v17(v, qc=qc, it=it, pt=pt):
                        W(v, "ve", s_tr, S_TR0 + it * STRP + 5 + qc)
                        W(v, "ve", s_red, S_RED0 + it * SEXP + 7)
                        if (it - 2) in so_dma:
                            W(v, "ve", c_gp, so_dma[it - 2])
                        i = v.tensor_scalar(out=s_o[:, it % 2, qc, :],
                                            in0=pt[:, 0:128],
                                            scalar1=szr[:, qc:qc + 1],
                                            scalar2=None, op0=ALU.mult)
                        s_red.inc(i)
                    VEs(v17)
            assert s_tr.n == S_TR0 + (it + 1) * STRP
            assert s_red.n == S_RED0 + (it + 1) * SEXP

            def g5(g, it=it, h=h, qb=qb):
                W(g, "gp", s_red, S_RED0 + (it + 1) * SEXP)
                dst = ag3_in.ap()[qb * QBS:(qb + 1) * QBS,
                                  h * 128:(h + 1) * 128]
                i = g.dma_start(
                    out=dst.rearrange("(qc p) d -> p qc d", p=128),
                    in_=s_o[:, it % 2, :, :])
                so_dma[it] = c_gp.inc(i, 16)
            GPs(g5)

        def g6(g):
            W(g, "gp", c_gp, c_gp.n)
            c_cc.inc(g.collective_compute(
                "AllGather", ALU.bypass, replica_groups=[list(range(R))],
                ins=[ag3_in.ap()], outs=[ag3_out.ap()]))
        GPs(g6)
        CC_AG3 = c_cc.n

        if stop_phase == 2:
            d_sofm = nc.dram_tensor("dbg_sofm", [128, QBS], F32,
                                    kind="ExternalOutput")
            d_szv = nc.dram_tensor("dbg_szv", [128, 4], F32,
                                   kind="ExternalOutput")
            d_szr = nc.dram_tensor("dbg_szr", [128, 4], F32,
                                   kind="ExternalOutput")
            d_nb = nc.dram_tensor("dbg_negB", [128, 2], F32,
                                  kind="ExternalOutput")
            d_sr = nc.dram_tensor("dbg_sr", [128, QBS], F32,
                                  kind="ExternalOutput")

            def s_d2(sy):
                for (c2, v2) in [(c3, c3.n) for c3 in ALL_CTRS]:
                    W(sy, "sy", c2, v2)
                c_st.inc(sy.dma_start(out=d_sofm.ap(), in_=sofm[:, :]), 16)
                c_st.inc(sy.dma_start(out=d_szv.ap(), in_=szv[:, :]), 16)
                c_st.inc(sy.dma_start(out=d_szr.ap(), in_=szr[:, :]), 16)
                c_st.inc(sy.dma_start(out=d_nb.ap(), in_=g_negB[:, :]), 16)
                c_st.inc(sy.dma_start(out=d_sr.ap(), in_=sr[:, :]), 16)
            SYs(s_d2)
            finish([(c, c.n) for c in ALL_CTRS])
            ph2.close()
            ph1.close()
            qkvT_st.close()
            return nc

        P2_DONE = [(s_red, s_red.n), (s_tr, s_tr.n), (c_gp, c_gp.n),
                   (c_ac, c_ac.n), (c_te, c_te.n), (c_ve, c_ve.n),
                   (c_sy, c_sy.n)]
        ph2.close()
        ph1.close()
        qkvT_st.close()
        ph3 = ExitStack()
        S3 = mkscr(ph3, "3")
        O_in = ph3.enter_context(nc.sbuf_tensor("O_in", [128, NCH, D], BF16))
        Oqfm = ph3.enter_context(nc.sbuf_tensor("Oqfm", [128, 16, TC], BF16))
        wob = ph3.enter_context(nc.sbuf_tensor("wob", [128, 2, 16, 512], BF16))
        dlt = ph3.enter_context(nc.sbuf_tensor("dlt", [128, 2, D], F32))

        def g7(g):
            for (ctr, val) in P2_DONE:
                W(g, "gp", ctr, val)
            W(g, "gp", c_cc, CC_AG3)
            c_gp.inc(g.dma_start(out=pid_sb[:, :],
                                 in_=nc.partition_id_tensor.ap()), 16)
            W(g, "gp", c_gp, c_gp.n)
            with g.register("pidr") as pid_reg:
                g.reg_load(pid_reg, pid_sb[0:1, 0:1])
                with g.register("offr") as off_reg:
                    g.reg_alu(off_reg, pid_reg, NCH, ALU.mult)
                    off = g.snap(off_reg, min_val=0,
                                 max_val=T // 128 - NCH)
                    for r in range(R):
                        src = ag3_out.ap()[r * T:(r + 1) * T, :].rearrange(
                            "(tc p) j -> p tc j", p=128)
                        i = g.dma_start(
                            out=O_in[:, :, r * 256:(r + 1) * 256],
                            in_=src[:, bass.ds(off, NCH), :])
                        c_gp.inc(i, 16)
        GPs(g7)
        OIN_TOK = c_gp.n

        o_te = []
        o_ve = []
        for tc in range(NCH):
            def v18(v, tc=tc):
                for (ctr, val) in P2_DONE:
                    W(v, "ve", ctr, val)
                W(v, "ve", c_gp, OIN_TOK)
                if tc >= 2:
                    W(v, "ve", c_te, o_te[tc - 2][3])
            VEs(v18)
            quant_scale(S3, O_in[:, tc, :], 1e-5, with_rstd=False)
            quant_core(S3, O_in[:, tc, :], scr_x[:, tc % 2, :])
            tt, vv = transpose16(
                lambda tc=tc: scr_x[:, tc % 2, :],
                [Oqfm[:, 4 * g:4 * g + 4, tc * 128:(tc + 1) * 128]
                 for g in range(4)],
                extra_first=P2_DONE if tc == 0 else (),
                ready=(c_ve, c_ve.n),
                prev_ve=(o_ve[tc - 1] if tc >= 1 else None))
            o_te.append(tt)
            o_ve.append(vv)
        OQFM_VE = o_ve[-1][3]

        wo_mm = {}
        dl_ev = {}
        x2_st = {}
        for tcp in range(4):
            for jb in range(4):
                def s6(sy, tcp=tcp, jb=jb):
                    for (ctr, val) in P2_DONE:
                        W(sy, "sy", ctr, val)
                    W(sy, "sy", c_cc, CC_AGW)
                    key = tcp * 4 + jb - 2
                    if key in wo_mm:
                        W(sy, "sy", c_te, wo_mm[key])
                    for rr in range(2):
                        r = 2 * jb + rr
                        i = sy.dma_start(
                            out=wob[:, jb % 2, :, rr * 256:(rr + 1) * 256],
                            in_=agw1_out.ap()[r * D:(r + 1) * D,
                                              0:256].rearrange(
                                "(kc p) j -> p kc j", p=128))
                        c_sy.inc(i, 16)
                SYs(s6)
                WO_LD = c_sy.n

                for tcd in range(2):
                    bank = PC if tcd == 0 else PD
                    tcg = tcp * 2 + tcd

                    def t12(t, tcg=tcg, jb=jb, bank=bank, WO_LD=WO_LD,
                            tcp=tcp, tcd=tcd):
                        W(t, "te", c_sy, WO_LD)
                        W(t, "te", c_ve, OQFM_VE)
                        key = (tcp, jb - 1, tcd) if jb >= 1 else \
                              (tcp - 1, 3, tcd)
                        if key in dl_ev:
                            W(t, "te", c_ve, dl_ev[key])
                        for kc in range(KC):
                            i = t.matmul(
                                bank[:, :],
                                Oqfm[:, kc, tcg * 128:(tcg + 1) * 128],
                                wob[:, jb % 2, kc, :],
                                start=(kc == 0), stop=(kc == KC - 1))
                        c_te.inc(i)
                    TEs(t12)
                    wo_mm[tcp * 4 + jb] = c_te.n
                    MM_TOK = c_te.n

                    def v19(v, tcp=tcp, jb=jb, tcd=tcd, bank=bank,
                            MM_TOK=MM_TOK):
                        W(v, "ve", c_te, MM_TOK)
                        if (tcp - 1) in x2_st:
                            W(v, "ve", c_sy, x2_st[tcp - 1])
                        i = v.tensor_scalar(
                            out=dlt[:, tcd, jb * 512:(jb + 1) * 512],
                            in0=bank[:, :], scalar1=g_sc[:, 5:6],
                            scalar2=None, op0=ALU.mult)
                        dl_ev[(tcp, jb, tcd)] = c_ve.inc(i)
                    VEs(v19)

            for tcd in range(2):
                tcg = tcp * 2 + tcd

                def s7(sy, tcg=tcg, tcd=tcd, tcp=tcp):
                    W(sy, "sy", c_ve, dl_ev[(tcp, 3, tcd)])
                    if tcg >= 2:
                        pass
                    i = sy.dma_start(out=scr_x[:, tcd, :],
                                     in_=x_in[tcg * 128:(tcg + 1) * 128, :])
                    c_sy.inc(i, 16)
                SYs(s7)
                XLD = c_sy.n

                ln_stats(S3, dlt[:, tcd, :], LN_EPS,
                         out_dst=dlt[:, tcd, :], base_ap=scr_x[:, tcd, :],
                         extra_ve_waits=[(c_ve, dl_ev[(tcp, 3, tcd)]),
                                         (c_sy, XLD)])
                LN_TOK = c_ve.n

                def s8(sy, tcg=tcg, tcd=tcd, tcp=tcp, LN_TOK=LN_TOK):
                    W(sy, "sy", c_ve, LN_TOK)
                    i = sy.dma_start(
                        out=x2_dram.ap()[tcg * 128:(tcg + 1) * 128, :],
                        in_=dlt[:, tcd, :])
                    c_sy.inc(i, 16)
                    x2_st[tcp] = c_sy.n
                SYs(s8)

        if stop_phase == 3:
            d_oin = nc.dram_tensor("dbg_oin", [128, NCH, D], BF16,
                                   kind="ExternalOutput")
            d_oq = nc.dram_tensor("dbg_oq", [128, 16, TC], BF16,
                                  kind="ExternalOutput")
            d_dlt = nc.dram_tensor("dbg_dlt", [128, 2, D], F32,
                                   kind="ExternalOutput")

            def s_d3(sy):
                for (c, v) in [(c2, c2.n) for c2 in ALL_CTRS]:
                    W(sy, "sy", c, v)
                c_st.inc(sy.dma_start(out=d_oin.ap(), in_=O_in[:, :, :]), 16)
                c_st.inc(sy.dma_start(out=d_oq.ap(), in_=Oqfm[:, :, :]), 16)
                c_st.inc(sy.dma_start(out=d_dlt.ap(), in_=dlt[:, :, :]), 16)
            SYs(s_d3)
            finish([(c, c.n) for c in ALL_CTRS])
            ph3.close()
            return nc

        P3_DONE = [(c_te, c_te.n), (c_ve, c_ve.n), (c_ac, c_ac.n),
                   (c_sy, c_sy.n), (c_gp, c_gp.n)]
        ph3.close()
        ph4 = ExitStack()
        S4 = mkscr(ph4, "4")
        xq2 = ph4.enter_context(nc.sbuf_tensor("xq2", [128, 16, TC], BF16))
        pbuf = ph4.enter_context(nc.sbuf_tensor("pbuf", [128, NHC, 512], BF16))
        wgu = ph4.enter_context(
            nc.sbuf_tensor("wgu", [128, 2, 2, 16, 128], BF16))
        wdb = ph4.enter_context(nc.sbuf_tensor("wdb", [128, 2, 8, 512], BF16))
        sgl = ph4.enter_context(nc.sbuf_tensor("sgl", [128, 2, 512], BF16))
        dl4 = ph4.enter_context(nc.sbuf_tensor("dl4", [128, 2, D], F32))

        f_te = []
        f_ve = []
        for tc in range(NCH):
            def s9(sy, tc=tc):
                for (ctr, val) in P3_DONE:
                    W(sy, "sy", ctr, val)
                if tc >= 1:
                    W(sy, "sy", c_te, f_te[tc - 1][3])
                i = sy.dma_start(
                    out=scr_x[:, 0, :],
                    in_=x2_dram.ap()[tc * 128:(tc + 1) * 128, :])
                c_sy.inc(i, 16)
            SYs(s9)
            X2LD = c_sy.n

            def v21(v, tc=tc, X2LD=X2LD):
                for (ctr, val) in P3_DONE:
                    W(v, "ve", ctr, val)
                W(v, "ve", c_sy, X2LD)
            VEs(v21)
            ln_stats(S4, scr_x[:, 0, :], LN_EPS, out_dst=scr_x[:, 1, :])
            ln_stats(S4, scr_x[:, 1, :], RMS_EPS)
            quant_scale(S4, S4.a[:], None, with_rstd=True)
            quant_core(S4, S4.a[:], scr_x[:, 0, :])
            tt, vv = transpose16(
                lambda: scr_x[:, 0, :],
                [xq2[:, 4 * g:4 * g + 4, tc * 128:(tc + 1) * 128]
                 for g in range(4)],
                extra_first=P3_DONE if tc == 0 else (),
                ready=(c_ve, c_ve.n),
                prev_ve=(f_ve[tc - 1] if tc >= 1 else None))
            f_te.append(tt)
            f_ve.append(vv)
        XQ2_VE = c_ve.n

        y_chunk = 0
        down_done = [None]
        for tb in range(2):
            gu_ld = {}
            p_mul = {}
            silu_tok = {}
            for hc in range(NHC):
                gbank = PC if hc % 2 == 0 else PE
                ubank = PD if hc % 2 == 0 else PF

                def s10(sy, tb=tb, hc=hc):
                    for (ctr, val) in P3_DONE:
                        W(sy, "sy", ctr, val)
                    W(sy, "sy", c_cc, CC_AGW)
                    if (tb, hc - 2) in p_mul:
                        W(sy, "sy", c_ve, p_mul[(tb, hc - 2)])
                    r = hc // 8
                    col = (hc % 8) * 128
                    for wsel, base in ((0, 256), (1, 1280)):
                        i = sy.dma_start(
                            out=wgu[:, hc % 2, wsel, :, :],
                            in_=agw1_out.ap()[r * D:(r + 1) * D,
                                              base + col:base + col + 128
                                              ].rearrange(
                                "(kc p) h -> p kc h", p=128))
                        c_sy.inc(i, 16)
                    gu_ld[hc] = c_sy.n
                SYs(s10)

                def t13(t, tb=tb, hc=hc, gbank=gbank, ubank=ubank):
                    for (ctr, val) in P3_DONE:
                        W(t, "te", ctr, val)
                    W(t, "te", c_sy, gu_ld[hc])
                    W(t, "te", c_ve, XQ2_VE)
                    if tb == 1 and down_done[0] is not None:
                        W(t, "te", c_te, down_done[0])
                    if (tb, hc - 2) in silu_tok:
                        W(t, "te", c_ac, silu_tok[(tb, hc - 2)])
                        W(t, "te", c_ve, p_mul[(tb, hc - 2)])
                    for kc in range(KC):
                        t.matmul(gbank[:, :], wgu[:, hc % 2, 0, kc, :],
                                 xq2[:, kc, tb * 512:(tb + 1) * 512],
                                 start=(kc == 0), stop=(kc == KC - 1))
                    for kc in range(KC):
                        i = t.matmul(ubank[:, :], wgu[:, hc % 2, 1, kc, :],
                                     xq2[:, kc, tb * 512:(tb + 1) * 512],
                                     start=(kc == 0), stop=(kc == KC - 1))
                    c_te.inc(i)
                TEs(t13)
                GU_MM = c_te.n

                def a6(sc, hc=hc, gbank=gbank, GU_MM=GU_MM):
                    W(sc, "ac", c_te, GU_MM)
                    i = sc.activation(out=sgl[:, hc % 2, :],
                                      in_=gbank[:, :], func=AF.Silu)
                    c_ac.inc(i)
                ACs(a6)
                silu_tok[(tb, hc)] = c_ac.n

                def v22(v, tb=tb, hc=hc, ubank=ubank):
                    W(v, "ve", c_ac, silu_tok[(tb, hc)])
                    i = v.tensor_tensor(out=pbuf[:, hc, :],
                                        in0=sgl[:, hc % 2, :],
                                        in1=ubank[:, :], op=ALU.mult)
                    p_mul[(tb, hc)] = c_ve.inc(i)
                VEs(v22)

            P_DONE = p_mul[(tb, NHC - 1)]

            wd_ld = {}
            acc_ev = {}
            for tcp in range(2):
                for db in range(4):
                    for hg in range(8):
                        def s11(sy, tb=tb, tcp=tcp, db=db, hg=hg):
                            for (ctr, val) in P3_DONE:
                                W(sy, "sy", ctr, val)
                            seq = (tcp * 4 + db) * 8 + hg
                            if seq >= 2 and (seq - 2) in wd_ld:
                                W(sy, "sy", c_te, wd_ld[seq - 2][1])
                            for rr in range(2):
                                r = 2 * db + rr
                                i = sy.dma_start(
                                    out=wdb[:, hg % 2, :,
                                            rr * 256:(rr + 1) * 256],
                                    in_=agw2_out.ap()[
                                        r * HFF + hg * 1024:
                                        r * HFF + (hg + 1) * 1024,
                                        :].rearrange("(hc p) d -> p hc d",
                                                     p=128))
                                c_sy.inc(i, 16)
                        SYs(s11)
                        WD_LD = c_sy.n

                        def t14(t, tb=tb, tcp=tcp, db=db, hg=hg,
                                WD_LD=WD_LD, P_DONE=P_DONE):
                            W(t, "te", c_sy, WD_LD)
                            W(t, "te", c_ve, P_DONE)
                            if hg == 0:
                                for tcd in range(2):
                                    key = (tcp, db - 1, tcd) if db >= 1 \
                                        else (tcp - 1, 3, tcd)
                                    if key in acc_ev:
                                        W(t, "te", c_ve, acc_ev[key])
                            for h8 in range(8):
                                hc = hg * 8 + h8
                                for tcd in range(2):
                                    tcl = tcp * 2 + tcd
                                    bank = PC if tcd == 0 else PD
                                    i = t.matmul(
                                        bank[:, :],
                                        pbuf[:, hc,
                                             tcl * 128:(tcl + 1) * 128],
                                        wdb[:, hg % 2, h8, :],
                                        start=(hc == 0), stop=(hc == NHC - 1),
                                        skip_group_check=True)
                            c_te.inc(i)
                        TEs(t14)
                        wd_ld[(tcp * 4 + db) * 8 + hg] = (WD_LD, c_te.n)
                    DP_MM = c_te.n

                    for tcd in range(2):
                        bank = PC if tcd == 0 else PD

                        def v23(v, tcp=tcp, db=db, tcd=tcd, bank=bank,
                                DP_MM=DP_MM):
                            W(v, "ve", c_te, DP_MM)
                            i = v.tensor_copy(
                                out=dl4[:, tcd, db * 512:(db + 1) * 512],
                                in_=bank[:, :])
                            acc_ev[(tcp, db, tcd)] = c_ve.inc(i)
                        VEs(v23)

                for tcd in range(2):
                    tcg = tb * 4 + tcp * 2 + tcd

                    def s12(sy, tcg=tcg, tcd=tcd, tcp=tcp):
                        W(sy, "sy", c_ve, acc_ev[(tcp, 3, tcd)])
                        i = sy.dma_start(
                            out=scr_x[:, tcd, :],
                            in_=x2_dram.ap()[tcg * 128:(tcg + 1) * 128, :])
                        c_sy.inc(i, 16)
                    SYs(s12)
                    XLD2 = c_sy.n

                    ln_stats(S4, dl4[:, tcd, :], LN_EPS,
                             out_dst=dl4[:, tcd, :],
                             base_ap=scr_x[:, tcd, :],
                             extra_ve_waits=[(c_ve, acc_ev[(tcp, 3, tcd)]),
                                             (c_sy, XLD2)])
                    LN_TOK4 = c_ve.n

                    def s13(sy, tcg=tcg, tcd=tcd, LN_TOK4=LN_TOK4):
                        W(sy, "sy", c_ve, LN_TOK4)
                        i = sy.dma_start(
                            out=y_out[tcg * 128:(tcg + 1) * 128, :],
                            in_=dl4[:, tcd, :])
                        c_st.inc(i, 16)
                    SYs(s13)
                    acc_ev[("store", tcp, tcd)] = None
            down_done[0] = c_te.n

        def s_end(sy):
            sy.wait_ge(c_st.h, c_st.n)
            for name, outh in dbg_outs.items():
                i = sy.dma_start(out=outh.ap(), in_=internals[name].ap())
                c_st.inc(i, 16)
            sy.wait_ge(c_st.h, c_st.n)
        SYs(s_end)
        ph4.close()

    return nc


def kernel(L0_attn_x, L0_attn_w_q, L0_attn_w_k, L0_attn_w_v, L0_attn_w_o,
           L0_ffn_W_g, L0_ffn_W_u, L0_ffn_W_d):
    x = np.ascontiguousarray(np.asarray(L0_attn_x, dtype=np.float32)
                             .reshape(T, D))
    wq = np.ascontiguousarray(np.asarray(L0_attn_w_q, dtype=np.float32))
    wk = np.ascontiguousarray(np.asarray(L0_attn_w_k, dtype=np.float32))
    wv = np.ascontiguousarray(np.asarray(L0_attn_w_v, dtype=np.float32))
    wo = np.ascontiguousarray(np.asarray(L0_attn_w_o, dtype=np.float32))
    wg = np.ascontiguousarray(np.asarray(L0_ffn_W_g, dtype=np.float32))
    wu = np.ascontiguousarray(np.asarray(L0_ffn_W_u, dtype=np.float32))
    wd = np.ascontiguousarray(np.asarray(L0_ffn_W_d, dtype=np.float32))

    nc = build()
    in_maps = []
    for c in range(R):
        in_maps.append({
            "x": x[c * TC:(c + 1) * TC],
            "wq": wq[c * 256:(c + 1) * 256],
            "wk": wk[c * 256:(c + 1) * 256],
            "wv": wv[c * 256:(c + 1) * 256],
            "wo": wo[c * 256:(c + 1) * 256],
            "wg": wg[c * (HFF // R):(c + 1) * (HFF // R)],
            "wu": wu[c * (HFF // R):(c + 1) * (HFF // R)],
            "wd": wd[c * (D // R):(c + 1) * (D // R)],
        })
    res = run_bass_kernel_spmd(nc, in_maps, core_ids=list(range(R)))
    y = np.concatenate([res.results[c]["y"] for c in range(R)], axis=0)
    return y.reshape(1, T, D)


# version 04
# speedup vs baseline: 1.1621x; 1.1621x over previous
import numpy as np
import concourse.bass as bass
import concourse.mybir as mybir
from concourse.bass_utils import run_bass_kernel_spmd
from contextlib import ExitStack

F32 = mybir.dt.float32
BF16 = mybir.dt.bfloat16
I32 = mybir.dt.int32
U32 = mybir.dt.uint32
AF = mybir.ActivationFunctionType
ALU = mybir.AluOpType
AX = mybir.AxisListType

R = 8
T = 8192
TC = T // R
D = 2048
DH = 128
HL = 2
HFF = 8192
NKC = T // 128
NPAIR = NKC // 2
QBS = 512
NQB = T // QBS
NIT = HL * NQB
SCALE = float(DH) ** -0.5
NCH = TC // 128
KC = D // 128
NHC = HFF // 128

LN_EPS = 1e-5
RMS_EPS = 1e-8


class Ctr:
    def __init__(self, nc, ctx, name):
        self.h = ctx.enter_context(nc.semaphore(name))
        self.n = 0
        self.name = name

    def inc(self, instr, k=1):
        instr.then_inc(self.h, k)
        self.n += k
        return self.n


def build(stop_phase=None, dbg=()):
    nc = bass.Bass("TRN2", target_bir_lowering=False)

    x_in = nc.dram_tensor("x", [TC, D], F32, kind="ExternalInput").ap()
    wq_in = nc.dram_tensor("wq", [2 * DH, D], F32, kind="ExternalInput").ap()
    wk_in = nc.dram_tensor("wk", [2 * DH, D], F32, kind="ExternalInput").ap()
    wv_in = nc.dram_tensor("wv", [2 * DH, D], F32, kind="ExternalInput").ap()
    wo_in = nc.dram_tensor("wo", [2 * DH, D], F32, kind="ExternalInput").ap()
    wg_in = nc.dram_tensor("wg", [HFF // R, D], F32, kind="ExternalInput").ap()
    wu_in = nc.dram_tensor("wu", [HFF // R, D], F32, kind="ExternalInput").ap()
    wd_in = nc.dram_tensor("wd", [D // R, HFF], F32, kind="ExternalInput").ap()
    y_out = nc.dram_tensor("y", [TC, D], F32, kind="ExternalOutput").ap()

    ag0_in = nc.dram_tensor("ag0_in", [1, 4], F32)
    ag0_out = nc.dram_tensor("ag0_out", [R, 4], F32)
    agw1_in = nc.dram_tensor("agw1_in", [D, 2304], BF16)
    agw1_out = nc.dram_tensor("agw1_out", [R * D, 2304], BF16)
    agw2_in = nc.dram_tensor("agw2_in", [HFF, 256], BF16)
    agw2_out = nc.dram_tensor("agw2_out", [R * HFF, 256], BF16)
    ag2_in = nc.dram_tensor("ag2_in", [D, TC], BF16)
    ag2_out = nc.dram_tensor("ag2_out", [R * D, TC], BF16)
    ag3_in = nc.dram_tensor("ag3_in", [T, 2 * DH], BF16)
    ag3_out = nc.dram_tensor("ag3_out", [R * T, 2 * DH], BF16)
    x2_dram = nc.dram_tensor("x2_dram", [TC, D], F32)
    internals = {"ag0_out": ag0_out, "agw1_out": agw1_out,
                 "agw2_out": agw2_out, "ag2_out": ag2_out,
                 "ag3_out": ag3_out, "x2_dram": x2_dram,
                 "ag2_in": ag2_in, "ag3_in": ag3_in, "agw1_in": agw1_in}
    dbg_outs = {}
    for name in dbg:
        h = internals[name]
        dbg_outs[name] = nc.dram_tensor(
            "dbg_" + name, list(h.shape), h.dtype, kind="ExternalOutput")

    hwm = {}
    top = ExitStack()
    with top:
        PA = top.enter_context(nc.psum_tensor("PA", [128, 1024], F32))
        PB = top.enter_context(nc.psum_tensor("PB", [128, 1024], F32))
        PC = top.enter_context(nc.psum_tensor("PC", [128, 512], F32))
        PD = top.enter_context(nc.psum_tensor("PD", [128, 512], F32))
        PE = top.enter_context(nc.psum_tensor("PE", [128, 512], F32))
        PF = top.enter_context(nc.psum_tensor("PF", [128, 512], F32))

        c_sy = Ctr(nc, top, "c_sy")
        c_gp = Ctr(nc, top, "c_gp")
        c_cc = Ctr(nc, top, "c_cc")
        c_te = Ctr(nc, top, "c_te")
        c_ve = Ctr(nc, top, "c_ve")
        c_ac = Ctr(nc, top, "c_ac")
        s_sc = Ctr(nc, top, "s_sc")
        s_ex = Ctr(nc, top, "s_ex")
        s_av = Ctr(nc, top, "s_av")
        s_z = Ctr(nc, top, "s_z")
        s_tr = Ctr(nc, top, "s_tr")
        s_red = Ctr(nc, top, "s_red")
        c_st = Ctr(nc, top, "c_st")
        ALL_CTRS = [c_sy, c_gp, c_cc, c_te, c_ve, c_ac, s_sc, s_ex, s_av,
                    s_z, s_tr, s_red]

        ones_r = top.enter_context(nc.sbuf_tensor("ones_r", [1, 128], F32))
        ones_c = top.enter_context(nc.sbuf_tensor("ones_c", [128, 1], F32))
        ident = top.enter_context(nc.sbuf_tensor("ident", [128, 128], F32))
        g_sc = top.enter_context(nc.sbuf_tensor("g_sc", [128, 8], F32))
        g_scn = top.enter_context(nc.sbuf_tensor("g_scn", [128, 4], F32))
        g_negB = top.enter_context(nc.sbuf_tensor("g_negB", [128, 2], F32))
        scr_x = top.enter_context(nc.sbuf_tensor("scr_x", [128, 2, D], F32))
        pid_sb = top.enter_context(nc.sbuf_tensor("pid_sb", [1, 1], U32))
        blk = top.enter_context(nc.Block())

        def W(eng, ename, ctr, val):
            if val is None or val <= 0:
                return
            key = (ename, ctr.name)
            if hwm.get(key, 0) >= val:
                return
            eng.wait_ge(ctr.h, val)
            hwm[key] = val

        def mkscr(stack, tag):
            class S:
                pass
            s = S()
            s.a = stack.enter_context(nc.sbuf_tensor(f"sa{tag}", [128, D], F32))
            s.b = stack.enter_context(nc.sbuf_tensor(f"sb{tag}", [128, D], F32))
            s.c = stack.enter_context(nc.sbuf_tensor(f"sc{tag}", [128, D], I32))
            s.d = stack.enter_context(nc.sbuf_tensor(f"sd{tag}", [128, D], F32))
            s.e = stack.enter_context(nc.sbuf_tensor(f"se{tag}", [128, D], BF16))
            s.f = stack.enter_context(nc.sbuf_tensor(f"sf{tag}", [128, D], BF16))
            s.st = stack.enter_context(nc.sbuf_tensor(f"st{tag}", [128, 16], F32))
            return s

        TEs, VEs, ACs, SYs, GPs = (blk.tensor, blk.vector, blk.scalar,
                                   blk.sync, blk.gpsimd)

        def finish(gates):
            def s_fin(sy):
                for (ctr, val) in gates:
                    W(sy, "sy", ctr, val)
                for name, outh in dbg_outs.items():
                    i = sy.dma_start(out=outh.ap(), in_=internals[name].ap())
                    c_st.inc(i, 16)
                if c_st.n:
                    sy.wait_ge(c_st.h, c_st.n)
            SYs(s_fin)

        def ln_stats(S, src_ap, eps, out_dst=None, base_ap=None,
                     extra_ve_waits=()):
            def v1(v):
                for (ctr, val) in extra_ve_waits:
                    W(v, "ve", ctr, val)
                c_ve.inc(v.reduce_sum(out=S.st[:, 0:1], in_=src_ap,
                                      axis=AX.X))
                W(v, "ve", c_ve, c_ve.n)
                c_ve.inc(v.tensor_scalar(out=S.st[:, 1:2], in0=S.st[:, 0:1],
                                         scalar1=1.0 / D, scalar2=None,
                                         op0=ALU.mult))
                W(v, "ve", c_ve, c_ve.n)
                v.tensor_scalar(out=S.a[:], in0=src_ap,
                                scalar1=S.st[:, 1:2], scalar2=None,
                                op0=ALU.subtract)
                v.tensor_tensor(out=S.b[:], in0=S.a[:], in1=S.a[:],
                                op=ALU.mult)
                c_ve.inc(v.reduce_sum(out=S.st[:, 2:3], in_=S.b[:],
                                      axis=AX.X))
                W(v, "ve", c_ve, c_ve.n)
                c_ve.inc(v.tensor_scalar(out=S.st[:, 3:4], in0=S.st[:, 2:3],
                                         scalar1=1.0 / D, scalar2=eps,
                                         op0=ALU.mult, op1=ALU.add))
            VEs(v1)

            def a1(sc):
                W(sc, "ac", c_ve, c_ve.n)
                c_ac.inc(sc.activation(out=S.st[:, 4:5], in_=S.st[:, 3:4],
                                       func=AF.Sqrt))
            ACs(a1)

            def v2(v):
                W(v, "ve", c_ac, c_ac.n)
                c_ve.inc(v.reciprocal(out=S.st[:, 5:6], in_=S.st[:, 4:5]))
                W(v, "ve", c_ve, c_ve.n)
                if out_dst is not None:
                    if base_ap is None:
                        i = v.tensor_scalar(out=out_dst, in0=S.a[:],
                                            scalar1=S.st[:, 5:6],
                                            scalar2=None, op0=ALU.mult)
                    else:
                        i = v.scalar_tensor_tensor(out=out_dst, in0=S.a[:],
                                                   scalar=S.st[:, 5:6],
                                                   in1=base_ap, op0=ALU.mult,
                                                   op1=ALU.add)
                    c_ve.inc(i)
            VEs(v2)

        def quant_scale(S, src_abs_ap, clip_eps, with_rstd):
            def v1(v):
                c_ve.inc(v.reduce_max(out=S.st[:, 6:7], in_=src_abs_ap,
                                      axis=AX.X, apply_absolute_value=True))
                W(v, "ve", c_ve, c_ve.n)
                if with_rstd:
                    c_ve.inc(v.tensor_tensor(out=S.st[:, 7:8],
                                             in0=S.st[:, 6:7],
                                             in1=S.st[:, 5:6], op=ALU.mult))
                else:
                    c_ve.inc(v.tensor_copy(out=S.st[:, 7:8],
                                           in_=S.st[:, 6:7]))
                W(v, "ve", c_ve, c_ve.n)
                if clip_eps is not None:
                    c_ve.inc(v.tensor_scalar(out=S.st[:, 7:8],
                                             in0=S.st[:, 7:8],
                                             scalar1=clip_eps, scalar2=None,
                                             op0=ALU.max))
                    W(v, "ve", c_ve, c_ve.n)
                v.tensor_scalar(out=S.st[:, 8:9], in0=S.st[:, 7:8],
                                scalar1=1.0 / 127.0, scalar2=None,
                                op0=ALU.mult)
                c_ve.inc(v.reciprocal(out=S.st[:, 9:10], in_=S.st[:, 7:8]))
                W(v, "ve", c_ve, c_ve.n)
                c_ve.inc(v.tensor_scalar(out=S.st[:, 10:11],
                                         in0=S.st[:, 9:10], scalar1=127.0,
                                         scalar2=None, op0=ALU.mult))
                W(v, "ve", c_ve, c_ve.n)
                if with_rstd:
                    c_ve.inc(v.tensor_tensor(out=S.st[:, 11:12],
                                             in0=S.st[:, 5:6],
                                             in1=S.st[:, 10:11],
                                             op=ALU.mult))
                else:
                    c_ve.inc(v.tensor_copy(out=S.st[:, 11:12],
                                           in_=S.st[:, 10:11]))
                W(v, "ve", c_ve, c_ve.n)
            VEs(v1)

        def quant_core(S, src_ap, dst_f32):
            def a1(sc):
                W(sc, "ac", c_ve, c_ve.n)
                sc.activation(out=S.b[:], in_=src_ap, func=AF.Abs)
                c_ac.inc(sc.activation(out=S.f[:], in_=src_ap, func=AF.Sign))
            ACs(a1)

            def v1(v):
                W(v, "ve", c_ac, c_ac.n)
                v.tensor_scalar(out=S.d[:], in0=S.b[:],
                                scalar1=S.st[:, 11:12], scalar2=None,
                                op0=ALU.mult)
                v.tensor_scalar(out=S.c[:], in0=S.b[:],
                                scalar1=S.st[:, 11:12], scalar2=None,
                                op0=ALU.mult)
                v.scalar_tensor_tensor(out=S.a[:], in0=S.c[:], scalar=-1.0,
                                       in1=S.d[:], op0=ALU.mult, op1=ALU.add)
                v.tensor_scalar(out=S.e[:], in0=S.a[:], scalar1=0.5,
                                scalar2=None, op0=ALU.is_ge)
                v.tensor_tensor(out=S.d[:], in0=S.c[:], in1=S.e[:],
                                op=ALU.add)
                v.tensor_scalar(out=S.b[:], in0=S.f[:],
                                scalar1=S.st[:, 8:9], scalar2=None,
                                op0=ALU.mult)
                c_ve.inc(v.tensor_tensor(out=dst_f32, in0=S.d[:],
                                         in1=S.b[:], op=ALU.mult))
            VEs(v1)

        def transpose16(src_getter, dst_slices, extra_first=(),
                        ready=None, prev_ve=None):
            te_tok = []
            ve_tok = []
            for grp in range(4):
                pt = PE if grp % 2 == 0 else PF

                def t1(t, grp=grp, pt=pt):
                    for (ctr, val) in extra_first:
                        W(t, "te", ctr, val)
                    if ready is not None:
                        W(t, "te", ready[0], ready[1])
                    if grp >= 2:
                        W(t, "te", c_ve, ve_tok[grp - 2])
                    elif prev_ve is not None:
                        W(t, "te", c_ve, prev_ve[grp + 2])
                    src = src_getter()
                    for j in range(4):
                        kc = grp * 4 + j
                        i = t.transpose(
                            pt.ap().rearrange("p (g c) -> p g c",
                                                     g=4)[:, j, :],
                            src[:, kc * 128:(kc + 1) * 128], ident[:])
                    te_tok.append(c_te.inc(i))
                TEs(t1)

                def v1(v, grp=grp, pt=pt):
                    W(v, "ve", c_te, te_tok[grp])
                    i = v.tensor_copy(
                        out=dst_slices[grp],
                        in_=pt.ap().rearrange("p (g c) -> p g c",
                                                     g=4))
                    ve_tok.append(c_ve.inc(i))
                VEs(v1)
            return te_tok, ve_tok

        qkvT_st = ExitStack()
        qkvT = qkvT_st.enter_context(
            nc.sbuf_tensor("qkvT", [128, 16, 768], BF16))
        ph0 = ExitStack()
        S0 = mkscr(ph0, "0")
        pstage = [ph0.enter_context(
            nc.sbuf_tensor(f"pstage{i}", [128, 4, 128], BF16))
            for i in range(16)]
        pcol = ph0.enter_context(nc.sbuf_tensor("pcol", [128, 8], F32))
        prow = ph0.enter_context(nc.sbuf_tensor("prow", [1, 8], F32))
        pag0 = ph0.enter_context(nc.sbuf_tensor("pag0", [8, 4], F32))
        pcv = ph0.enter_context(nc.sbuf_tensor("pcv", [1, 8], F32))
        xqfm = ph0.enter_context(nc.sbuf_tensor("xqfm", [128, 16, TC], BF16))

        def g0(g):
            g.memset(ones_r[:], 1.0)
            g.memset(ones_c[:], 1.0)
            g.memset(ident[:], 0.0)
            i = g.affine_select(out=ident[:], in_=ident[:],
                                compare_op=ALU.not_equal, fill=1.0, base=0,
                                pattern=[[-1, 128]], channel_multiplier=1)
            c_gp.inc(i, 1)
        GPs(g0)
        GP_CONSTS = c_gp.n

        abs_srcs = []
        for ap in (wq_in, wk_in, wv_in, wo_in):
            for ch in range(2):
                abs_srcs.append(ap[ch * 128:(ch + 1) * 128, :])
        abs_ld = []
        abs_red = []
        for i8, src in enumerate(abs_srcs):
            def s1(sy, i8=i8, src=src):
                if i8 >= 2:
                    W(sy, "sy", c_ve, abs_red[i8 - 2])
                c_sy.inc(sy.dma_start(out=scr_x[:, i8 % 2, :], in_=src), 16)
                abs_ld.append(c_sy.n)
            SYs(s1)

            def v1(v, i8=i8):
                W(v, "ve", c_sy, abs_ld[i8])
                abs_red.append(c_ve.inc(
                    v.reduce_sum(out=pcol[:, i8:i8 + 1],
                                 in_=scr_x[:, i8 % 2, :], axis=AX.X,
                                 apply_absolute_value=True)))
            VEs(v1)

        def t2(t):
            W(t, "te", c_ve, abs_red[-1])
            W(t, "te", c_gp, GP_CONSTS)
            c_te.inc(t.matmul(PE[0:1, 0:8], ones_c[:, :], pcol[:, :],
                              start=True, stop=True))
        TEs(t2)
        TE_ABS = c_te.n

        def v3(v):
            W(v, "ve", c_te, TE_ABS)
            c_ve.inc(v.tensor_copy(out=prow[:, :], in_=PE[0:1, 0:8]))
            W(v, "ve", c_ve, c_ve.n)
            pr = prow.ap()
            c_ve.inc(v.tensor_tensor(out=prow[:, 0:4], in0=pr[0:1, 0:8:2],
                                     in1=pr[0:1, 1:8:2], op=ALU.add))
        VEs(v3)
        VE_PROW = c_ve.n

        def g1(g):
            W(g, "gp", c_ve, VE_PROW)
            c_gp.inc(g.dma_start(out=ag0_in.ap(), in_=prow[:, 0:4]), 16)
            W(g, "gp", c_gp, c_gp.n)
            c_cc.inc(g.collective_compute(
                "AllGather", ALU.bypass, replica_groups=[list(range(R))],
                ins=[ag0_in.ap()], outs=[ag0_out.ap()]))
        GPs(g1)
        CC_AG0 = c_cc.n

        def s2(sy):
            W(sy, "sy", c_cc, CC_AG0)
            c_sy.inc(sy.dma_start(out=pag0[:, :], in_=ag0_out.ap()), 16)
        SYs(s2)
        SY_PAG0 = c_sy.n

        def t3(t):
            W(t, "te", c_sy, SY_PAG0)
            c_te.inc(t.matmul(PF[0:1, 0:4], ones_c[0:8, :], pag0[:, :],
                              start=True, stop=True))
        TEs(t3)
        TE_AG0 = c_te.n

        def v4(v):
            W(v, "ve", c_te, TE_AG0)
            c_ve.inc(v.tensor_scalar(out=pcv[:, 0:4], in0=PF[0:1, 0:4],
                                     scalar1=1.0 / (D * D), scalar2=1e-5,
                                     op0=ALU.mult, op1=ALU.max))
            W(v, "ve", c_ve, c_ve.n)
            c_ve.inc(v.tensor_scalar(out=pcv[:, 4:8], in0=pcv[:, 0:4],
                                     scalar1=0.5, scalar2=None,
                                     op0=ALU.mult))
            W(v, "ve", c_ve, c_ve.n)
            c_ve.inc(v.tensor_tensor(out=pcv[:, 0:1], in0=pcv[:, 0:1],
                                     in1=pcv[:, 1:2], op=ALU.mult))
            W(v, "ve", c_ve, c_ve.n)
            c_ve.inc(v.tensor_scalar(out=pcv[:, 1:2], in0=pcv[:, 0:1],
                                     scalar1=SCALE, scalar2=None,
                                     op0=ALU.mult))
        VEs(v4)
        VE_PCV = c_ve.n

        def t4(t):
            W(t, "te", c_ve, VE_PCV)
            t.matmul(PF[:, 8:12], ones_r[:, :], pcv[0:1, 4:8],
                     start=True, stop=True)
            t.matmul(PF[:, 12:14], ones_r[:, :], pcv[0:1, 2:4],
                     start=True, stop=True)
            c_te.inc(t.matmul(PF[:, 14:15], ones_r[:, :], pcv[0:1, 1:2],
                              start=True, stop=True))
        TEs(t4)
        TE_BCS = c_te.n

        def v5(v):
            W(v, "ve", c_te, TE_BCS)
            c_ve.inc(v.tensor_copy(out=g_sc[:, 0:7], in_=PF[:, 8:15]))
            W(v, "ve", c_ve, c_ve.n)
            c_ve.inc(v.tensor_scalar(out=g_scn[:, 0:4], in0=g_sc[:, 0:4],
                                     scalar1=-1.0, scalar2=None,
                                     op0=ALU.mult))
        VEs(v5)
        VE_SCALES = c_ve.n

        x_te = []
        x_ve = []
        xq_ld = []
        for tc in range(NCH):
            def s4(sy, tc=tc):
                if tc == 0:
                    W(sy, "sy", c_ve, abs_red[6])
                elif tc == 1:
                    W(sy, "sy", c_ve, abs_red[7])
                else:
                    W(sy, "sy", c_te, x_te[tc - 2][3])
                c_sy.inc(sy.dma_start(out=scr_x[:, tc % 2, :],
                                      in_=x_in[tc * 128:(tc + 1) * 128, :]),
                         16)
                xq_ld.append(c_sy.n)
            SYs(s4)

            def v8(v, tc=tc):
                W(v, "ve", c_sy, xq_ld[tc])
            VEs(v8)
            ln_stats(S0, scr_x[:, tc % 2, :], LN_EPS)
            quant_scale(S0, S0.a[:], 1e-5, with_rstd=True)
            quant_core(S0, S0.a[:], scr_x[:, tc % 2, :])
            tt, vv = transpose16(
                lambda tc=tc: scr_x[:, tc % 2, :],
                [xqfm[:, 4 * g:4 * g + 4, tc * 128:(tc + 1) * 128]
                 for g in range(4)],
                ready=(c_ve, c_ve.n),
                prev_ve=(x_ve[tc - 1] if tc >= 1 else None))
            x_te.append(tt)
            x_ve.append(vv)
        XQ_VE_END = c_ve.n

        def g3(g):
            W(g, "gp", c_ve, x_ve[-1][3])
            i = g.dma_start(
                out=ag2_in.ap().rearrange("(kc p) t -> p kc t", p=128),
                in_=xqfm[:, :, :])
            c_gp.inc(i, 16)
            W(g, "gp", c_gp, c_gp.n)
            c_cc.inc(g.collective_compute(
                "AllGather", ALU.bypass, replica_groups=[list(range(R))],
                ins=[ag2_in.ap()], outs=[ag2_out.ap()]))
        GPs(g3)
        CC_AG2 = CC_AG0 + 1
        CC_AGW = CC_AG0 + 3

        pieces = []
        for ch in range(8):
            pieces.append(("sign", "wg", wg_in[ch * 128:(ch + 1) * 128, :],
                           None, ch, 0))
        for ch in range(8):
            pieces.append(("sign", "wu", wu_in[ch * 128:(ch + 1) * 128, :],
                           None, ch, 0))
        for ch in range(2):
            for cp in range(4):
                pieces.append(("sign", "wd",
                               wd_in[ch * 128:(ch + 1) * 128,
                                     cp * D:(cp + 1) * D], None, ch, cp))
        for wi, (nm, ap) in enumerate([("wq", wq_in), ("wk", wk_in),
                                       ("wv", wv_in), ("wo", wo_in)]):
            for ch in range(2):
                pieces.append(("tern", nm, ap[ch * 128:(ch + 1) * 128, :],
                               wi, ch, 0))

        pld = []
        pq_tok = []
        p_te = []
        p_ve = []
        pst_tok = {}
        pstage_last = {}
        pending_stage = []
        stage_ix = [0]

        def flush_stage(extra):
            while pending_stage:
                fpi, fnm, fch, fcp, fve, fbx = pending_stage.pop(0)

                def g2(g, fpi=fpi, fnm=fnm, fch=fch, fcp=fcp, fve=fve,
                       fbx=fbx, extra=extra):
                    if extra is not None:
                        W(g, "sy", extra[0], extra[1])
                    for grp in range(4):
                        W(g, "sy", c_ve, fve[grp])
                        if fnm == "wo":
                            dst = agw1_in.ap()[grp * 512:(grp + 1) * 512,
                                               fch * 128:(fch + 1) * 128]
                        elif fnm == "wg":
                            dst = agw1_in.ap()[grp * 512:(grp + 1) * 512,
                                               256 + fch * 128:
                                               256 + (fch + 1) * 128]
                        elif fnm == "wu":
                            dst = agw1_in.ap()[grp * 512:(grp + 1) * 512,
                                               1280 + fch * 128:
                                               1280 + (fch + 1) * 128]
                        else:
                            dst = agw2_in.ap()[fcp * D + grp * 512:
                                               fcp * D + (grp + 1) * 512,
                                               fch * 128:(fch + 1) * 128]
                        slot = (fbx + grp) % 16
                        i = g.dma_start(
                            out=dst.rearrange("(g p) c -> p g c", p=128),
                            in_=pstage[slot][:, :, :])
                        pst_tok[(fpi, grp)] = c_sy.inc(i, 16)
                        pstage_last[slot] = pst_tok[(fpi, grp)]
                SYs(g2)

        for pi, (kind, nm, src, wi, ch, cp) in enumerate(pieces):
            if pi >= 1 and pending_stage:
                pass

            def s3(sy, pi=pi, src=src):
                if pi == 0:
                    W(sy, "sy", c_te, x_te[6][3])
                elif pi == 1:
                    W(sy, "sy", c_te, x_te[7][3])
                else:
                    ctr2, v2 = pq_tok[pi - 2]
                    W(sy, "sy", ctr2, v2)
                c_sy.inc(sy.dma_start(out=scr_x[:, pi % 2, :], in_=src), 16)
                pld.append(c_sy.n)
            SYs(s3)

            if kind == "sign":
                def a2(sc, pi=pi):
                    W(sc, "ac", c_sy, pld[pi])
                    if pi >= 1:
                        W(sc, "ac", c_te, p_te[pi - 1][3])
                    else:
                        W(sc, "ac", c_ve, XQ_VE_END)
                        W(sc, "ac", c_te, x_te[-1][3])
                    c_ac.inc(sc.activation(out=S0.d[:],
                                           in_=scr_x[:, pi % 2, :],
                                           func=AF.Sign))
                ACs(a2)
                pq_tok.append((c_ac, c_ac.n))
            else:
                def v6(v, pi=pi, wi=wi):
                    W(v, "ve", c_sy, pld[pi])
                    W(v, "ve", c_ve, VE_SCALES)
                    if pi >= 1:
                        W(v, "ve", c_te, p_te[pi - 1][3])
                    v.tensor_scalar(out=S0.e[:], in0=scr_x[:, pi % 2, :],
                                    scalar1=g_sc[:, wi:wi + 1], scalar2=None,
                                    op0=ALU.is_ge)
                    v.tensor_scalar(out=S0.f[:], in0=scr_x[:, pi % 2, :],
                                    scalar1=g_scn[:, wi:wi + 1],
                                    scalar2=None, op0=ALU.is_le)
                    c_ve.inc(v.tensor_tensor(out=S0.d[:], in0=S0.e[:],
                                             in1=S0.f[:], op=ALU.subtract))
                VEs(v6)
                pq_tok.append((c_ve, c_ve.n))

            if pi >= 1 and pending_stage and pending_stage[0][0] == pi - 1:
                flush_stage(pq_tok[pi])
            if kind == "tern" and nm in ("wq", "wk", "wv"):
                widx = {"wq": 0, "wk": 1, "wv": 2}[nm]
                dsts = [qkvT[:, 4 * g:4 * g + 4,
                             widx * 256 + ch * 128:
                             widx * 256 + (ch + 1) * 128] for g in range(4)]
                stage = False
            else:
                base_ix = stage_ix[0]
                dsts = [pstage[(base_ix + g) % 16][:, :, :] for g in range(4)]
                stage = True
                stage_ix[0] += 4

            te_tok = []
            ve_tok = []
            qc_, qv_ = pq_tok[pi]
            for grp in range(4):
                pt = PE if grp % 2 == 0 else PF

                def t5(t, grp=grp, pt=pt, pi=pi, qc_=qc_, qv_=qv_):
                    W(t, "te", qc_, qv_)
                    if grp >= 2:
                        W(t, "te", c_ve, ve_tok[grp - 2])
                    elif pi >= 1:
                        W(t, "te", c_ve, p_ve[pi - 1][grp + 2])
                    else:
                        W(t, "te", c_ve, x_ve[-1][grp + 2])
                    for j in range(4):
                        kc = grp * 4 + j
                        i = t.transpose(
                            pt.ap().rearrange("p (g c) -> p g c",
                                                     g=4)[:, j, :],
                            S0.d[:, kc * 128:(kc + 1) * 128], ident[:])
                    te_tok.append(c_te.inc(i))
                TEs(t5)

                def v7(v, grp=grp, pt=pt, pi=pi, stage=stage, dsts=dsts):
                    W(v, "ve", c_te, te_tok[grp])
                    if stage:
                        slot = (base_ix + grp) % 16
                        if slot in pstage_last:
                            W(v, "ve", c_sy, pstage_last[slot])
                    i = v.tensor_copy(
                        out=dsts[grp],
                        in_=pt.ap().rearrange("p (g c) -> p g c",
                                                     g=4))
                    ve_tok.append(c_ve.inc(i))
                VEs(v7)

            if stage:
                pending_stage.append((pi, nm, ch, cp, list(ve_tok), base_ix))
            p_te.append(te_tok)
            p_ve.append(ve_tok)

        def v_sp(v):
            c_ve.inc(v.tensor_copy(out=S0.st[:, 14:15], in_=S0.st[:, 0:1]))
            c_ve.inc(v.tensor_copy(out=S0.st[:, 15:16], in_=S0.st[:, 0:1]))
        VEs(v_sp)
        flush_stage((c_ve, c_ve.n))
        AGW_STAGED = c_sy.n

        if stop_phase == 0:
            finish([(c, c.n) for c in ALL_CTRS])
            ph0.close()
            qkvT_st.close()
            return nc

        P0_DONE = [(c_te, c_te.n), (c_ve, c_ve.n), (c_ac, c_ac.n),
                   (c_gp, c_gp.n), (c_sy, c_sy.n)]
        ph0.close()
        ph1 = ExitStack()
        s_q = ph1.enter_context(nc.sbuf_tensor("s_q", [128, HL, T], BF16))
        s_k = ph1.enter_context(nc.sbuf_tensor("s_k", [128, HL, T], BF16))
        s_v = ph1.enter_context(nc.sbuf_tensor("s_v", [128, NKC, 256], BF16))
        ph1b = ExitStack()
        xb = ph1b.enter_context(nc.sbuf_tensor("xb", [128, 16, 512], BF16))
        nsc = [ph1b.enter_context(nc.sbuf_tensor(f"nsc{i}", [128, 512], F32))
               for i in range(2)]
        mxrow = ph1b.enter_context(nc.sbuf_tensor("mxrow", [1, 16], F32))
        bstat = ph1b.enter_context(nc.sbuf_tensor("bstat", [1, 8], F32))

        BANKS4 = [PC, PD, PE, PF]
        bank_free = [None, None, None, None]
        xb_free = [None]

        for tb in range(R):
            for th in range(2):
                def s5(sy, tb=tb, th=th):
                    for (ctr, val) in P0_DONE:
                        W(sy, "sy", ctr, val)
                    W(sy, "sy", c_cc, CC_AG2)
                    if xb_free[0] is not None:
                        W(sy, "sy", c_te, xb_free[0])
                    i = sy.dma_start(
                        out=xb[:, :, :],
                        in_=ag2_out.ap()[tb * D:(tb + 1) * D, :].rearrange(
                            "(kc p) t -> p kc t",
                            p=128)[:, :, th * 512:(th + 1) * 512])
                    c_sy.inc(i, 16)
                SYs(s5)
                LD_XB = c_sy.n

                groups = [("q", 0), ("q", 1), ("k", 0), ("k", 1),
                          ("v", 0), ("v", 1), ("v", 2), ("v", 3)]
                for gidx, (gk, gsub) in enumerate(groups):
                    bank = BANKS4[gidx % 4]

                    def t6(t, gk=gk, gsub=gsub, gidx=gidx, bank=bank,
                           LD_XB=LD_XB):
                        for (ctr, val) in P0_DONE:
                            W(t, "te", ctr, val)
                        W(t, "te", c_sy, LD_XB)
                        bf = bank_free[gidx % 4]
                        if bf is not None:
                            W(t, "te", bf[0], bf[1])
                        if gk == "q":
                            for kc in range(KC):
                                i = t.matmul(
                                    bank[:, :],
                                    qkvT[:, kc, gsub * 128:(gsub + 1) * 128],
                                    xb[:, kc, :], start=(kc == 0),
                                    stop=(kc == KC - 1))
                        elif gk == "k":
                            for kc in range(KC):
                                i = t.matmul(
                                    bank[:, :],
                                    qkvT[:, kc, 256 + gsub * 128:
                                         256 + (gsub + 1) * 128],
                                    xb[:, kc, :], start=(kc == 0),
                                    stop=(kc == KC - 1))
                        else:
                            for kc in range(KC):
                                i = t.matmul(
                                    bank[:, 0:256],
                                    xb[:, kc, gsub * 128:(gsub + 1) * 128],
                                    qkvT[:, kc, 512:768], start=(kc == 0),
                                    stop=(kc == KC - 1))
                        c_te.inc(i)
                    TEs(t6)
                    MM_TOK = c_te.n

                    def a3(sc, gk=gk, gsub=gsub, bank=bank, tb=tb, th=th,
                           MM_TOK=MM_TOK):
                        W(sc, "ac", c_te, MM_TOK)
                        t0 = tb * 1024 + th * 512
                        if gk == "q":
                            i = sc.copy(out=s_q[:, gsub, t0:t0 + 512],
                                        in_=bank[:, :])
                        elif gk == "k":
                            i = sc.copy(out=s_k[:, gsub, t0:t0 + 512],
                                        in_=bank[:, :])
                        else:
                            i = sc.copy(out=s_v[:, (t0 // 128) + gsub, :],
                                        in_=bank[:, 0:256])
                        c_ac.inc(i)
                    ACs(a3)
                    bank_free[gidx % 4] = (c_ac, c_ac.n)
                xb_free[0] = c_te.n

        QKV_AC = c_ac.n

        combos = [("q", 0), ("k", 0), ("q", 1), ("k", 1)]
        for ci, (src_k, h) in enumerate(combos):
            src = s_q if src_k == "q" else s_k
            sq_te = []
            sq_ve = []
            for bx in range(16):
                buf = nsc[bx % 2]
                pt = PE if bx % 2 == 0 else PF

                def a4(sc, src=src, h=h, bx=bx, buf=buf):
                    W(sc, "ac", c_ac, QKV_AC)
                    if bx >= 2:
                        W(sc, "ac", c_te, sq_te[bx - 2])
                    i = sc.activation(out=buf[:, :],
                                      in_=src[:, h, bx * 512:(bx + 1) * 512],
                                      func=AF.Square)
                    c_ac.inc(i)
                ACs(a4)
                SQ_AC = c_ac.n

                def t7(t, buf=buf, pt=pt, bx=bx, SQ_AC=SQ_AC):
                    W(t, "te", c_ac, SQ_AC)
                    if bx >= 2:
                        W(t, "te", c_ve, sq_ve[bx - 2])
                    elif bank_free[2 + (bx % 2)] is not None:
                        bf = bank_free[2 + (bx % 2)]
                        W(t, "te", bf[0], bf[1])
                    i = t.matmul(pt[0:1, :], ones_c[:, :], buf[:, :],
                                 start=True, stop=True)
                    sq_te.append(c_te.inc(i))
                TEs(t7)

                def v9(v, pt=pt, bx=bx):
                    W(v, "ve", c_te, sq_te[bx])
                    i = v.reduce_max(out=mxrow[:, bx:bx + 1],
                                     in_=pt[0:1, :], axis=AX.X)
                    sq_ve.append(c_ve.inc(i))
                VEs(v9)

            def v10(v, ci=ci):
                W(v, "ve", c_ve, sq_ve[15])
                c_ve.inc(v.reduce_max(out=bstat[:, ci:ci + 1],
                                      in_=mxrow[:, :], axis=AX.X))
            VEs(v10)

        def v11(v):
            W(v, "ve", c_ve, c_ve.n)
            bs = bstat.ap()
            c_ve.inc(v.tensor_tensor(out=bstat[:, 4:6], in0=bs[0:1, 0:4:2],
                                     in1=bs[0:1, 1:4:2], op=ALU.add))
            W(v, "ve", c_ve, c_ve.n)
            c_ve.inc(v.tensor_scalar(out=bstat[:, 4:6], in0=bstat[:, 4:6],
                                     scalar1=-0.5, scalar2=None,
                                     op0=ALU.mult))
            W(v, "ve", c_ve, c_ve.n)
            c_ve.inc(v.tensor_scalar(out=bstat[:, 4:6], in0=bstat[:, 4:6],
                                     scalar1=g_sc[0:1, 6:7], scalar2=None,
                                     op0=ALU.mult))
        VEs(v11)
        VE_NEGB_ROW = c_ve.n

        def t8(t):
            W(t, "te", c_ve, VE_NEGB_ROW)
            c_te.inc(t.matmul(PE[:, 0:2], ones_r[:, :], bstat[0:1, 4:6],
                              start=True, stop=True))
        TEs(t8)
        TE_NEGB = c_te.n

        def v12(v):
            W(v, "ve", c_te, TE_NEGB)
            c_ve.inc(v.tensor_copy(out=g_negB[:, :], in_=PE[:, 0:2]))
        VEs(v12)
        P2_READY = [(c_ve, c_ve.n), (c_ac, c_ac.n)]

        if stop_phase == 1:
            finish([(c, c.n) for c in ALL_CTRS])
            ph1b.close()
            ph1.close()
            qkvT_st.close()
            return nc

        ph1b.close()
        ph2 = ExitStack()
        sp = ph2.enter_context(nc.sbuf_tensor("sp", [128, 2, 2, QBS], BF16))
        sr = ph2.enter_context(nc.sbuf_tensor("sr", [128, QBS], F32))
        sofm = ph2.enter_context(nc.sbuf_tensor("sofm", [128, QBS], F32))
        szv = ph2.enter_context(nc.sbuf_tensor("szv", [128, 4], F32))
        szr = ph2.enter_context(nc.sbuf_tensor("szr", [128, 4], F32))
        s_o = ph2.enter_context(nc.sbuf_tensor("s_o", [128, 2, 4, DH], BF16))

        SEXP = 12
        STRP = 8
        so_dma = {}
        S_SC0, S_EX0, S_AV0, S_Z0 = s_sc.n, s_ex.n, s_av.n, s_z.n
        S_TR0, S_RED0 = s_tr.n, s_red.n

        def v13(v):
            for (ctr, val) in P2_READY:
                W(v, "ve", ctr, val)
            c_ve.inc(v.memset(sr[:, :], 0.0))
        VEs(v13)
        SR_INIT = c_ve.n

        for it in range(NIT):
            h = it // NQB
            qb = it % NQB
            qt = s_q[:, h, qb * QBS:(qb + 1) * QBS]

            def t9(t, it=it, h=h, qt=qt):
                if it == 0:
                    for (ctr, val) in P2_READY:
                        W(t, "te", ctr, val)
                for pr in range(NPAIR):
                    g = it * NPAIR + pr
                    ps = PA if g % 2 == 0 else PB
                    if g >= 2:
                        W(t, "te", s_ex, S_EX0 + g - 1)
                    k0 = (2 * pr) * 128
                    t.matmul(ps[:, 0:QBS], s_k[:, h, k0:k0 + 128], qt,
                             start=True, stop=True)
                    i = t.matmul(ps[:, QBS:2 * QBS],
                                 s_k[:, h, k0 + 128:k0 + 256], qt,
                                 start=True, stop=True)
                    s_sc.inc(i)
                    if pr >= 1:
                        av = g - 1
                        avp = pr - 1
                        W(t, "te", s_ex, S_EX0 + av + 1)
                        if avp == 0 and it >= 1:
                            W(t, "te", s_red, S_RED0 + (it - 1) * SEXP + 5)
                        kc0 = 2 * avp
                        t.matmul(PC[:, :],
                                 s_v[:, kc0, h * 128:(h + 1) * 128],
                                 sp[:, av % 2, 0, :], start=(avp == 0),
                                 stop=False, skip_group_check=True)
                        i = t.matmul(PC[:, :],
                                     s_v[:, kc0 + 1, h * 128:(h + 1) * 128],
                                     sp[:, av % 2, 1, :], start=False,
                                     stop=False, skip_group_check=True)
                        s_av.inc(i)
                av = (it + 1) * NPAIR - 1
                W(t, "te", s_ex, S_EX0 + av + 1)
                kc0 = 2 * (NPAIR - 1)
                t.matmul(PC[:, :], s_v[:, kc0, h * 128:(h + 1) * 128],
                         sp[:, av % 2, 0, :], start=False, stop=False,
                         skip_group_check=True)
                i = t.matmul(PC[:, :], s_v[:, kc0 + 1, h * 128:(h + 1) * 128],
                             sp[:, av % 2, 1, :], start=False, stop=True,
                             skip_group_check=True)
                s_av.inc(i)
                for i8 in range(8):
                    pt = PE if i8 % 2 == 0 else PF
                    if i8 < 4:
                        W(t, "te", s_z, S_Z0 + (it + 1) * NPAIR)
                        if i8 < 2:
                            if it >= 1:
                                W(t, "te", s_red, S_RED0 + (it - 1) * SEXP +
                                  (11 if i8 == 0 else 12))
                        else:
                            W(t, "te", s_red, S_RED0 + it * SEXP + (i8 - 1))
                        src = sr[:, i8 * 128:(i8 + 1) * 128]
                    else:
                        if i8 < 6:
                            W(t, "te", s_red, S_RED0 + it * SEXP + 5)
                        else:
                            W(t, "te", s_red, S_RED0 + it * SEXP + (i8 + 3))
                        src = sofm[:, (i8 - 4) * 128:(i8 - 3) * 128]
                    i = t.transpose(pt[:, 0:128], src, ident[:])
                    s_tr.inc(i)
            TEs(t9)
            assert s_sc.n == S_SC0 + (it + 1) * NPAIR
            assert s_av.n == S_AV0 + (it + 1) * NPAIR
            assert s_tr.n == S_TR0 + (it + 1) * STRP

            def a5(sc, it=it, h=h):
                for pr in range(NPAIR):
                    g = it * NPAIR + pr
                    ps = PA if g % 2 == 0 else PB
                    W(sc, "ac", s_sc, S_SC0 + g + 1)
                    if g >= 2:
                        W(sc, "ac", s_av, S_AV0 + g - 1)
                        W(sc, "ac", s_z, S_Z0 + g - 1)
                    i = sc.activation(out=sp[:, g % 2, :, :], in_=ps[:, :],
                                      func=AF.Exp,
                                      bias=g_negB[:, h:h + 1],
                                      scale=g_sc[:, 6:7])
                    s_ex.inc(i)
            ACs(a5)
            assert s_ex.n == S_EX0 + (it + 1) * NPAIR

            def v14(v, it=it, pt=None):
                if it == 0:
                    W(v, "ve", c_ve, SR_INIT)
                for pr in range(NPAIR):
                    g = it * NPAIR + pr
                    W(v, "ve", s_ex, S_EX0 + g + 1)
                    v.tensor_tensor(out=sr[:, :], in0=sr[:, :],
                                    in1=sp[:, g % 2, 0, :], op=ALU.add)
                    i = v.tensor_tensor(out=sr[:, :], in0=sr[:, :],
                                        in1=sp[:, g % 2, 1, :], op=ALU.add)
                    s_z.inc(i)
                for qc in range(4):
                    ptq = PE if qc % 2 == 0 else PF
                    W(v, "ve", s_tr, S_TR0 + it * STRP + qc + 1)
                    i = v.reduce_sum(out=szv[:, qc:qc + 1],
                                     in_=ptq[:, 0:128], axis=AX.X)
                    s_red.inc(i)
                W(v, "ve", s_av, S_AV0 + (it + 1) * NPAIR)
                if it >= 1:
                    W(v, "ve", s_tr, S_TR0 + (it - 1) * STRP + 8)
                i = v.tensor_copy(out=sofm[:, :], in_=PC[:, :])
                s_red.inc(i)
                W(v, "ve", s_red, S_RED0 + it * SEXP + 4)
                i = v.reciprocal(out=szr[:, :], in_=szv[:, :])
                s_red.inc(i)
                W(v, "ve", s_red, S_RED0 + it * SEXP + 6)
                i = v.tensor_scalar(out=szr[:, :], in0=szr[:, :],
                                    scalar1=g_sc[:, 4:5],
                                    scalar2=None, op0=ALU.mult)
                s_red.inc(i)
                W(v, "ve", s_tr, S_TR0 + it * STRP + 4)
                i = v.memset(sr[:, :], 0.0)
                s_red.inc(i)
                for qc in range(4):
                    ptq = PE if qc % 2 == 0 else PF
                    W(v, "ve", s_tr, S_TR0 + it * STRP + 5 + qc)
                    W(v, "ve", s_red, S_RED0 + it * SEXP + 7)
                    if (it - 2) in so_dma:
                        W(v, "ve", c_sy, so_dma[it - 2])
                    i = v.tensor_scalar(out=s_o[:, it % 2, qc, :],
                                        in0=ptq[:, 0:128],
                                        scalar1=szr[:, qc:qc + 1],
                                        scalar2=None, op0=ALU.mult)
                    s_red.inc(i)
            VEs(v14)
            assert s_z.n == S_Z0 + (it + 1) * NPAIR
            assert s_red.n == S_RED0 + (it + 1) * SEXP

            def g5(g, it=it, h=h, qb=qb):
                W(g, "sy", s_red, S_RED0 + (it + 1) * SEXP)
                dst = ag3_in.ap()[qb * QBS:(qb + 1) * QBS,
                                  h * 128:(h + 1) * 128]
                i = g.dma_start(
                    out=dst.rearrange("(qc p) d -> p qc d", p=128),
                    in_=s_o[:, it % 2, :, :])
                so_dma[it] = c_sy.inc(i, 16)
            SYs(g5)

            if it == 2:
                def g4b(g):
                    W(g, "gp", c_gp, c_gp.n)
                    W(g, "gp", c_sy, AGW_STAGED)
                    c_cc.inc(g.collective_compute(
                        "AllGather", ALU.bypass,
                        replica_groups=[list(range(R))],
                        ins=[agw1_in.ap()], outs=[agw1_out.ap()]))
                    c_cc.inc(g.collective_compute(
                        "AllGather", ALU.bypass,
                        replica_groups=[list(range(R))],
                        ins=[agw2_in.ap()], outs=[agw2_out.ap()]))
                GPs(g4b)

        def g6(g):
            W(g, "gp", c_gp, c_gp.n)
            W(g, "gp", c_sy, so_dma[NIT - 1])
            c_cc.inc(g.collective_compute(
                "AllGather", ALU.bypass, replica_groups=[list(range(R))],
                ins=[ag3_in.ap()], outs=[ag3_out.ap()]))
        GPs(g6)
        CC_AG3 = c_cc.n

        if stop_phase == 2:
            d_sofm = nc.dram_tensor("dbg_sofm", [128, QBS], F32,
                                    kind="ExternalOutput")
            d_szv = nc.dram_tensor("dbg_szv", [128, 4], F32,
                                   kind="ExternalOutput")
            d_szr = nc.dram_tensor("dbg_szr", [128, 4], F32,
                                   kind="ExternalOutput")
            d_nb = nc.dram_tensor("dbg_negB", [128, 2], F32,
                                  kind="ExternalOutput")
            d_sr = nc.dram_tensor("dbg_sr", [128, QBS], F32,
                                  kind="ExternalOutput")

            def s_d2(sy):
                for (c2, v2) in [(c3, c3.n) for c3 in ALL_CTRS]:
                    W(sy, "sy", c2, v2)
                c_st.inc(sy.dma_start(out=d_sofm.ap(), in_=sofm[:, :]), 16)
                c_st.inc(sy.dma_start(out=d_szv.ap(), in_=szv[:, :]), 16)
                c_st.inc(sy.dma_start(out=d_szr.ap(), in_=szr[:, :]), 16)
                c_st.inc(sy.dma_start(out=d_nb.ap(), in_=g_negB[:, :]), 16)
                c_st.inc(sy.dma_start(out=d_sr.ap(), in_=sr[:, :]), 16)
            SYs(s_d2)
            finish([(c, c.n) for c in ALL_CTRS])
            ph2.close()
            ph1.close()
            qkvT_st.close()
            return nc

        P2_DONE = [(s_red, s_red.n), (s_tr, s_tr.n), (c_gp, c_gp.n),
                   (c_ac, c_ac.n), (c_te, c_te.n), (c_ve, c_ve.n),
                   (c_sy, c_sy.n)]
        ph2.close()
        ph1.close()
        qkvT_st.close()
        ph3 = ExitStack()
        S3 = mkscr(ph3, "3")
        O_in = ph3.enter_context(nc.sbuf_tensor("O_in", [128, NCH, D], BF16))
        Oqfm = ph3.enter_context(nc.sbuf_tensor("Oqfm", [128, 16, TC], BF16))
        wob = ph3.enter_context(nc.sbuf_tensor("wob", [128, 2, 16, 512], BF16))
        dlt = ph3.enter_context(nc.sbuf_tensor("dlt", [128, 2, D], F32))

        def g7(g):
            for (ctr, val) in P2_DONE:
                W(g, "gp", ctr, val)
            W(g, "gp", c_cc, CC_AG3)
            c_gp.inc(g.dma_start(out=pid_sb[:, :],
                                 in_=nc.partition_id_tensor.ap()), 16)
            W(g, "gp", c_gp, c_gp.n)
            with g.register("pidr") as pid_reg:
                g.reg_load(pid_reg, pid_sb[0:1, 0:1])
                with g.register("offr") as off_reg:
                    g.reg_alu(off_reg, pid_reg, NCH, ALU.mult)
                    off = g.snap(off_reg, min_val=0,
                                 max_val=T // 128 - NCH)
                    for r in range(R):
                        src = ag3_out.ap()[r * T:(r + 1) * T, :].rearrange(
                            "(tc p) j -> p tc j", p=128)
                        i = g.dma_start(
                            out=O_in[:, :, r * 256:(r + 1) * 256],
                            in_=src[:, bass.ds(off, NCH), :])
                        c_gp.inc(i, 16)
        GPs(g7)
        OIN_TOK = c_gp.n

        o_te = []
        o_ve = []
        for tc in range(NCH):
            def v18(v, tc=tc):
                for (ctr, val) in P2_DONE:
                    W(v, "ve", ctr, val)
                W(v, "ve", c_gp, OIN_TOK)
                if tc >= 2:
                    W(v, "ve", c_te, o_te[tc - 2][3])
            VEs(v18)
            quant_scale(S3, O_in[:, tc, :], 1e-5, with_rstd=False)
            quant_core(S3, O_in[:, tc, :], scr_x[:, tc % 2, :])
            tt, vv = transpose16(
                lambda tc=tc: scr_x[:, tc % 2, :],
                [Oqfm[:, 4 * g:4 * g + 4, tc * 128:(tc + 1) * 128]
                 for g in range(4)],
                extra_first=P2_DONE if tc == 0 else (),
                ready=(c_ve, c_ve.n),
                prev_ve=(o_ve[tc - 1] if tc >= 1 else None))
            o_te.append(tt)
            o_ve.append(vv)
        OQFM_VE = o_ve[-1][3]

        wo_mm = {}
        dl_ev = {}
        x2_st = {}
        for tcp in range(4):
            for jb in range(4):
                def s6(sy, tcp=tcp, jb=jb):
                    for (ctr, val) in P2_DONE:
                        W(sy, "sy", ctr, val)
                    W(sy, "sy", c_cc, CC_AGW)
                    key = tcp * 4 + jb - 2
                    if key in wo_mm:
                        W(sy, "sy", c_te, wo_mm[key])
                    for rr in range(2):
                        r = 2 * jb + rr
                        i = sy.dma_start(
                            out=wob[:, jb % 2, :, rr * 256:(rr + 1) * 256],
                            in_=agw1_out.ap()[r * D:(r + 1) * D,
                                              0:256].rearrange(
                                "(kc p) j -> p kc j", p=128))
                        c_sy.inc(i, 16)
                SYs(s6)
                WO_LD = c_sy.n

                for tcd in range(2):
                    bank = PC if tcd == 0 else PD
                    tcg = tcp * 2 + tcd

                    def t12(t, tcg=tcg, jb=jb, bank=bank, WO_LD=WO_LD,
                            tcp=tcp, tcd=tcd):
                        W(t, "te", c_sy, WO_LD)
                        W(t, "te", c_ve, OQFM_VE)
                        key = (tcp, jb - 1, tcd) if jb >= 1 else \
                              (tcp - 1, 3, tcd)
                        if key in dl_ev:
                            W(t, "te", c_ve, dl_ev[key])
                        for kc in range(KC):
                            i = t.matmul(
                                bank[:, :],
                                Oqfm[:, kc, tcg * 128:(tcg + 1) * 128],
                                wob[:, jb % 2, kc, :],
                                start=(kc == 0), stop=(kc == KC - 1))
                        c_te.inc(i)
                    TEs(t12)
                    wo_mm[tcp * 4 + jb] = c_te.n
                    MM_TOK = c_te.n

                    def v19(v, tcp=tcp, jb=jb, tcd=tcd, bank=bank,
                            MM_TOK=MM_TOK):
                        W(v, "ve", c_te, MM_TOK)
                        if (tcp - 1) in x2_st:
                            W(v, "ve", c_sy, x2_st[tcp - 1])
                        i = v.tensor_scalar(
                            out=dlt[:, tcd, jb * 512:(jb + 1) * 512],
                            in0=bank[:, :], scalar1=g_sc[:, 5:6],
                            scalar2=None, op0=ALU.mult)
                        dl_ev[(tcp, jb, tcd)] = c_ve.inc(i)
                    VEs(v19)

            for tcd in range(2):
                tcg = tcp * 2 + tcd

                def s7(sy, tcg=tcg, tcd=tcd, tcp=tcp):
                    W(sy, "sy", c_ve, dl_ev[(tcp, 3, tcd)])
                    if tcg >= 2:
                        pass
                    i = sy.dma_start(out=scr_x[:, tcd, :],
                                     in_=x_in[tcg * 128:(tcg + 1) * 128, :])
                    c_sy.inc(i, 16)
                SYs(s7)
                XLD = c_sy.n

                ln_stats(S3, dlt[:, tcd, :], LN_EPS,
                         out_dst=dlt[:, tcd, :], base_ap=scr_x[:, tcd, :],
                         extra_ve_waits=[(c_ve, dl_ev[(tcp, 3, tcd)]),
                                         (c_sy, XLD)])
                LN_TOK = c_ve.n

                def s8(sy, tcg=tcg, tcd=tcd, tcp=tcp, LN_TOK=LN_TOK):
                    W(sy, "sy", c_ve, LN_TOK)
                    i = sy.dma_start(
                        out=x2_dram.ap()[tcg * 128:(tcg + 1) * 128, :],
                        in_=dlt[:, tcd, :])
                    c_sy.inc(i, 16)
                    x2_st[tcp] = c_sy.n
                SYs(s8)

        if stop_phase == 3:
            d_oin = nc.dram_tensor("dbg_oin", [128, NCH, D], BF16,
                                   kind="ExternalOutput")
            d_oq = nc.dram_tensor("dbg_oq", [128, 16, TC], BF16,
                                  kind="ExternalOutput")
            d_dlt = nc.dram_tensor("dbg_dlt", [128, 2, D], F32,
                                   kind="ExternalOutput")

            def s_d3(sy):
                for (c, v) in [(c2, c2.n) for c2 in ALL_CTRS]:
                    W(sy, "sy", c, v)
                c_st.inc(sy.dma_start(out=d_oin.ap(), in_=O_in[:, :, :]), 16)
                c_st.inc(sy.dma_start(out=d_oq.ap(), in_=Oqfm[:, :, :]), 16)
                c_st.inc(sy.dma_start(out=d_dlt.ap(), in_=dlt[:, :, :]), 16)
            SYs(s_d3)
            finish([(c, c.n) for c in ALL_CTRS])
            ph3.close()
            return nc

        P3_DONE = [(c_te, c_te.n), (c_ve, c_ve.n), (c_ac, c_ac.n),
                   (c_sy, c_sy.n), (c_gp, c_gp.n)]
        ph3.close()
        ph4 = ExitStack()
        S4 = mkscr(ph4, "4")
        xq2 = ph4.enter_context(nc.sbuf_tensor("xq2", [128, 16, TC], BF16))
        pbuf = ph4.enter_context(nc.sbuf_tensor("pbuf", [128, NHC, 512], BF16))
        wgu = ph4.enter_context(
            nc.sbuf_tensor("wgu", [128, 2, 2, 16, 128], BF16))
        wdb = ph4.enter_context(nc.sbuf_tensor("wdb", [128, 2, 8, 512], BF16))
        sgl = ph4.enter_context(nc.sbuf_tensor("sgl", [128, 2, 512], BF16))
        dl4 = ph4.enter_context(nc.sbuf_tensor("dl4", [128, 2, D], F32))

        f_te = []
        f_ve = []
        for tc in range(NCH):
            def s9(sy, tc=tc):
                for (ctr, val) in P3_DONE:
                    W(sy, "sy", ctr, val)
                if tc >= 1:
                    W(sy, "sy", c_te, f_te[tc - 1][3])
                i = sy.dma_start(
                    out=scr_x[:, 0, :],
                    in_=x2_dram.ap()[tc * 128:(tc + 1) * 128, :])
                c_sy.inc(i, 16)
            SYs(s9)
            X2LD = c_sy.n

            def v21(v, tc=tc, X2LD=X2LD):
                for (ctr, val) in P3_DONE:
                    W(v, "ve", ctr, val)
                W(v, "ve", c_sy, X2LD)
            VEs(v21)
            ln_stats(S4, scr_x[:, 0, :], LN_EPS, out_dst=scr_x[:, 1, :])
            ln_stats(S4, scr_x[:, 1, :], RMS_EPS)
            quant_scale(S4, S4.a[:], None, with_rstd=True)
            quant_core(S4, S4.a[:], scr_x[:, 0, :])
            tt, vv = transpose16(
                lambda: scr_x[:, 0, :],
                [xq2[:, 4 * g:4 * g + 4, tc * 128:(tc + 1) * 128]
                 for g in range(4)],
                extra_first=P3_DONE if tc == 0 else (),
                ready=(c_ve, c_ve.n),
                prev_ve=(f_ve[tc - 1] if tc >= 1 else None))
            f_te.append(tt)
            f_ve.append(vv)
        XQ2_VE = c_ve.n

        y_chunk = 0
        down_done = [None]
        yst_tok = {}
        for tb in range(2):
            gu_ld = {}
            p_mul = {}
            silu_tok = {}
            for hc in range(NHC):
                gbank = PC if hc % 2 == 0 else PE
                ubank = PD if hc % 2 == 0 else PF

                def s10(sy, tb=tb, hc=hc):
                    for (ctr, val) in P3_DONE:
                        W(sy, "sy", ctr, val)
                    W(sy, "sy", c_cc, CC_AGW)
                    if (tb, hc - 2) in p_mul:
                        W(sy, "sy", c_ve, p_mul[(tb, hc - 2)])
                    r = hc // 8
                    col = (hc % 8) * 128
                    for wsel, base in ((0, 256), (1, 1280)):
                        i = sy.dma_start(
                            out=wgu[:, hc % 2, wsel, :, :],
                            in_=agw1_out.ap()[r * D:(r + 1) * D,
                                              base + col:base + col + 128
                                              ].rearrange(
                                "(kc p) h -> p kc h", p=128))
                        c_sy.inc(i, 16)
                    gu_ld[hc] = c_sy.n
                SYs(s10)

                def t13(t, tb=tb, hc=hc, gbank=gbank, ubank=ubank):
                    for (ctr, val) in P3_DONE:
                        W(t, "te", ctr, val)
                    W(t, "te", c_sy, gu_ld[hc])
                    W(t, "te", c_ve, XQ2_VE)
                    if tb == 1 and down_done[0] is not None:
                        W(t, "te", c_te, down_done[0])
                    if (tb, hc - 2) in silu_tok:
                        W(t, "te", c_ac, silu_tok[(tb, hc - 2)])
                        W(t, "te", c_ve, p_mul[(tb, hc - 2)])
                    for kc in range(KC):
                        t.matmul(gbank[:, :], wgu[:, hc % 2, 0, kc, :],
                                 xq2[:, kc, tb * 512:(tb + 1) * 512],
                                 start=(kc == 0), stop=(kc == KC - 1))
                    for kc in range(KC):
                        i = t.matmul(ubank[:, :], wgu[:, hc % 2, 1, kc, :],
                                     xq2[:, kc, tb * 512:(tb + 1) * 512],
                                     start=(kc == 0), stop=(kc == KC - 1))
                    c_te.inc(i)
                TEs(t13)
                GU_MM = c_te.n

                def a6(sc, hc=hc, gbank=gbank, GU_MM=GU_MM):
                    W(sc, "ac", c_te, GU_MM)
                    i = sc.activation(out=sgl[:, hc % 2, :],
                                      in_=gbank[:, :], func=AF.Silu)
                    c_ac.inc(i)
                ACs(a6)
                silu_tok[(tb, hc)] = c_ac.n

                def v22(v, tb=tb, hc=hc, ubank=ubank):
                    W(v, "ve", c_ac, silu_tok[(tb, hc)])
                    i = v.tensor_tensor(out=pbuf[:, hc, :],
                                        in0=sgl[:, hc % 2, :],
                                        in1=ubank[:, :], op=ALU.mult)
                    p_mul[(tb, hc)] = c_ve.inc(i)
                VEs(v22)

            P_DONE = p_mul[(tb, NHC - 1)]

            wd_ld = {}
            acc_ev = {}
            for tcp in range(2):
                for db in range(4):
                    for hg in range(8):
                        def s11(sy, tb=tb, tcp=tcp, db=db, hg=hg):
                            for (ctr, val) in P3_DONE:
                                W(sy, "sy", ctr, val)
                            seq = (tcp * 4 + db) * 8 + hg
                            if seq >= 2 and (seq - 2) in wd_ld:
                                W(sy, "sy", c_te, wd_ld[seq - 2][1])
                            for rr in range(2):
                                r = 2 * db + rr
                                i = sy.dma_start(
                                    out=wdb[:, hg % 2, :,
                                            rr * 256:(rr + 1) * 256],
                                    in_=agw2_out.ap()[
                                        r * HFF + hg * 1024:
                                        r * HFF + (hg + 1) * 1024,
                                        :].rearrange("(hc p) d -> p hc d",
                                                     p=128))
                                c_sy.inc(i, 16)
                        SYs(s11)
                        WD_LD = c_sy.n

                        def t14(t, tb=tb, tcp=tcp, db=db, hg=hg,
                                WD_LD=WD_LD, P_DONE=P_DONE):
                            W(t, "te", c_sy, WD_LD)
                            W(t, "te", c_ve, P_DONE)
                            if hg == 0:
                                for tcd in range(2):
                                    key = (tcp, db - 1, tcd) if db >= 1 \
                                        else (tcp - 1, 3, tcd)
                                    if key in acc_ev:
                                        W(t, "te", c_ve, acc_ev[key])
                            for h8 in range(8):
                                hc = hg * 8 + h8
                                for tcd in range(2):
                                    tcl = tcp * 2 + tcd
                                    bank = PC if tcd == 0 else PD
                                    i = t.matmul(
                                        bank[:, :],
                                        pbuf[:, hc,
                                             tcl * 128:(tcl + 1) * 128],
                                        wdb[:, hg % 2, h8, :],
                                        start=(hc == 0), stop=(hc == NHC - 1),
                                        skip_group_check=True)
                            c_te.inc(i)
                        TEs(t14)
                        wd_ld[(tcp * 4 + db) * 8 + hg] = (WD_LD, c_te.n)
                    DP_MM = c_te.n

                    for tcd in range(2):
                        bank = PC if tcd == 0 else PD

                        def v23(v, tb=tb, tcp=tcp, db=db, tcd=tcd,
                                bank=bank, DP_MM=DP_MM):
                            W(v, "ve", c_te, DP_MM)
                            prev = (tb, tcp - 1, tcd) if tcp >= 1 else \
                                   (tb - 1, 1, tcd)
                            if prev in yst_tok:
                                W(v, "ve", c_st, yst_tok[prev])
                            i = v.tensor_copy(
                                out=dl4[:, tcd, db * 512:(db + 1) * 512],
                                in_=bank[:, :])
                            acc_ev[(tcp, db, tcd)] = c_ve.inc(i)
                        VEs(v23)

                for tcd in range(2):
                    tcg = tb * 4 + tcp * 2 + tcd

                    def s12(sy, tcg=tcg, tcd=tcd, tcp=tcp):
                        W(sy, "sy", c_ve, acc_ev[(tcp, 3, tcd)])
                        i = sy.dma_start(
                            out=scr_x[:, tcd, :],
                            in_=x2_dram.ap()[tcg * 128:(tcg + 1) * 128, :])
                        c_sy.inc(i, 16)
                    SYs(s12)
                    XLD2 = c_sy.n

                    ln_stats(S4, dl4[:, tcd, :], LN_EPS,
                             out_dst=dl4[:, tcd, :],
                             base_ap=scr_x[:, tcd, :],
                             extra_ve_waits=[(c_ve, acc_ev[(tcp, 3, tcd)]),
                                             (c_sy, XLD2)])
                    LN_TOK4 = c_ve.n

                    def s13(sy, tb=tb, tcg=tcg, tcp=tcp, tcd=tcd,
                            LN_TOK4=LN_TOK4):
                        W(sy, "sy", c_ve, LN_TOK4)
                        i = sy.dma_start(
                            out=y_out[tcg * 128:(tcg + 1) * 128, :],
                            in_=dl4[:, tcd, :])
                        yst_tok[(tb, tcp, tcd)] = c_st.inc(i, 16)
                    SYs(s13)
            down_done[0] = c_te.n

        def s_end(sy):
            sy.wait_ge(c_st.h, c_st.n)
            for name, outh in dbg_outs.items():
                i = sy.dma_start(out=outh.ap(), in_=internals[name].ap())
                c_st.inc(i, 16)
            sy.wait_ge(c_st.h, c_st.n)
        SYs(s_end)
        ph4.close()

    return nc


def kernel(L0_attn_x, L0_attn_w_q, L0_attn_w_k, L0_attn_w_v, L0_attn_w_o,
           L0_ffn_W_g, L0_ffn_W_u, L0_ffn_W_d):
    x = np.ascontiguousarray(np.asarray(L0_attn_x, dtype=np.float32)
                             .reshape(T, D))
    wq = np.ascontiguousarray(np.asarray(L0_attn_w_q, dtype=np.float32))
    wk = np.ascontiguousarray(np.asarray(L0_attn_w_k, dtype=np.float32))
    wv = np.ascontiguousarray(np.asarray(L0_attn_w_v, dtype=np.float32))
    wo = np.ascontiguousarray(np.asarray(L0_attn_w_o, dtype=np.float32))
    wg = np.ascontiguousarray(np.asarray(L0_ffn_W_g, dtype=np.float32))
    wu = np.ascontiguousarray(np.asarray(L0_ffn_W_u, dtype=np.float32))
    wd = np.ascontiguousarray(np.asarray(L0_ffn_W_d, dtype=np.float32))

    nc = build()
    in_maps = []
    for c in range(R):
        in_maps.append({
            "x": x[c * TC:(c + 1) * TC],
            "wq": wq[c * 256:(c + 1) * 256],
            "wk": wk[c * 256:(c + 1) * 256],
            "wv": wv[c * 256:(c + 1) * 256],
            "wo": wo[c * 256:(c + 1) * 256],
            "wg": wg[c * (HFF // R):(c + 1) * (HFF // R)],
            "wu": wu[c * (HFF // R):(c + 1) * (HFF // R)],
            "wd": wd[c * (D // R):(c + 1) * (D // R)],
        })
    res = run_bass_kernel_spmd(nc, in_maps, core_ids=list(range(R)))
    y = np.concatenate([res.results[c]["y"] for c in range(R)], axis=0)
    return y.reshape(1, T, D)


# version 05 (final)
# speedup vs baseline: 1.1658x; 1.0032x over previous
import numpy as np
import concourse.bass as bass
import concourse.mybir as mybir
from concourse.bass_utils import run_bass_kernel_spmd
from contextlib import ExitStack

F32 = mybir.dt.float32
BF16 = mybir.dt.bfloat16
I32 = mybir.dt.int32
U32 = mybir.dt.uint32
AF = mybir.ActivationFunctionType
ALU = mybir.AluOpType
AX = mybir.AxisListType

R = 8
T = 8192
TC = T // R
D = 2048
DH = 128
HL = 2
HFF = 8192
NKC = T // 128
NPAIR = NKC // 2
QBS = 512
NQB = T // QBS
NIT = HL * NQB
SCALE = float(DH) ** -0.5
NCH = TC // 128
KC = D // 128
NHC = HFF // 128

LN_EPS = 1e-5
RMS_EPS = 1e-8


class Ctr:
    def __init__(self, nc, ctx, name):
        self.h = ctx.enter_context(nc.semaphore(name))
        self.n = 0
        self.name = name

    def inc(self, instr, k=1):
        instr.then_inc(self.h, k)
        self.n += k
        return self.n


def build(stop_phase=None, dbg=()):
    nc = bass.Bass("TRN2", target_bir_lowering=False)

    x_in = nc.dram_tensor("x", [TC, D], F32, kind="ExternalInput").ap()
    wq_in = nc.dram_tensor("wq", [2 * DH, D], F32, kind="ExternalInput").ap()
    wk_in = nc.dram_tensor("wk", [2 * DH, D], F32, kind="ExternalInput").ap()
    wv_in = nc.dram_tensor("wv", [2 * DH, D], F32, kind="ExternalInput").ap()
    wo_in = nc.dram_tensor("wo", [2 * DH, D], F32, kind="ExternalInput").ap()
    wg_in = nc.dram_tensor("wg", [HFF // R, D], F32, kind="ExternalInput").ap()
    wu_in = nc.dram_tensor("wu", [HFF // R, D], F32, kind="ExternalInput").ap()
    wd_in = nc.dram_tensor("wd", [D // R, HFF], F32, kind="ExternalInput").ap()
    y_out = nc.dram_tensor("y", [TC, D], F32, kind="ExternalOutput").ap()

    ag0_in = nc.dram_tensor("ag0_in", [1, 4], F32)
    ag0_out = nc.dram_tensor("ag0_out", [R, 4], F32)
    agw1_in = nc.dram_tensor("agw1_in", [D, 2304], BF16)
    agw1_out = nc.dram_tensor("agw1_out", [R * D, 2304], BF16)
    agw2_in = nc.dram_tensor("agw2_in", [HFF, 256], BF16)
    agw2_out = nc.dram_tensor("agw2_out", [R * HFF, 256], BF16)
    ag2_in = nc.dram_tensor("ag2_in", [D, TC], BF16)
    ag2_out = nc.dram_tensor("ag2_out", [R * D, TC], BF16)
    ag3_in = nc.dram_tensor("ag3_in", [T, 2 * DH], BF16)
    ag3_out = nc.dram_tensor("ag3_out", [R * T, 2 * DH], BF16)
    x2_dram = nc.dram_tensor("x2_dram", [TC, D], F32)
    internals = {"ag0_out": ag0_out, "agw1_out": agw1_out,
                 "agw2_out": agw2_out, "ag2_out": ag2_out,
                 "ag3_out": ag3_out, "x2_dram": x2_dram,
                 "ag2_in": ag2_in, "ag3_in": ag3_in, "agw1_in": agw1_in}
    dbg_outs = {}
    for name in dbg:
        h = internals[name]
        dbg_outs[name] = nc.dram_tensor(
            "dbg_" + name, list(h.shape), h.dtype, kind="ExternalOutput")

    hwm = {}
    top = ExitStack()
    with top:
        PA = top.enter_context(nc.psum_tensor("PA", [128, 1024], F32))
        PB = top.enter_context(nc.psum_tensor("PB", [128, 1024], F32))
        PC = top.enter_context(nc.psum_tensor("PC", [128, 512], F32))
        PD = top.enter_context(nc.psum_tensor("PD", [128, 512], F32))
        PE = top.enter_context(nc.psum_tensor("PE", [128, 512], F32))
        PF = top.enter_context(nc.psum_tensor("PF", [128, 512], F32))

        c_sy = Ctr(nc, top, "c_sy")
        c_gp = Ctr(nc, top, "c_gp")
        c_cc = Ctr(nc, top, "c_cc")
        c_te = Ctr(nc, top, "c_te")
        c_ve = Ctr(nc, top, "c_ve")
        c_ac = Ctr(nc, top, "c_ac")
        s_sc = Ctr(nc, top, "s_sc")
        s_ex = Ctr(nc, top, "s_ex")
        s_av = Ctr(nc, top, "s_av")
        s_z = Ctr(nc, top, "s_z")
        s_tr = Ctr(nc, top, "s_tr")
        s_red = Ctr(nc, top, "s_red")
        c_st = Ctr(nc, top, "c_st")
        ALL_CTRS = [c_sy, c_gp, c_cc, c_te, c_ve, c_ac, s_sc, s_ex, s_av,
                    s_z, s_tr, s_red]

        ones_r = top.enter_context(nc.sbuf_tensor("ones_r", [1, 128], F32))
        ones_c = top.enter_context(nc.sbuf_tensor("ones_c", [128, 1], F32))
        ident = top.enter_context(nc.sbuf_tensor("ident", [128, 128], F32))
        g_sc = top.enter_context(nc.sbuf_tensor("g_sc", [128, 8], F32))
        g_scn = top.enter_context(nc.sbuf_tensor("g_scn", [128, 4], F32))
        g_negB = top.enter_context(nc.sbuf_tensor("g_negB", [128, 2], F32))
        scr_x = top.enter_context(nc.sbuf_tensor("scr_x", [128, 2, D], F32))
        pid_sb = top.enter_context(nc.sbuf_tensor("pid_sb", [1, 1], U32))
        blk = top.enter_context(nc.Block())

        def W(eng, ename, ctr, val):
            if val is None or val <= 0:
                return
            key = (ename, ctr.name)
            if hwm.get(key, 0) >= val:
                return
            eng.wait_ge(ctr.h, val)
            hwm[key] = val

        def mkscr(stack, tag):
            class S:
                pass
            s = S()
            s.a = stack.enter_context(nc.sbuf_tensor(f"sa{tag}", [128, D], F32))
            s.b = stack.enter_context(nc.sbuf_tensor(f"sb{tag}", [128, D], F32))
            s.c = stack.enter_context(nc.sbuf_tensor(f"sc{tag}", [128, D], I32))
            s.d = stack.enter_context(nc.sbuf_tensor(f"sd{tag}", [128, D], F32))
            s.e = stack.enter_context(nc.sbuf_tensor(f"se{tag}", [128, D], BF16))
            s.f = stack.enter_context(nc.sbuf_tensor(f"sf{tag}", [128, D], BF16))
            s.st = stack.enter_context(nc.sbuf_tensor(f"st{tag}", [128, 16], F32))
            return s

        TEs, VEs, ACs, SYs, GPs = (blk.tensor, blk.vector, blk.scalar,
                                   blk.sync, blk.gpsimd)

        def finish(gates):
            def s_fin(sy):
                for (ctr, val) in gates:
                    W(sy, "sy", ctr, val)
                for name, outh in dbg_outs.items():
                    i = sy.dma_start(out=outh.ap(), in_=internals[name].ap())
                    c_st.inc(i, 16)
                if c_st.n:
                    sy.wait_ge(c_st.h, c_st.n)
            SYs(s_fin)

        def ln_stats(S, src_ap, eps, out_dst=None, base_ap=None,
                     extra_ve_waits=()):
            def v1(v):
                for (ctr, val) in extra_ve_waits:
                    W(v, "ve", ctr, val)
                c_ve.inc(v.reduce_sum(out=S.st[:, 0:1], in_=src_ap,
                                      axis=AX.X))
                W(v, "ve", c_ve, c_ve.n)
                c_ve.inc(v.tensor_scalar(out=S.st[:, 1:2], in0=S.st[:, 0:1],
                                         scalar1=1.0 / D, scalar2=None,
                                         op0=ALU.mult))
                W(v, "ve", c_ve, c_ve.n)
                v.tensor_scalar(out=S.a[:], in0=src_ap,
                                scalar1=S.st[:, 1:2], scalar2=None,
                                op0=ALU.subtract)
                v.tensor_tensor(out=S.b[:], in0=S.a[:], in1=S.a[:],
                                op=ALU.mult)
                c_ve.inc(v.reduce_sum(out=S.st[:, 2:3], in_=S.b[:],
                                      axis=AX.X))
                W(v, "ve", c_ve, c_ve.n)
                c_ve.inc(v.tensor_scalar(out=S.st[:, 3:4], in0=S.st[:, 2:3],
                                         scalar1=1.0 / D, scalar2=eps,
                                         op0=ALU.mult, op1=ALU.add))
            VEs(v1)

            def a1(sc):
                W(sc, "ac", c_ve, c_ve.n)
                c_ac.inc(sc.activation(out=S.st[:, 4:5], in_=S.st[:, 3:4],
                                       func=AF.Sqrt))
            ACs(a1)

            def v2(v):
                W(v, "ve", c_ac, c_ac.n)
                c_ve.inc(v.reciprocal(out=S.st[:, 5:6], in_=S.st[:, 4:5]))
                W(v, "ve", c_ve, c_ve.n)
                if out_dst is not None:
                    if base_ap is None:
                        i = v.tensor_scalar(out=out_dst, in0=S.a[:],
                                            scalar1=S.st[:, 5:6],
                                            scalar2=None, op0=ALU.mult)
                    else:
                        i = v.scalar_tensor_tensor(out=out_dst, in0=S.a[:],
                                                   scalar=S.st[:, 5:6],
                                                   in1=base_ap, op0=ALU.mult,
                                                   op1=ALU.add)
                    c_ve.inc(i)
            VEs(v2)

        def quant_scale(S, src_abs_ap, clip_eps, with_rstd):
            def v1(v):
                c_ve.inc(v.reduce_max(out=S.st[:, 6:7], in_=src_abs_ap,
                                      axis=AX.X, apply_absolute_value=True))
                W(v, "ve", c_ve, c_ve.n)
                if with_rstd:
                    c_ve.inc(v.tensor_tensor(out=S.st[:, 7:8],
                                             in0=S.st[:, 6:7],
                                             in1=S.st[:, 5:6], op=ALU.mult))
                else:
                    c_ve.inc(v.tensor_copy(out=S.st[:, 7:8],
                                           in_=S.st[:, 6:7]))
                W(v, "ve", c_ve, c_ve.n)
                if clip_eps is not None:
                    c_ve.inc(v.tensor_scalar(out=S.st[:, 7:8],
                                             in0=S.st[:, 7:8],
                                             scalar1=clip_eps, scalar2=None,
                                             op0=ALU.max))
                    W(v, "ve", c_ve, c_ve.n)
                v.tensor_scalar(out=S.st[:, 8:9], in0=S.st[:, 7:8],
                                scalar1=1.0 / 127.0, scalar2=None,
                                op0=ALU.mult)
                c_ve.inc(v.reciprocal(out=S.st[:, 9:10], in_=S.st[:, 7:8]))
                W(v, "ve", c_ve, c_ve.n)
                c_ve.inc(v.tensor_scalar(out=S.st[:, 10:11],
                                         in0=S.st[:, 9:10], scalar1=127.0,
                                         scalar2=None, op0=ALU.mult))
                W(v, "ve", c_ve, c_ve.n)
                if with_rstd:
                    c_ve.inc(v.tensor_tensor(out=S.st[:, 11:12],
                                             in0=S.st[:, 5:6],
                                             in1=S.st[:, 10:11],
                                             op=ALU.mult))
                else:
                    c_ve.inc(v.tensor_copy(out=S.st[:, 11:12],
                                           in_=S.st[:, 10:11]))
                W(v, "ve", c_ve, c_ve.n)
            VEs(v1)

        def quant_core(S, src_ap, dst_f32):
            def a1(sc):
                W(sc, "ac", c_ve, c_ve.n)
                sc.activation(out=S.b[:], in_=src_ap, func=AF.Abs)
                c_ac.inc(sc.activation(out=S.f[:], in_=src_ap, func=AF.Sign))
            ACs(a1)

            def v1(v):
                W(v, "ve", c_ac, c_ac.n)
                v.tensor_scalar(out=S.d[:], in0=S.b[:],
                                scalar1=S.st[:, 11:12], scalar2=None,
                                op0=ALU.mult)
                v.tensor_scalar(out=S.c[:], in0=S.b[:],
                                scalar1=S.st[:, 11:12], scalar2=None,
                                op0=ALU.mult)
                v.scalar_tensor_tensor(out=S.a[:], in0=S.c[:], scalar=-1.0,
                                       in1=S.d[:], op0=ALU.mult, op1=ALU.add)
                v.tensor_scalar(out=S.e[:], in0=S.a[:], scalar1=0.5,
                                scalar2=None, op0=ALU.is_ge)
                v.tensor_tensor(out=S.d[:], in0=S.c[:], in1=S.e[:],
                                op=ALU.add)
                v.tensor_scalar(out=S.b[:], in0=S.f[:],
                                scalar1=S.st[:, 8:9], scalar2=None,
                                op0=ALU.mult)
                c_ve.inc(v.tensor_tensor(out=dst_f32, in0=S.d[:],
                                         in1=S.b[:], op=ALU.mult))
            VEs(v1)

        def transpose16(src_getter, dst_slices, extra_first=(),
                        ready=None, prev_ve=None):
            te_tok = []
            ve_tok = []
            for grp in range(4):
                pt = PE if grp % 2 == 0 else PF

                def t1(t, grp=grp, pt=pt):
                    for (ctr, val) in extra_first:
                        W(t, "te", ctr, val)
                    if ready is not None:
                        W(t, "te", ready[0], ready[1])
                    if grp >= 2:
                        W(t, "te", c_ve, ve_tok[grp - 2])
                    elif prev_ve is not None:
                        W(t, "te", c_ve, prev_ve[grp + 2])
                    src = src_getter()
                    for j in range(4):
                        kc = grp * 4 + j
                        i = t.transpose(
                            pt.ap().rearrange("p (g c) -> p g c",
                                                     g=4)[:, j, :],
                            src[:, kc * 128:(kc + 1) * 128], ident[:])
                    te_tok.append(c_te.inc(i))
                TEs(t1)

                def v1(v, grp=grp, pt=pt):
                    W(v, "ve", c_te, te_tok[grp])
                    i = v.tensor_copy(
                        out=dst_slices[grp],
                        in_=pt.ap().rearrange("p (g c) -> p g c",
                                                     g=4))
                    ve_tok.append(c_ve.inc(i))
                VEs(v1)
            return te_tok, ve_tok

        qkvT_st = ExitStack()
        qkvT = qkvT_st.enter_context(
            nc.sbuf_tensor("qkvT", [128, 16, 768], BF16))
        ph0 = ExitStack()
        S0 = mkscr(ph0, "0")
        pstage = [ph0.enter_context(
            nc.sbuf_tensor(f"pstage{i}", [128, 4, 128], BF16))
            for i in range(16)]
        pcol = ph0.enter_context(nc.sbuf_tensor("pcol", [128, 8], F32))
        prow = ph0.enter_context(nc.sbuf_tensor("prow", [1, 8], F32))
        pag0 = ph0.enter_context(nc.sbuf_tensor("pag0", [8, 4], F32))
        pcv = ph0.enter_context(nc.sbuf_tensor("pcv", [1, 8], F32))
        xqfm = ph0.enter_context(nc.sbuf_tensor("xqfm", [128, 16, TC], BF16))

        def g0(g):
            g.memset(ones_r[:], 1.0)
            g.memset(ones_c[:], 1.0)
            g.memset(ident[:], 0.0)
            i = g.affine_select(out=ident[:], in_=ident[:],
                                compare_op=ALU.not_equal, fill=1.0, base=0,
                                pattern=[[-1, 128]], channel_multiplier=1)
            c_gp.inc(i, 1)
        GPs(g0)
        GP_CONSTS = c_gp.n

        abs_srcs = []
        for ap in (wq_in, wk_in, wv_in, wo_in):
            for ch in range(2):
                abs_srcs.append(ap[ch * 128:(ch + 1) * 128, :])
        abs_ld = []
        abs_red = []
        for i8, src in enumerate(abs_srcs):
            def s1(sy, i8=i8, src=src):
                if i8 >= 2:
                    W(sy, "sy", c_ve, abs_red[i8 - 2])
                c_sy.inc(sy.dma_start(out=scr_x[:, i8 % 2, :], in_=src), 16)
                abs_ld.append(c_sy.n)
            SYs(s1)

            def v1(v, i8=i8):
                W(v, "ve", c_sy, abs_ld[i8])
                abs_red.append(c_ve.inc(
                    v.reduce_sum(out=pcol[:, i8:i8 + 1],
                                 in_=scr_x[:, i8 % 2, :], axis=AX.X,
                                 apply_absolute_value=True)))
            VEs(v1)

        def t2(t):
            W(t, "te", c_ve, abs_red[-1])
            W(t, "te", c_gp, GP_CONSTS)
            c_te.inc(t.matmul(PE[0:1, 0:8], ones_c[:, :], pcol[:, :],
                              start=True, stop=True))
        TEs(t2)
        TE_ABS = c_te.n

        def v3(v):
            W(v, "ve", c_te, TE_ABS)
            c_ve.inc(v.tensor_copy(out=prow[:, :], in_=PE[0:1, 0:8]))
            W(v, "ve", c_ve, c_ve.n)
            pr = prow.ap()
            c_ve.inc(v.tensor_tensor(out=prow[:, 0:4], in0=pr[0:1, 0:8:2],
                                     in1=pr[0:1, 1:8:2], op=ALU.add))
        VEs(v3)
        VE_PROW = c_ve.n

        def g1(g):
            W(g, "gp", c_ve, VE_PROW)
            c_gp.inc(g.dma_start(out=ag0_in.ap(), in_=prow[:, 0:4]), 16)
            W(g, "gp", c_gp, c_gp.n)
            c_cc.inc(g.collective_compute(
                "AllGather", ALU.bypass, replica_groups=[list(range(R))],
                ins=[ag0_in.ap()], outs=[ag0_out.ap()]))
        GPs(g1)
        CC_AG0 = c_cc.n

        def s2(sy):
            W(sy, "sy", c_cc, CC_AG0)
            c_sy.inc(sy.dma_start(out=pag0[:, :], in_=ag0_out.ap()), 16)
        SYs(s2)
        SY_PAG0 = c_sy.n

        def t3(t):
            W(t, "te", c_sy, SY_PAG0)
            c_te.inc(t.matmul(PF[0:1, 0:4], ones_c[0:8, :], pag0[:, :],
                              start=True, stop=True))
        TEs(t3)
        TE_AG0 = c_te.n

        def v4(v):
            W(v, "ve", c_te, TE_AG0)
            c_ve.inc(v.tensor_scalar(out=pcv[:, 0:4], in0=PF[0:1, 0:4],
                                     scalar1=1.0 / (D * D), scalar2=1e-5,
                                     op0=ALU.mult, op1=ALU.max))
            W(v, "ve", c_ve, c_ve.n)
            c_ve.inc(v.tensor_scalar(out=pcv[:, 4:8], in0=pcv[:, 0:4],
                                     scalar1=0.5, scalar2=None,
                                     op0=ALU.mult))
            W(v, "ve", c_ve, c_ve.n)
            c_ve.inc(v.tensor_tensor(out=pcv[:, 0:1], in0=pcv[:, 0:1],
                                     in1=pcv[:, 1:2], op=ALU.mult))
            W(v, "ve", c_ve, c_ve.n)
            c_ve.inc(v.tensor_scalar(out=pcv[:, 1:2], in0=pcv[:, 0:1],
                                     scalar1=SCALE, scalar2=None,
                                     op0=ALU.mult))
        VEs(v4)
        VE_PCV = c_ve.n

        def t4(t):
            W(t, "te", c_ve, VE_PCV)
            t.matmul(PF[:, 8:12], ones_r[:, :], pcv[0:1, 4:8],
                     start=True, stop=True)
            t.matmul(PF[:, 12:14], ones_r[:, :], pcv[0:1, 2:4],
                     start=True, stop=True)
            c_te.inc(t.matmul(PF[:, 14:15], ones_r[:, :], pcv[0:1, 1:2],
                              start=True, stop=True))
        TEs(t4)
        TE_BCS = c_te.n

        def v5(v):
            W(v, "ve", c_te, TE_BCS)
            c_ve.inc(v.tensor_copy(out=g_sc[:, 0:7], in_=PF[:, 8:15]))
            W(v, "ve", c_ve, c_ve.n)
            c_ve.inc(v.tensor_scalar(out=g_scn[:, 0:4], in0=g_sc[:, 0:4],
                                     scalar1=-1.0, scalar2=None,
                                     op0=ALU.mult))
        VEs(v5)
        VE_SCALES = c_ve.n

        x_te = []
        x_ve = []
        xq_ld = []
        for tc in range(NCH):
            def s4(sy, tc=tc):
                if tc == 0:
                    W(sy, "sy", c_ve, abs_red[6])
                elif tc == 1:
                    W(sy, "sy", c_ve, abs_red[7])
                else:
                    W(sy, "sy", c_te, x_te[tc - 2][3])
                c_sy.inc(sy.dma_start(out=scr_x[:, tc % 2, :],
                                      in_=x_in[tc * 128:(tc + 1) * 128, :]),
                         16)
                xq_ld.append(c_sy.n)
            SYs(s4)

            def v8(v, tc=tc):
                W(v, "ve", c_sy, xq_ld[tc])
            VEs(v8)
            ln_stats(S0, scr_x[:, tc % 2, :], LN_EPS)
            quant_scale(S0, S0.a[:], 1e-5, with_rstd=True)
            quant_core(S0, S0.a[:], scr_x[:, tc % 2, :])
            tt, vv = transpose16(
                lambda tc=tc: scr_x[:, tc % 2, :],
                [xqfm[:, 4 * g:4 * g + 4, tc * 128:(tc + 1) * 128]
                 for g in range(4)],
                ready=(c_ve, c_ve.n),
                prev_ve=(x_ve[tc - 1] if tc >= 1 else None))
            x_te.append(tt)
            x_ve.append(vv)
        XQ_VE_END = c_ve.n

        def g3(g):
            W(g, "gp", c_ve, x_ve[-1][3])
            i = g.dma_start(
                out=ag2_in.ap().rearrange("(kc p) t -> p kc t", p=128),
                in_=xqfm[:, :, :])
            c_gp.inc(i, 16)
            W(g, "gp", c_gp, c_gp.n)
            c_cc.inc(g.collective_compute(
                "AllGather", ALU.bypass, replica_groups=[list(range(R))],
                ins=[ag2_in.ap()], outs=[ag2_out.ap()]))
        GPs(g3)
        CC_AG2 = CC_AG0 + 1
        CC_AGW = CC_AG0 + 3

        pieces = []
        for ch in range(8):
            pieces.append(("sign", "wg", wg_in[ch * 128:(ch + 1) * 128, :],
                           None, ch, 0))
        for ch in range(8):
            pieces.append(("sign", "wu", wu_in[ch * 128:(ch + 1) * 128, :],
                           None, ch, 0))
        for ch in range(2):
            for cp in range(4):
                pieces.append(("sign", "wd",
                               wd_in[ch * 128:(ch + 1) * 128,
                                     cp * D:(cp + 1) * D], None, ch, cp))
        for wi, (nm, ap) in enumerate([("wq", wq_in), ("wk", wk_in),
                                       ("wv", wv_in), ("wo", wo_in)]):
            for ch in range(2):
                pieces.append(("tern", nm, ap[ch * 128:(ch + 1) * 128, :],
                               wi, ch, 0))

        pld = []
        pq_tok = []
        p_te = []
        p_ve = []
        pst_tok = {}
        pstage_last = {}
        pending_stage = []
        stage_ix = [0]

        def flush_stage(extra):
            while pending_stage:
                fpi, fnm, fch, fcp, fve, fbx = pending_stage.pop(0)

                def g2(g, fpi=fpi, fnm=fnm, fch=fch, fcp=fcp, fve=fve,
                       fbx=fbx, extra=extra):
                    if extra is not None:
                        W(g, "sy", extra[0], extra[1])
                    for grp in range(4):
                        W(g, "sy", c_ve, fve[grp])
                        if fnm == "wo":
                            dst = agw1_in.ap()[grp * 512:(grp + 1) * 512,
                                               fch * 128:(fch + 1) * 128]
                        elif fnm == "wg":
                            dst = agw1_in.ap()[grp * 512:(grp + 1) * 512,
                                               256 + fch * 128:
                                               256 + (fch + 1) * 128]
                        elif fnm == "wu":
                            dst = agw1_in.ap()[grp * 512:(grp + 1) * 512,
                                               1280 + fch * 128:
                                               1280 + (fch + 1) * 128]
                        else:
                            dst = agw2_in.ap()[fcp * D + grp * 512:
                                               fcp * D + (grp + 1) * 512,
                                               fch * 128:(fch + 1) * 128]
                        slot = (fbx + grp) % 16
                        i = g.dma_start(
                            out=dst.rearrange("(g p) c -> p g c", p=128),
                            in_=pstage[slot][:, :, :])
                        pst_tok[(fpi, grp)] = c_sy.inc(i, 16)
                        pstage_last[slot] = pst_tok[(fpi, grp)]
                SYs(g2)

        for pi, (kind, nm, src, wi, ch, cp) in enumerate(pieces):
            if pi >= 1 and pending_stage:
                pass

            def s3(sy, pi=pi, src=src):
                if pi == 0:
                    W(sy, "sy", c_te, x_te[6][3])
                elif pi == 1:
                    W(sy, "sy", c_te, x_te[7][3])
                else:
                    ctr2, v2 = pq_tok[pi - 2]
                    W(sy, "sy", ctr2, v2)
                c_sy.inc(sy.dma_start(out=scr_x[:, pi % 2, :], in_=src), 16)
                pld.append(c_sy.n)
            SYs(s3)

            if kind == "sign":
                def a2(sc, pi=pi):
                    W(sc, "ac", c_sy, pld[pi])
                    if pi >= 1:
                        W(sc, "ac", c_te, p_te[pi - 1][3])
                    else:
                        W(sc, "ac", c_ve, XQ_VE_END)
                        W(sc, "ac", c_te, x_te[-1][3])
                    c_ac.inc(sc.activation(out=S0.d[:],
                                           in_=scr_x[:, pi % 2, :],
                                           func=AF.Sign))
                ACs(a2)
                pq_tok.append((c_ac, c_ac.n))
            else:
                def v6(v, pi=pi, wi=wi):
                    W(v, "ve", c_sy, pld[pi])
                    W(v, "ve", c_ve, VE_SCALES)
                    if pi >= 1:
                        W(v, "ve", c_te, p_te[pi - 1][3])
                    v.tensor_scalar(out=S0.e[:], in0=scr_x[:, pi % 2, :],
                                    scalar1=g_sc[:, wi:wi + 1], scalar2=None,
                                    op0=ALU.is_ge)
                    v.tensor_scalar(out=S0.f[:], in0=scr_x[:, pi % 2, :],
                                    scalar1=g_scn[:, wi:wi + 1],
                                    scalar2=None, op0=ALU.is_le)
                    c_ve.inc(v.tensor_tensor(out=S0.d[:], in0=S0.e[:],
                                             in1=S0.f[:], op=ALU.subtract))
                VEs(v6)
                pq_tok.append((c_ve, c_ve.n))

            if pi >= 1 and pending_stage and pending_stage[0][0] == pi - 1:
                flush_stage(pq_tok[pi])
            if kind == "tern" and nm in ("wq", "wk", "wv"):
                widx = {"wq": 0, "wk": 1, "wv": 2}[nm]
                dsts = [qkvT[:, 4 * g:4 * g + 4,
                             widx * 256 + ch * 128:
                             widx * 256 + (ch + 1) * 128] for g in range(4)]
                stage = False
            else:
                base_ix = stage_ix[0]
                dsts = [pstage[(base_ix + g) % 16][:, :, :] for g in range(4)]
                stage = True
                stage_ix[0] += 4

            te_tok = []
            ve_tok = []
            qc_, qv_ = pq_tok[pi]
            for grp in range(4):
                pt = PE if grp % 2 == 0 else PF

                def t5(t, grp=grp, pt=pt, pi=pi, qc_=qc_, qv_=qv_):
                    W(t, "te", qc_, qv_)
                    if grp >= 2:
                        W(t, "te", c_ve, ve_tok[grp - 2])
                    elif pi >= 1:
                        W(t, "te", c_ve, p_ve[pi - 1][grp + 2])
                    else:
                        W(t, "te", c_ve, x_ve[-1][grp + 2])
                    for j in range(4):
                        kc = grp * 4 + j
                        i = t.transpose(
                            pt.ap().rearrange("p (g c) -> p g c",
                                                     g=4)[:, j, :],
                            S0.d[:, kc * 128:(kc + 1) * 128], ident[:])
                    te_tok.append(c_te.inc(i))
                TEs(t5)

                def v7(v, grp=grp, pt=pt, pi=pi, stage=stage, dsts=dsts):
                    W(v, "ve", c_te, te_tok[grp])
                    if stage:
                        slot = (base_ix + grp) % 16
                        if slot in pstage_last:
                            W(v, "ve", c_sy, pstage_last[slot])
                    i = v.tensor_copy(
                        out=dsts[grp],
                        in_=pt.ap().rearrange("p (g c) -> p g c",
                                                     g=4))
                    ve_tok.append(c_ve.inc(i))
                VEs(v7)

            if stage:
                pending_stage.append((pi, nm, ch, cp, list(ve_tok), base_ix))
            p_te.append(te_tok)
            p_ve.append(ve_tok)

        def v_sp(v):
            c_ve.inc(v.tensor_copy(out=S0.st[:, 14:15], in_=S0.st[:, 0:1]))
            c_ve.inc(v.tensor_copy(out=S0.st[:, 15:16], in_=S0.st[:, 0:1]))
        VEs(v_sp)
        flush_stage((c_ve, c_ve.n))
        AGW_STAGED = c_sy.n

        if stop_phase == 0:
            finish([(c, c.n) for c in ALL_CTRS])
            ph0.close()
            qkvT_st.close()
            return nc

        P0_DONE = [(c_te, c_te.n), (c_ve, c_ve.n), (c_ac, c_ac.n),
                   (c_gp, c_gp.n), (c_sy, c_sy.n)]
        ph0.close()
        ph1 = ExitStack()
        s_q = ph1.enter_context(nc.sbuf_tensor("s_q", [128, HL, T], BF16))
        s_k = ph1.enter_context(nc.sbuf_tensor("s_k", [128, HL, T], BF16))
        s_v = ph1.enter_context(nc.sbuf_tensor("s_v", [128, NKC, 256], BF16))
        ph1b = ExitStack()
        xb = ph1b.enter_context(nc.sbuf_tensor("xb", [128, 2, 16, 512], BF16))
        nsc = [ph1b.enter_context(nc.sbuf_tensor(f"nsc{i}", [128, 512], F32))
               for i in range(2)]
        mxrow = ph1b.enter_context(nc.sbuf_tensor("mxrow", [1, 16], F32))
        bstat = ph1b.enter_context(nc.sbuf_tensor("bstat", [1, 8], F32))

        BANKS4 = [PC, PD, PE, PF]
        bank_free = [None, None, None, None]
        xb_free = [None, None]

        for tb in range(R):
            for th in range(2):
                gi = tb * 2 + th

                def s5(sy, tb=tb, th=th, gi=gi):
                    for (ctr, val) in P0_DONE:
                        W(sy, "sy", ctr, val)
                    W(sy, "sy", c_cc, CC_AG2)
                    if xb_free[0] is not None:
                        W(sy, "sy", c_te, xb_free[0])
                    i = sy.dma_start(
                        out=xb[:, 0, :, :],
                        in_=ag2_out.ap()[tb * D:(tb + 1) * D, :].rearrange(
                            "(kc p) t -> p kc t",
                            p=128)[:, :, th * 512:(th + 1) * 512])
                    c_sy.inc(i, 16)
                SYs(s5)
                LD_XB = c_sy.n

                groups = [("q", 0), ("q", 1), ("k", 0), ("k", 1),
                          ("v", 0), ("v", 1), ("v", 2), ("v", 3)]
                for gidx, (gk, gsub) in enumerate(groups):
                    bank = BANKS4[gidx % 4]

                    def t6(t, gk=gk, gsub=gsub, gidx=gidx, bank=bank,
                           LD_XB=LD_XB):
                        for (ctr, val) in P0_DONE:
                            W(t, "te", ctr, val)
                        W(t, "te", c_sy, LD_XB)
                        bf = bank_free[gidx % 4]
                        if bf is not None:
                            W(t, "te", bf[0], bf[1])
                        xbv = xb[:, 0, :, :]
                        if gk == "q":
                            for kc in range(KC):
                                i = t.matmul(
                                    bank[:, :],
                                    qkvT[:, kc, gsub * 128:(gsub + 1) * 128],
                                    xbv[:, kc, :], start=(kc == 0),
                                    stop=(kc == KC - 1))
                        elif gk == "k":
                            for kc in range(KC):
                                i = t.matmul(
                                    bank[:, :],
                                    qkvT[:, kc, 256 + gsub * 128:
                                         256 + (gsub + 1) * 128],
                                    xbv[:, kc, :], start=(kc == 0),
                                    stop=(kc == KC - 1))
                        else:
                            for kc in range(KC):
                                i = t.matmul(
                                    bank[:, 0:256],
                                    xbv[:, kc, gsub * 128:(gsub + 1) * 128],
                                    qkvT[:, kc, 512:768], start=(kc == 0),
                                    stop=(kc == KC - 1))
                        c_te.inc(i)
                    TEs(t6)
                    MM_TOK = c_te.n

                    def a3(sc, gk=gk, gsub=gsub, bank=bank, tb=tb, th=th,
                           MM_TOK=MM_TOK):
                        W(sc, "ac", c_te, MM_TOK)
                        t0 = tb * 1024 + th * 512
                        if gk == "q":
                            i = sc.copy(out=s_q[:, gsub, t0:t0 + 512],
                                        in_=bank[:, :])
                        elif gk == "k":
                            i = sc.copy(out=s_k[:, gsub, t0:t0 + 512],
                                        in_=bank[:, :])
                        else:
                            i = sc.copy(out=s_v[:, (t0 // 128) + gsub, :],
                                        in_=bank[:, 0:256])
                        c_ac.inc(i)
                    ACs(a3)
                    bank_free[gidx % 4] = (c_ac, c_ac.n)
                xb_free[0] = c_te.n

        QKV_AC = c_ac.n

        combos = [("q", 0), ("k", 0), ("q", 1), ("k", 1)]
        for ci, (src_k, h) in enumerate(combos):
            src = s_q if src_k == "q" else s_k
            sq_te = []
            sq_ve = []
            for bx in range(16):
                buf = nsc[bx % 2]
                pt = PE if bx % 2 == 0 else PF

                def a4(sc, src=src, h=h, bx=bx, buf=buf):
                    W(sc, "ac", c_ac, QKV_AC)
                    if bx >= 2:
                        W(sc, "ac", c_te, sq_te[bx - 2])
                    i = sc.activation(out=buf[:, :],
                                      in_=src[:, h, bx * 512:(bx + 1) * 512],
                                      func=AF.Square)
                    c_ac.inc(i)
                ACs(a4)
                SQ_AC = c_ac.n

                def t7(t, buf=buf, pt=pt, bx=bx, SQ_AC=SQ_AC):
                    W(t, "te", c_ac, SQ_AC)
                    if bx >= 2:
                        W(t, "te", c_ve, sq_ve[bx - 2])
                    elif bank_free[2 + (bx % 2)] is not None:
                        bf = bank_free[2 + (bx % 2)]
                        W(t, "te", bf[0], bf[1])
                    i = t.matmul(pt[0:1, :], ones_c[:, :], buf[:, :],
                                 start=True, stop=True)
                    sq_te.append(c_te.inc(i))
                TEs(t7)

                def v9(v, pt=pt, bx=bx):
                    W(v, "ve", c_te, sq_te[bx])
                    i = v.reduce_max(out=mxrow[:, bx:bx + 1],
                                     in_=pt[0:1, :], axis=AX.X)
                    sq_ve.append(c_ve.inc(i))
                VEs(v9)

            def v10(v, ci=ci):
                W(v, "ve", c_ve, sq_ve[15])
                c_ve.inc(v.reduce_max(out=bstat[:, ci:ci + 1],
                                      in_=mxrow[:, :], axis=AX.X))
            VEs(v10)

        def v11(v):
            W(v, "ve", c_ve, c_ve.n)
            bs = bstat.ap()
            c_ve.inc(v.tensor_tensor(out=bstat[:, 4:6], in0=bs[0:1, 0:4:2],
                                     in1=bs[0:1, 1:4:2], op=ALU.add))
            W(v, "ve", c_ve, c_ve.n)
            c_ve.inc(v.tensor_scalar(out=bstat[:, 4:6], in0=bstat[:, 4:6],
                                     scalar1=-0.5, scalar2=None,
                                     op0=ALU.mult))
            W(v, "ve", c_ve, c_ve.n)
            c_ve.inc(v.tensor_scalar(out=bstat[:, 4:6], in0=bstat[:, 4:6],
                                     scalar1=g_sc[0:1, 6:7], scalar2=None,
                                     op0=ALU.mult))
        VEs(v11)
        VE_NEGB_ROW = c_ve.n

        def t8(t):
            W(t, "te", c_ve, VE_NEGB_ROW)
            c_te.inc(t.matmul(PE[:, 0:2], ones_r[:, :], bstat[0:1, 4:6],
                              start=True, stop=True))
        TEs(t8)
        TE_NEGB = c_te.n

        def v12(v):
            W(v, "ve", c_te, TE_NEGB)
            c_ve.inc(v.tensor_copy(out=g_negB[:, :], in_=PE[:, 0:2]))
        VEs(v12)
        P2_READY = [(c_ve, c_ve.n), (c_ac, c_ac.n)]

        if stop_phase == 1:
            finish([(c, c.n) for c in ALL_CTRS])
            ph1b.close()
            ph1.close()
            qkvT_st.close()
            return nc

        ph1b.close()
        ph2 = ExitStack()
        sp = ph2.enter_context(nc.sbuf_tensor("sp", [128, 2, 2, QBS], BF16))
        sr = ph2.enter_context(nc.sbuf_tensor("sr", [128, QBS], F32))
        sr2 = ph2.enter_context(nc.sbuf_tensor("sr2", [128, 2 * QBS], F32))
        srb = ph2.enter_context(nc.sbuf_tensor("srb", [128, 2 * QBS], BF16))
        sofm = ph2.enter_context(nc.sbuf_tensor("sofm", [128, QBS], F32))
        szv = ph2.enter_context(nc.sbuf_tensor("szv", [128, 4], F32))
        szr = ph2.enter_context(nc.sbuf_tensor("szr", [128, 4], F32))
        s_o = ph2.enter_context(nc.sbuf_tensor("s_o", [128, 2, 4, DH], BF16))

        SEXP = 12
        STRP = 8
        so_dma = {}
        S_SC0, S_EX0, S_AV0, S_Z0 = s_sc.n, s_ex.n, s_av.n, s_z.n
        S_TR0, S_RED0 = s_tr.n, s_red.n



        for it in range(NIT):
            h = it // NQB
            qb = it % NQB
            qt = s_q[:, h, qb * QBS:(qb + 1) * QBS]

            def t9(t, it=it, h=h, qt=qt):
                if it == 0:
                    for (ctr, val) in P2_READY:
                        W(t, "te", ctr, val)
                for pr in range(NPAIR):
                    g = it * NPAIR + pr
                    ps = PA if g % 2 == 0 else PB
                    if g >= 2:
                        W(t, "te", s_ex, S_EX0 + g - 1)
                    k0 = (2 * pr) * 128
                    t.matmul(ps[:, 0:QBS], s_k[:, h, k0:k0 + 128], qt,
                             start=True, stop=True)
                    i = t.matmul(ps[:, QBS:2 * QBS],
                                 s_k[:, h, k0 + 128:k0 + 256], qt,
                                 start=True, stop=True)
                    s_sc.inc(i)
                    if pr >= 1:
                        av = g - 1
                        avp = pr - 1
                        W(t, "te", s_ex, S_EX0 + av + 1)
                        if avp == 0 and it >= 1:
                            W(t, "te", s_red, S_RED0 + (it - 1) * SEXP + 6)
                        kc0 = 2 * avp
                        t.matmul(PC[:, :],
                                 s_v[:, kc0, h * 128:(h + 1) * 128],
                                 sp[:, av % 2, 0, :], start=(avp == 0),
                                 stop=False, skip_group_check=True)
                        i = t.matmul(PC[:, :],
                                     s_v[:, kc0 + 1, h * 128:(h + 1) * 128],
                                     sp[:, av % 2, 1, :], start=False,
                                     stop=False, skip_group_check=True)
                        s_av.inc(i)
                av = (it + 1) * NPAIR - 1
                W(t, "te", s_ex, S_EX0 + av + 1)
                kc0 = 2 * (NPAIR - 1)
                t.matmul(PC[:, :], s_v[:, kc0, h * 128:(h + 1) * 128],
                         sp[:, av % 2, 0, :], start=False, stop=False,
                         skip_group_check=True)
                i = t.matmul(PC[:, :], s_v[:, kc0 + 1, h * 128:(h + 1) * 128],
                             sp[:, av % 2, 1, :], start=False, stop=True,
                             skip_group_check=True)
                s_av.inc(i)
                for i8 in range(8):
                    pt = PE if i8 % 2 == 0 else PF
                    if i8 < 4:
                        W(t, "te", s_red, S_RED0 + it * SEXP + 1)
                        if i8 < 2:
                            if it >= 1:
                                W(t, "te", s_red, S_RED0 + (it - 1) * SEXP +
                                  (11 if i8 == 0 else 12))
                        else:
                            W(t, "te", s_red, S_RED0 + it * SEXP + i8)
                        src = sr[:, i8 * 128:(i8 + 1) * 128]
                    else:
                        if i8 < 6:
                            W(t, "te", s_red, S_RED0 + it * SEXP + 6)
                        else:
                            W(t, "te", s_red, S_RED0 + it * SEXP + (i8 + 3))
                        src = sofm[:, (i8 - 4) * 128:(i8 - 3) * 128]
                    i = t.transpose(pt[:, 0:128], src, ident[:])
                    s_tr.inc(i)
            TEs(t9)
            assert s_sc.n == S_SC0 + (it + 1) * NPAIR
            assert s_av.n == S_AV0 + (it + 1) * NPAIR
            assert s_tr.n == S_TR0 + (it + 1) * STRP

            def a5(sc, it=it, h=h):
                for pr in range(NPAIR):
                    g = it * NPAIR + pr
                    ps = PA if g % 2 == 0 else PB
                    W(sc, "ac", s_sc, S_SC0 + g + 1)
                    if g >= 2:
                        W(sc, "ac", s_av, S_AV0 + g - 1)
                        W(sc, "ac", s_z, S_Z0 + g - 1)
                    i = sc.activation(out=sp[:, g % 2, :, :], in_=ps[:, :],
                                      func=AF.Exp,
                                      bias=g_negB[:, h:h + 1],
                                      scale=g_sc[:, 6:7])
                    s_ex.inc(i)
            ACs(a5)
            assert s_ex.n == S_EX0 + (it + 1) * NPAIR

            def v14(v, it=it, pt=None):
                for pr in range(NPAIR):
                    g = it * NPAIR + pr
                    W(v, "ve", s_ex, S_EX0 + g + 1)
                    spv = sp[:, g % 2, :, :]
                    if pr % 4 == 0:
                        i = v.tensor_copy(out=srb[:, :], in_=spv)
                    else:
                        i = v.tensor_tensor(out=srb[:, :], in0=srb[:, :],
                                            in1=spv, op=ALU.add)
                    s_z.inc(i)
                    if pr % 4 == 3:
                        if pr == 3:
                            v.tensor_copy(out=sr2[:, :], in_=srb[:, :])
                        else:
                            v.tensor_tensor(out=sr2[:, :], in0=sr2[:, :],
                                            in1=srb[:, :], op=ALU.add)
                if it >= 1:
                    W(v, "ve", s_tr, S_TR0 + (it - 1) * STRP + 4)
                i = v.tensor_tensor(out=sr[:, :], in0=sr2[:, 0:QBS],
                                    in1=sr2[:, QBS:2 * QBS], op=ALU.add)
                s_red.inc(i)
                for qc in range(4):
                    ptq = PE if qc % 2 == 0 else PF
                    W(v, "ve", s_tr, S_TR0 + it * STRP + qc + 1)
                    i = v.reduce_sum(out=szv[:, qc:qc + 1],
                                     in_=ptq[:, 0:128], axis=AX.X)
                    s_red.inc(i)
                W(v, "ve", s_av, S_AV0 + (it + 1) * NPAIR)
                if it >= 1:
                    W(v, "ve", s_tr, S_TR0 + (it - 1) * STRP + 8)
                i = v.tensor_copy(out=sofm[:, :], in_=PC[:, :])
                s_red.inc(i)
                W(v, "ve", s_red, S_RED0 + it * SEXP + 5)
                i = v.reciprocal(out=szr[:, :], in_=szv[:, :])
                s_red.inc(i)
                W(v, "ve", s_red, S_RED0 + it * SEXP + 7)
                i = v.tensor_scalar(out=szr[:, :], in0=szr[:, :],
                                    scalar1=g_sc[:, 4:5],
                                    scalar2=None, op0=ALU.mult)
                s_red.inc(i)
                for qc in range(4):
                    ptq = PE if qc % 2 == 0 else PF
                    W(v, "ve", s_tr, S_TR0 + it * STRP + 5 + qc)
                    W(v, "ve", s_red, S_RED0 + it * SEXP + 8)
                    if (it - 2) in so_dma:
                        W(v, "ve", c_sy, so_dma[it - 2])
                    i = v.tensor_scalar(out=s_o[:, it % 2, qc, :],
                                        in0=ptq[:, 0:128],
                                        scalar1=szr[:, qc:qc + 1],
                                        scalar2=None, op0=ALU.mult)
                    s_red.inc(i)
            VEs(v14)
            assert s_z.n == S_Z0 + (it + 1) * NPAIR
            assert s_red.n == S_RED0 + (it + 1) * SEXP

            def g5(g, it=it, h=h, qb=qb):
                W(g, "sy", s_red, S_RED0 + (it + 1) * SEXP)
                dst = ag3_in.ap()[qb * QBS:(qb + 1) * QBS,
                                  h * 128:(h + 1) * 128]
                i = g.dma_start(
                    out=dst.rearrange("(qc p) d -> p qc d", p=128),
                    in_=s_o[:, it % 2, :, :])
                so_dma[it] = c_sy.inc(i, 16)
            SYs(g5)

            if it == 6:
                def g4b(g):
                    W(g, "gp", c_gp, c_gp.n)
                    W(g, "gp", c_sy, AGW_STAGED)
                    c_cc.inc(g.collective_compute(
                        "AllGather", ALU.bypass,
                        replica_groups=[list(range(R))],
                        ins=[agw1_in.ap()], outs=[agw1_out.ap()]))
                    c_cc.inc(g.collective_compute(
                        "AllGather", ALU.bypass,
                        replica_groups=[list(range(R))],
                        ins=[agw2_in.ap()], outs=[agw2_out.ap()]))
                GPs(g4b)

        def g6(g):
            W(g, "gp", c_gp, c_gp.n)
            W(g, "gp", c_sy, so_dma[NIT - 1])
            c_cc.inc(g.collective_compute(
                "AllGather", ALU.bypass, replica_groups=[list(range(R))],
                ins=[ag3_in.ap()], outs=[ag3_out.ap()]))
        GPs(g6)
        CC_AG3 = c_cc.n

        if stop_phase == 2:
            d_sofm = nc.dram_tensor("dbg_sofm", [128, QBS], F32,
                                    kind="ExternalOutput")
            d_szv = nc.dram_tensor("dbg_szv", [128, 4], F32,
                                   kind="ExternalOutput")
            d_szr = nc.dram_tensor("dbg_szr", [128, 4], F32,
                                   kind="ExternalOutput")
            d_nb = nc.dram_tensor("dbg_negB", [128, 2], F32,
                                  kind="ExternalOutput")
            d_sr = nc.dram_tensor("dbg_sr", [128, QBS], F32,
                                  kind="ExternalOutput")

            def s_d2(sy):
                for (c2, v2) in [(c3, c3.n) for c3 in ALL_CTRS]:
                    W(sy, "sy", c2, v2)
                c_st.inc(sy.dma_start(out=d_sofm.ap(), in_=sofm[:, :]), 16)
                c_st.inc(sy.dma_start(out=d_szv.ap(), in_=szv[:, :]), 16)
                c_st.inc(sy.dma_start(out=d_szr.ap(), in_=szr[:, :]), 16)
                c_st.inc(sy.dma_start(out=d_nb.ap(), in_=g_negB[:, :]), 16)
                c_st.inc(sy.dma_start(out=d_sr.ap(), in_=sr[:, :]), 16)
            SYs(s_d2)
            finish([(c, c.n) for c in ALL_CTRS])
            ph2.close()
            ph1.close()
            qkvT_st.close()
            return nc

        P2_DONE = [(s_red, s_red.n), (s_tr, s_tr.n), (c_gp, c_gp.n),
                   (c_ac, c_ac.n), (c_te, c_te.n), (c_ve, c_ve.n),
                   (c_sy, c_sy.n)]
        ph2.close()
        ph1.close()
        qkvT_st.close()
        ph3 = ExitStack()
        S3 = mkscr(ph3, "3")
        O_in = ph3.enter_context(nc.sbuf_tensor("O_in", [128, NCH, D], BF16))
        Oqfm = ph3.enter_context(nc.sbuf_tensor("Oqfm", [128, 16, TC], BF16))
        wob = ph3.enter_context(nc.sbuf_tensor("wob", [128, 2, 16, 512], BF16))
        dlt = ph3.enter_context(nc.sbuf_tensor("dlt", [128, 2, D], F32))

        def g7(g):
            for (ctr, val) in P2_DONE:
                W(g, "gp", ctr, val)
            W(g, "gp", c_cc, CC_AG3)
            c_gp.inc(g.dma_start(out=pid_sb[:, :],
                                 in_=nc.partition_id_tensor.ap()), 16)
            W(g, "gp", c_gp, c_gp.n)
            with g.register("pidr") as pid_reg:
                g.reg_load(pid_reg, pid_sb[0:1, 0:1])
                with g.register("offr") as off_reg:
                    g.reg_alu(off_reg, pid_reg, NCH, ALU.mult)
                    off = g.snap(off_reg, min_val=0,
                                 max_val=T // 128 - NCH)
                    for r in range(R):
                        src = ag3_out.ap()[r * T:(r + 1) * T, :].rearrange(
                            "(tc p) j -> p tc j", p=128)
                        i = g.dma_start(
                            out=O_in[:, :, r * 256:(r + 1) * 256],
                            in_=src[:, bass.ds(off, NCH), :])
                        c_gp.inc(i, 16)
        GPs(g7)
        OIN_TOK = c_gp.n

        o_te = []
        o_ve = []
        for tc in range(NCH):
            def v18(v, tc=tc):
                for (ctr, val) in P2_DONE:
                    W(v, "ve", ctr, val)
                W(v, "ve", c_gp, OIN_TOK)
                if tc >= 2:
                    W(v, "ve", c_te, o_te[tc - 2][3])
            VEs(v18)
            quant_scale(S3, O_in[:, tc, :], 1e-5, with_rstd=False)
            quant_core(S3, O_in[:, tc, :], scr_x[:, tc % 2, :])
            tt, vv = transpose16(
                lambda tc=tc: scr_x[:, tc % 2, :],
                [Oqfm[:, 4 * g:4 * g + 4, tc * 128:(tc + 1) * 128]
                 for g in range(4)],
                extra_first=P2_DONE if tc == 0 else (),
                ready=(c_ve, c_ve.n),
                prev_ve=(o_ve[tc - 1] if tc >= 1 else None))
            o_te.append(tt)
            o_ve.append(vv)
        OQFM_VE = o_ve[-1][3]

        wo_mm = {}
        dl_ev = {}
        x2_st = {}
        for tcp in range(4):
            for jb in range(4):
                def s6(sy, tcp=tcp, jb=jb):
                    for (ctr, val) in P2_DONE:
                        W(sy, "sy", ctr, val)
                    W(sy, "sy", c_cc, CC_AGW)
                    key = tcp * 4 + jb - 2
                    if key in wo_mm:
                        W(sy, "sy", c_te, wo_mm[key])
                    for rr in range(2):
                        r = 2 * jb + rr
                        i = sy.dma_start(
                            out=wob[:, jb % 2, :, rr * 256:(rr + 1) * 256],
                            in_=agw1_out.ap()[r * D:(r + 1) * D,
                                              0:256].rearrange(
                                "(kc p) j -> p kc j", p=128))
                        c_sy.inc(i, 16)
                SYs(s6)
                WO_LD = c_sy.n

                for tcd in range(2):
                    bank = PC if tcd == 0 else PD
                    tcg = tcp * 2 + tcd

                    def t12(t, tcg=tcg, jb=jb, bank=bank, WO_LD=WO_LD,
                            tcp=tcp, tcd=tcd):
                        W(t, "te", c_sy, WO_LD)
                        W(t, "te", c_ve, o_ve[tcg][3])
                        key = (tcp, jb - 1, tcd) if jb >= 1 else \
                              (tcp - 1, 3, tcd)
                        if key in dl_ev:
                            W(t, "te", c_ve, dl_ev[key])
                        for kc in range(KC):
                            i = t.matmul(
                                bank[:, :],
                                Oqfm[:, kc, tcg * 128:(tcg + 1) * 128],
                                wob[:, jb % 2, kc, :],
                                start=(kc == 0), stop=(kc == KC - 1))
                        c_te.inc(i)
                    TEs(t12)
                    wo_mm[tcp * 4 + jb] = c_te.n
                    MM_TOK = c_te.n

                    def v19(v, tcp=tcp, jb=jb, tcd=tcd, bank=bank,
                            MM_TOK=MM_TOK):
                        W(v, "ve", c_te, MM_TOK)
                        if (tcp - 1) in x2_st:
                            W(v, "ve", c_sy, x2_st[tcp - 1])
                        i = v.tensor_scalar(
                            out=dlt[:, tcd, jb * 512:(jb + 1) * 512],
                            in0=bank[:, :], scalar1=g_sc[:, 5:6],
                            scalar2=None, op0=ALU.mult)
                        dl_ev[(tcp, jb, tcd)] = c_ve.inc(i)
                    VEs(v19)

            for tcd in range(2):
                tcg = tcp * 2 + tcd

                def s7(sy, tcg=tcg, tcd=tcd, tcp=tcp):
                    W(sy, "sy", c_ve, dl_ev[(tcp, 3, tcd)])
                    if tcg >= 2:
                        pass
                    i = sy.dma_start(out=scr_x[:, tcd, :],
                                     in_=x_in[tcg * 128:(tcg + 1) * 128, :])
                    c_sy.inc(i, 16)
                SYs(s7)
                XLD = c_sy.n

                ln_stats(S3, dlt[:, tcd, :], LN_EPS,
                         out_dst=dlt[:, tcd, :], base_ap=scr_x[:, tcd, :],
                         extra_ve_waits=[(c_ve, dl_ev[(tcp, 3, tcd)]),
                                         (c_sy, XLD)])
                LN_TOK = c_ve.n

                def s8(sy, tcg=tcg, tcd=tcd, tcp=tcp, LN_TOK=LN_TOK):
                    W(sy, "sy", c_ve, LN_TOK)
                    i = sy.dma_start(
                        out=x2_dram.ap()[tcg * 128:(tcg + 1) * 128, :],
                        in_=dlt[:, tcd, :])
                    c_sy.inc(i, 16)
                    x2_st[tcp] = c_sy.n
                SYs(s8)

        if stop_phase == 3:
            d_oin = nc.dram_tensor("dbg_oin", [128, NCH, D], BF16,
                                   kind="ExternalOutput")
            d_oq = nc.dram_tensor("dbg_oq", [128, 16, TC], BF16,
                                  kind="ExternalOutput")
            d_dlt = nc.dram_tensor("dbg_dlt", [128, 2, D], F32,
                                   kind="ExternalOutput")

            def s_d3(sy):
                for (c, v) in [(c2, c2.n) for c2 in ALL_CTRS]:
                    W(sy, "sy", c, v)
                c_st.inc(sy.dma_start(out=d_oin.ap(), in_=O_in[:, :, :]), 16)
                c_st.inc(sy.dma_start(out=d_oq.ap(), in_=Oqfm[:, :, :]), 16)
                c_st.inc(sy.dma_start(out=d_dlt.ap(), in_=dlt[:, :, :]), 16)
            SYs(s_d3)
            finish([(c, c.n) for c in ALL_CTRS])
            ph3.close()
            return nc

        P3_DONE = [(c_te, c_te.n), (c_ve, c_ve.n), (c_ac, c_ac.n),
                   (c_sy, c_sy.n), (c_gp, c_gp.n)]
        ph3.close()
        ph4 = ExitStack()
        S4 = mkscr(ph4, "4")
        xq2 = ph4.enter_context(nc.sbuf_tensor("xq2", [128, 16, TC], BF16))
        pbuf = ph4.enter_context(nc.sbuf_tensor("pbuf", [128, NHC, 512], BF16))
        wgu = ph4.enter_context(
            nc.sbuf_tensor("wgu", [128, 2, 2, 16, 128], BF16))
        wdb = ph4.enter_context(nc.sbuf_tensor("wdb", [128, 2, 8, 512], BF16))
        sgl = ph4.enter_context(nc.sbuf_tensor("sgl", [128, 2, 512], BF16))
        dl4 = ph4.enter_context(nc.sbuf_tensor("dl4", [128, 2, D], F32))

        f_te = []
        f_ve = []
        for tc in range(NCH):
            def s9(sy, tc=tc):
                for (ctr, val) in P3_DONE:
                    W(sy, "sy", ctr, val)
                if tc >= 1:
                    W(sy, "sy", c_te, f_te[tc - 1][3])
                i = sy.dma_start(
                    out=scr_x[:, 0, :],
                    in_=x2_dram.ap()[tc * 128:(tc + 1) * 128, :])
                c_sy.inc(i, 16)
            SYs(s9)
            X2LD = c_sy.n

            def v21(v, tc=tc, X2LD=X2LD):
                for (ctr, val) in P3_DONE:
                    W(v, "ve", ctr, val)
                W(v, "ve", c_sy, X2LD)
            VEs(v21)
            ln_stats(S4, scr_x[:, 0, :], LN_EPS, out_dst=scr_x[:, 1, :])
            ln_stats(S4, scr_x[:, 1, :], RMS_EPS)
            quant_scale(S4, S4.a[:], None, with_rstd=True)
            quant_core(S4, S4.a[:], scr_x[:, 0, :])
            tt, vv = transpose16(
                lambda: scr_x[:, 0, :],
                [xq2[:, 4 * g:4 * g + 4, tc * 128:(tc + 1) * 128]
                 for g in range(4)],
                extra_first=P3_DONE if tc == 0 else (),
                ready=(c_ve, c_ve.n),
                prev_ve=(f_ve[tc - 1] if tc >= 1 else None))
            f_te.append(tt)
            f_ve.append(vv)
        XQ2_VE = c_ve.n

        y_chunk = 0
        down_done = [None]
        yst_tok = {}
        for tb in range(2):
            gu_ld = {}
            p_mul = {}
            silu_tok = {}
            for hc in range(NHC):
                gbank = PC if hc % 2 == 0 else PE
                ubank = PD if hc % 2 == 0 else PF

                def s10(sy, tb=tb, hc=hc):
                    for (ctr, val) in P3_DONE:
                        W(sy, "sy", ctr, val)
                    W(sy, "sy", c_cc, CC_AGW)
                    if (tb, hc - 2) in p_mul:
                        W(sy, "sy", c_ve, p_mul[(tb, hc - 2)])
                    r = hc // 8
                    col = (hc % 8) * 128
                    for wsel, base in ((0, 256), (1, 1280)):
                        i = sy.dma_start(
                            out=wgu[:, hc % 2, wsel, :, :],
                            in_=agw1_out.ap()[r * D:(r + 1) * D,
                                              base + col:base + col + 128
                                              ].rearrange(
                                "(kc p) h -> p kc h", p=128))
                        c_sy.inc(i, 16)
                    gu_ld[hc] = c_sy.n
                SYs(s10)

                def t13(t, tb=tb, hc=hc, gbank=gbank, ubank=ubank):
                    for (ctr, val) in P3_DONE:
                        W(t, "te", ctr, val)
                    W(t, "te", c_sy, gu_ld[hc])
                    W(t, "te", c_ve, f_ve[4 * tb + 3][3])
                    if tb == 1 and down_done[0] is not None:
                        W(t, "te", c_te, down_done[0])
                    if (tb, hc - 2) in silu_tok:
                        W(t, "te", c_ac, silu_tok[(tb, hc - 2)])
                        W(t, "te", c_ve, p_mul[(tb, hc - 2)])
                    for kc in range(KC):
                        t.matmul(gbank[:, :], wgu[:, hc % 2, 0, kc, :],
                                 xq2[:, kc, tb * 512:(tb + 1) * 512],
                                 start=(kc == 0), stop=(kc == KC - 1))
                    for kc in range(KC):
                        i = t.matmul(ubank[:, :], wgu[:, hc % 2, 1, kc, :],
                                     xq2[:, kc, tb * 512:(tb + 1) * 512],
                                     start=(kc == 0), stop=(kc == KC - 1))
                    c_te.inc(i)
                TEs(t13)
                GU_MM = c_te.n

                def a6(sc, hc=hc, gbank=gbank, GU_MM=GU_MM):
                    W(sc, "ac", c_te, GU_MM)
                    i = sc.activation(out=sgl[:, hc % 2, :],
                                      in_=gbank[:, :], func=AF.Silu)
                    c_ac.inc(i)
                ACs(a6)
                silu_tok[(tb, hc)] = c_ac.n

                def v22(v, tb=tb, hc=hc, ubank=ubank):
                    W(v, "ve", c_ac, silu_tok[(tb, hc)])
                    i = v.tensor_tensor(out=pbuf[:, hc, :],
                                        in0=sgl[:, hc % 2, :],
                                        in1=ubank[:, :], op=ALU.mult)
                    p_mul[(tb, hc)] = c_ve.inc(i)
                VEs(v22)

            P_DONE = p_mul[(tb, NHC - 1)]

            wd_ld = {}
            acc_ev = {}
            for tcp in range(2):
                for db in range(4):
                    for hg in range(8):
                        def s11(sy, tb=tb, tcp=tcp, db=db, hg=hg):
                            for (ctr, val) in P3_DONE:
                                W(sy, "sy", ctr, val)
                            seq = (tcp * 4 + db) * 8 + hg
                            if seq >= 2 and (seq - 2) in wd_ld:
                                W(sy, "sy", c_te, wd_ld[seq - 2][1])
                            for rr in range(2):
                                r = 2 * db + rr
                                i = sy.dma_start(
                                    out=wdb[:, hg % 2, :,
                                            rr * 256:(rr + 1) * 256],
                                    in_=agw2_out.ap()[
                                        r * HFF + hg * 1024:
                                        r * HFF + (hg + 1) * 1024,
                                        :].rearrange("(hc p) d -> p hc d",
                                                     p=128))
                                c_sy.inc(i, 16)
                        SYs(s11)
                        WD_LD = c_sy.n

                        def t14(t, tb=tb, tcp=tcp, db=db, hg=hg,
                                WD_LD=WD_LD, P_DONE=P_DONE):
                            W(t, "te", c_sy, WD_LD)
                            W(t, "te", c_ve, P_DONE)
                            if hg == 0:
                                for tcd in range(2):
                                    key = (tcp, db - 1, tcd) if db >= 1 \
                                        else (tcp - 1, 3, tcd)
                                    if key in acc_ev:
                                        W(t, "te", c_ve, acc_ev[key])
                            for h8 in range(8):
                                hc = hg * 8 + h8
                                for tcd in range(2):
                                    tcl = tcp * 2 + tcd
                                    bank = PC if tcd == 0 else PD
                                    i = t.matmul(
                                        bank[:, :],
                                        pbuf[:, hc,
                                             tcl * 128:(tcl + 1) * 128],
                                        wdb[:, hg % 2, h8, :],
                                        start=(hc == 0), stop=(hc == NHC - 1),
                                        skip_group_check=True)
                            c_te.inc(i)
                        TEs(t14)
                        wd_ld[(tcp * 4 + db) * 8 + hg] = (WD_LD, c_te.n)
                    DP_MM = c_te.n

                    for tcd in range(2):
                        bank = PC if tcd == 0 else PD

                        def v23(v, tb=tb, tcp=tcp, db=db, tcd=tcd,
                                bank=bank, DP_MM=DP_MM):
                            W(v, "ve", c_te, DP_MM)
                            prev = (tb, tcp - 1, tcd) if tcp >= 1 else \
                                   (tb - 1, 1, tcd)
                            if prev in yst_tok:
                                W(v, "ve", c_st, yst_tok[prev])
                            i = v.tensor_copy(
                                out=dl4[:, tcd, db * 512:(db + 1) * 512],
                                in_=bank[:, :])
                            acc_ev[(tcp, db, tcd)] = c_ve.inc(i)
                        VEs(v23)

                for tcd in range(2):
                    tcg = tb * 4 + tcp * 2 + tcd

                    def s12(sy, tcg=tcg, tcd=tcd, tcp=tcp):
                        W(sy, "sy", c_ve, acc_ev[(tcp, 3, tcd)])
                        i = sy.dma_start(
                            out=scr_x[:, tcd, :],
                            in_=x2_dram.ap()[tcg * 128:(tcg + 1) * 128, :])
                        c_sy.inc(i, 16)
                    SYs(s12)
                    XLD2 = c_sy.n

                    ln_stats(S4, dl4[:, tcd, :], LN_EPS,
                             out_dst=dl4[:, tcd, :],
                             base_ap=scr_x[:, tcd, :],
                             extra_ve_waits=[(c_ve, acc_ev[(tcp, 3, tcd)]),
                                             (c_sy, XLD2)])
                    LN_TOK4 = c_ve.n

                    def s13(sy, tb=tb, tcg=tcg, tcp=tcp, tcd=tcd,
                            LN_TOK4=LN_TOK4):
                        W(sy, "sy", c_ve, LN_TOK4)
                        i = sy.dma_start(
                            out=y_out[tcg * 128:(tcg + 1) * 128, :],
                            in_=dl4[:, tcd, :])
                        yst_tok[(tb, tcp, tcd)] = c_st.inc(i, 16)
                    SYs(s13)
            down_done[0] = c_te.n

        def s_end(sy):
            sy.wait_ge(c_st.h, c_st.n)
            for name, outh in dbg_outs.items():
                i = sy.dma_start(out=outh.ap(), in_=internals[name].ap())
                c_st.inc(i, 16)
            sy.wait_ge(c_st.h, c_st.n)
        SYs(s_end)
        ph4.close()

    return nc


def kernel(L0_attn_x, L0_attn_w_q, L0_attn_w_k, L0_attn_w_v, L0_attn_w_o,
           L0_ffn_W_g, L0_ffn_W_u, L0_ffn_W_d):
    x = np.ascontiguousarray(np.asarray(L0_attn_x, dtype=np.float32)
                             .reshape(T, D))
    wq = np.ascontiguousarray(np.asarray(L0_attn_w_q, dtype=np.float32))
    wk = np.ascontiguousarray(np.asarray(L0_attn_w_k, dtype=np.float32))
    wv = np.ascontiguousarray(np.asarray(L0_attn_w_v, dtype=np.float32))
    wo = np.ascontiguousarray(np.asarray(L0_attn_w_o, dtype=np.float32))
    wg = np.ascontiguousarray(np.asarray(L0_ffn_W_g, dtype=np.float32))
    wu = np.ascontiguousarray(np.asarray(L0_ffn_W_u, dtype=np.float32))
    wd = np.ascontiguousarray(np.asarray(L0_ffn_W_d, dtype=np.float32))

    nc = build()
    in_maps = []
    for c in range(R):
        in_maps.append({
            "x": x[c * TC:(c + 1) * TC],
            "wq": wq[c * 256:(c + 1) * 256],
            "wk": wk[c * 256:(c + 1) * 256],
            "wv": wv[c * 256:(c + 1) * 256],
            "wo": wo[c * 256:(c + 1) * 256],
            "wg": wg[c * (HFF // R):(c + 1) * (HFF // R)],
            "wu": wu[c * (HFF // R):(c + 1) * (HFF // R)],
            "wd": wd[c * (D // R):(c + 1) * (D // R)],
        })
    res = run_bass_kernel_spmd(nc, in_maps, core_ids=list(range(R)))
    y = np.concatenate([res.results[c]["y"] for c in range(R)], axis=0)
    return y.reshape(1, T, D)
